# version 13
# baseline (speedup 1.0000x reference)
import math
import numpy as np
from contextlib import ExitStack

import concourse.bass as bass
import concourse.mybir as mybir
from concourse.bass_utils import run_bass_kernel_spmd

F32 = mybir.dt.float32
BF16 = mybir.dt.bfloat16
AF = mybir.ActivationFunctionType

NCORE = 8
S = 8192
HID = 4096
QSCALE = 1.0 / math.sqrt(128.0)
EPS = 1e-5

DEBUG_TAPS = []
MAX_PHASE = 5
NCORES_RUN = 8
NCHUNKS = 16
A_LEVEL = 4
LR_LEVEL = 4


def zig_blocks(c):
    return sorted([c, 15 - c, 16 + c, 31 - c, 32 + c, 47 - c, 48 + c, 63 - c])


BIG0 = 0
BIG1 = 65536
MID = 131072
SM_PT = 163840
SM_OTSB = 167936
SM_INVL = 172032
SM_ABC = 176128
SM_BBC = 178176
SM_TMP = 180224
SM_SCL = 182272
SM_MSK = 186368
SM_XSQ = 190464
SM_RA = 192512
SM_RB = 196608
SM_RC = 200704
SM_RBF = 204800
SM_ONE = 208896
SM_ONR = 208928
SM_DUM = 209216

ENGS = ("tensor", "vector", "scalar", "gpsimd", "sync")


class _Done(Exception):
    pass


class Prog:
    def __init__(self):
        self.q = {e: [] for e in ENGS}
        self.cnt = {}
        self.nbar = 0

    def bump(self, sem, n):
        self.cnt[sem] = self.cnt.get(sem, 0) + n
        return self.cnt[sem]

    def val(self, sem):
        return self.cnt.get(sem, 0)

    def wait(self, eng, sem, v):
        if v <= 0:
            return
        assert v <= self.val(sem), f"{eng} waits {sem}>={v} but only {self.val(sem)} emitted"
        self.q[eng].append(lambda E, SS, sem=sem, v=v: E.wait_ge(SS[sem], v))

    def op(self, eng, builder, inc=None):
        if inc is not None:
            sem, n = inc
            v = self.bump(sem, n)
            self.q[eng].append(
                lambda E, SS, b=builder, sem=sem, n=n: b(E, SS).then_inc(SS[sem], n))
            return v
        self.q[eng].append(lambda E, SS, b=builder: b(E, SS))
        return None


def build_program(debug_taps=(), max_phase=5, nchunks=16, a_level=4, lr_level=4, ncores=8):
    nc = bass.Bass("TRN2", target_bir_lowering=False)

    d_xt = nc.dram_tensor("xt_sw", [16, 128, 32 * 512], F32, kind="ExternalInput").ap()
    d_xo = nc.dram_tensor("xt_own", [HID, 1024], F32, kind="ExternalInput").ap()
    d_xo2 = nc.dram_tensor("xo_sw", [128, 32 * 1024], F32, kind="ExternalInput").ap()
    d_wdt = nc.dram_tensor("wdt_sw", [128, 32 * 512], F32, kind="ExternalInput").ap()
    d_wqt = nc.dram_tensor("wqt_sw", [32, 128, 32 * 128], F32, kind="ExternalInput").ap()
    d_wot = nc.dram_tensor("wot_sw", [32, 128, 32 * 128], F32, kind="ExternalInput").ap()
    d_wft = nc.dram_tensor("wft_sw", [32, 128, 32 * 128], F32, kind="ExternalInput").ap()
    d_wk2 = nc.dram_tensor("wk2", [128, 4, 1024], F32, kind="ExternalInput").ap()
    d_wv2 = nc.dram_tensor("wv2", [128, 4, 1024], F32, kind="ExternalInput").ap()
    d_scl = nc.dram_tensor("scale_pp", [128, 1024], F32, kind="ExternalInput").ap()
    d_msk = nc.dram_tensor("mask01", [16, 128, 128], F32, kind="ExternalInput").ap()

    d_qt = nc.dram_tensor("qt_spill", [32, 128, 1024], BF16, kind="Internal").ap()
    d_ot = nc.dram_tensor("ot_spill", [32, 128, 1024], BF16, kind="Internal").ap()
    d_x1 = nc.dram_tensor("x1_spill", [32, 128, 1024], F32, kind="Internal").ap()
    d_out = nc.dram_tensor("o", [32, 128, 1024], F32, kind="ExternalOutput").ap()

    d_dbg = {}
    if "cfull" in debug_taps:
        d_dbg["cfull"] = nc.dram_tensor("dbg_cfull", [128, 4 * S], BF16, kind="ExternalOutput").ap()
    if "ab" in debug_taps:
        d_dbg["ab"] = nc.dram_tensor("dbg_ab", [128, 2048], BF16, kind="ExternalOutput").ap()
        d_dbg["rows"] = nc.dram_tensor("dbg_rows", [1, 3072], F32, kind="ExternalOutput").ap()
    if "ab2" in debug_taps:
        d_dbg["ab2"] = nc.dram_tensor("dbg_ab2", [128, 2048], BF16, kind="ExternalOutput").ap()
        d_dbg["x1n"] = nc.dram_tensor("dbg_x1n", [128, 32 * 1024], BF16, kind="ExternalOutput").ap()
    if "qt" in debug_taps:
        d_dbg["qt"] = nc.dram_tensor("dbg_qt", [32, 128, 1024], BF16, kind="ExternalOutput").ap()
    if "ot" in debug_taps:
        d_dbg["ot"] = nc.dram_tensor("dbg_ot", [32, 128, 1024], BF16, kind="ExternalOutput").ap()
    if "x1" in debug_taps:
        d_dbg["x1"] = nc.dram_tensor("dbg_x1", [32, 128, 1024], F32, kind="ExternalOutput").ap()

    P = Prog()

    with ExitStack() as ctx:
        E = ctx.enter_context
        _SBUF_BASE = 16512

        def A(name, shape, dtype, offset):
            return nc.alloc_sbuf_tensor_at(name, shape, dtype, offset=offset + _SBUF_BASE)

        sb_pt = A("pt", [128, 4 * 512], BF16, offset=SM_PT)
        sb_otsb = A("otsb", [128, 2 * 1024], BF16, offset=SM_OTSB)
        sb_invl = A("invl", [128, 1024], F32, offset=SM_INVL)
        sb_abc = A("abc", [128, 1024], BF16, offset=SM_ABC)
        sb_bbc = A("bbc", [128, 1024], BF16, offset=SM_BBC)
        sb_tmp = A("tmpb", [128, 1024], BF16, offset=SM_TMP)
        sb_scl = A("sclpp", [128, 1024], F32, offset=SM_SCL)
        sb_msk = A("maskb", [128, 16 * 128], BF16, offset=SM_MSK)
        sb_xsq = A("xsq", [128, 1024], BF16, offset=SM_XSQ)
        sb_ra = A("rowa", [1, 1024], F32, offset=SM_RA)
        sb_rb = A("rowb", [1, 1024], F32, offset=SM_RB)
        sb_rc = A("rowc", [1, 1024], F32, offset=SM_RC)
        sb_rbf = A("rowbf", [1, 2048], BF16, offset=SM_RBF)
        sb_one = A("onec", [128, 8], BF16, offset=SM_ONE)
        sb_onr = A("oner", [1, 128], BF16, offset=SM_ONR)
        sb_dum = A("dum", [128, 16], F32, offset=SM_DUM)

        sb_xt = A("xt", [128, 2 * 32 * 512], BF16, offset=BIG0)
        sb_xo = A("xo", [128, 32 * 1024], BF16, offset=BIG0)
        sb_cf = A("cfull", [128, 4 * S], BF16, offset=BIG1)
        sb_wdt = A("wdtb", [128, 32 * 512], BF16, offset=MID)
        sb_wq = A("wqb", [128, 2 * 32 * 128], BF16, offset=MID)
        sb_qt = A("qtb", [128, 2 * 1024], BF16, offset=MID + 16384)
        sb_kcm = A("kcmp", [128, 16384], BF16, offset=BIG0)
        sb_vcm = A("vcmp", [128, 16384], BF16, offset=BIG0 + 32768)
        sb_kt = A("ktp", [128, 2 * 8192], BF16, offset=BIG1)
        sb_vp = A("vp", [128, 64 * 256], BF16, offset=BIG1 + 32768)
        sb_qg = A("qg", [128, 2 * 4 * 1024], BF16, offset=MID)
        sb_wk2 = A("wk2b", [128, 4 * 1024], BF16, offset=MID + 16384)
        sb_wv2 = A("wv2b", [128, 4 * 1024], BF16, offset=MID + 24576)
        sb_ota = A("otall", [128, 32 * 1024], BF16, offset=BIG0)
        sb_x1b = A("x1b", [128, 32 * 1024], BF16, offset=BIG1)
        sb_wo = A("wob", [128, 2 * 32 * 128], BF16, offset=MID)
        sb_xr = A("xr", [128, 2 * 1024], F32, offset=MID + 16384)
        sb_x1s = A("x1s", [128, 2 * 1024], F32, offset=MID + 24576)
        sb_wf = A("wfb", [128, 2 * 32 * 128], F32, offset=MID)
        sb_wb = A("wbb", [128, 2 * 32 * 128], BF16, offset=BIG0)
        sb_x1r = A("x1r", [128, 2 * 1024], F32, offset=BIG0 + 16384)
        sb_os = A("osb", [128, 2 * 1024], F32, offset=BIG0 + 24576)

        ps = [E(nc.psum_tensor(f"ps{i}", [128, 512], F32)) for i in range(4)]
        pc = E(nc.psum_tensor("pc", [128, 1024], F32))
        pd = E(nc.psum_tensor("pd", [128, 1024], F32))

        sems = {}
        for s_ in ("GQ", "SQ", "TM", "VE", "SE", "PB",
                   "CH0", "CH1", "WQ0", "WQ1", "QG0", "QG1", "WO0", "WO1",
                   "QS0", "QS1", "OS0", "OS1", "OP", "XR0", "XR1",
                   "XS0", "XS1", "WF0", "WF1", "XB0", "XB1", "OU0", "OU1", "GS"):
            sems[s_] = E(nc.semaphore(s_))

        blk = E(nc.Block())

        def barrier():
            P.nbar += 1
            tgt = 5 * P.nbar
            P.bump("PB", 5)
            P.q["tensor"].append(lambda E_, SS: E_.matmul(
                ps[0][0:8, 0:4], sb_onr[0:1, 0:8], sb_onr[0:1, 4:8],
                start=True, stop=True, skip_group_check=True).then_inc(SS["PB"], 1))
            P.q["sync"].append(lambda E_, SS: E_.nop().then_inc(SS["PB"], 1))
            P.q["gpsimd"].append(lambda E_, SS: E_.nop().then_inc(SS["PB"], 1))
            P.q["vector"].append(
                lambda E_, SS: E_.tensor_copy(sb_dum[:, 0:4], sb_dum[:, 4:8]).then_inc(SS["PB"], 1))
            P.q["scalar"].append(
                lambda E_, SS: E_.copy(sb_dum[:, 8:12], sb_dum[:, 12:16]).then_inc(SS["PB"], 1))
            for e in ENGS:
                P.wait(e, "PB", tgt)

        def emit_stats(src_aps, sumA, sumB, n_si):
            tm_sq = {}
            occ = {}
            lastocc = {}
            for i, (_, co) in enumerate(src_aps):
                lastocc[co] = i
            for i, (src, co) in enumerate(src_aps):
                slot = (i % 2) * 512
                if i >= 2:
                    P.wait("scalar", "TM", tm_sq[i - 2])
                P.op("scalar", lambda E_, SS, src=src, slot=slot: E_.activation(
                    sb_xsq[:, slot:slot + 512], src, AF.Square), inc=("SE", 1))
                sev = P.val("SE")
                P.wait("tensor", "SE", sev)
                pa, ca = sumA[co]
                pb, cb = sumB[co]
                stA = co not in occ
                occ[co] = True
                last = lastocc[co] == i
                P.op("tensor", lambda E_, SS, src=src, pa=pa, ca=ca, stA=stA, last=last: E_.matmul(
                    pa[0:1, ca:ca + 512], sb_one[:, 0:1], src, start=stA, stop=last))
                v = P.op("tensor", lambda E_, SS, pb=pb, cb=cb, slot=slot, stA=stA, last=last: E_.matmul(
                    pb[0:1, cb:cb + 512], sb_one[:, 0:1], sb_xsq[:, slot:slot + 512],
                    start=stA, stop=last), inc=("TM", 1))
                tm_sq[i] = v

        def ln_rows(n_si, out_co, rowA_srcs, rowB_srcs):
            P.wait("vector", "TM", P.val("TM"))
            off = 0
            for p_, c_, n_ in rowA_srcs:
                P.op("vector", lambda E_, SS, p_=p_, c_=c_, n_=n_, off=off: E_.tensor_copy(
                    sb_ra[0:1, off:off + n_], p_[0:1, c_:c_ + n_]))
                off += n_
            off = 0
            for p_, c_, n_ in rowB_srcs:
                P.op("vector", lambda E_, SS, p_=p_, c_=c_, n_=n_, off=off: E_.tensor_copy(
                    sb_rb[0:1, off:off + n_], p_[0:1, c_:c_ + n_]))
                off += n_
            n = n_si
            P.op("vector", lambda E_, SS, n=n: E_.tensor_scalar_mul(sb_ra[0:1, 0:n], sb_ra[0:1, 0:n], 1.0 / HID))
            P.op("vector", lambda E_, SS, n=n: E_.tensor_scalar_mul(sb_rb[0:1, 0:n], sb_rb[0:1, 0:n], 1.0 / HID))
            P.op("vector", lambda E_, SS, n=n: E_.tensor_mul(sb_rc[0:1, 0:n], sb_ra[0:1, 0:n], sb_ra[0:1, 0:n]))
            P.op("vector", lambda E_, SS, n=n: E_.tensor_sub(sb_rb[0:1, 0:n], sb_rb[0:1, 0:n], sb_rc[0:1, 0:n]))
            P.op("vector", lambda E_, SS, n=n: E_.tensor_scalar_add(sb_rb[0:1, 0:n], sb_rb[0:1, 0:n], EPS),
                 inc=("VE", 1))
            P.wait("scalar", "VE", P.val("VE"))
            P.op("scalar", lambda E_, SS, n=n: E_.activation(sb_rc[0:1, 0:n], sb_rb[0:1, 0:n], AF.Ln))
            P.op("scalar", lambda E_, SS, n=n: E_.activation(sb_rc[0:1, 0:n], sb_rc[0:1, 0:n], AF.Exp, scale=-0.5),
                 inc=("SE", 1))
            P.wait("vector", "SE", P.val("SE"))
            P.op("vector", lambda E_, SS, n=n: E_.tensor_copy(sb_rbf[0:1, 0:n], sb_rc[0:1, 0:n]))
            P.op("vector", lambda E_, SS, n=n: E_.tensor_mul(sb_ra[0:1, 0:n], sb_ra[0:1, 0:n], sb_rc[0:1, 0:n]))
            P.op("vector", lambda E_, SS, n=n: E_.tensor_scalar_mul(sb_ra[0:1, 0:n], sb_ra[0:1, 0:n], -1.0))
            P.op("vector", lambda E_, SS, n=n: E_.tensor_copy(sb_rbf[0:1, 1024:1024 + n], sb_ra[0:1, 0:n]),
                 inc=("VE", 1))
            P.wait("tensor", "VE", P.val("VE"))
            for h in range(0, n, 512):
                nn = min(512, n - h)
                P.op("tensor", lambda E_, SS, h=h, nn=nn: E_.matmul(
                    pc[:, h:h + nn], sb_onr[0:1, 0:128], sb_rbf[0:1, h:h + nn], start=True, stop=True))
            hs = list(range(0, n, 512))
            for h in hs[:-1]:
                P.op("tensor", lambda E_, SS, h=h: E_.matmul(
                    pd[:, h:h + 512], sb_onr[0:1, 0:128], sb_rbf[0:1, 1024 + h:1024 + h + 512],
                    start=True, stop=True))
            h = hs[-1]
            nn = min(512, n - h)
            tv = P.op("tensor", lambda E_, SS, h=h, nn=nn: E_.matmul(
                pd[:, h:h + nn], sb_onr[0:1, 0:128], sb_rbf[0:1, 1024 + h:1024 + h + nn],
                start=True, stop=True), inc=("TM", 1))
            P.wait("vector", "TM", tv)
            P.op("vector", lambda E_, SS, n=n, co=out_co: E_.tensor_copy(sb_abc[:, co:co + n], pc[:, 0:n]))
            P.op("vector", lambda E_, SS, n=n, co=out_co: E_.tensor_copy(sb_bbc[:, co:co + n], pd[:, 0:n]),
                 inc=("VE", 1))
            return P.val("VE")

        def emit_ln_apply(tiles, bcast_ve):
            P.wait("vector", "VE", bcast_ve)
            for x_ap, co in tiles:
                P.op("vector", lambda E_, SS, x_ap=x_ap, co=co: E_.tensor_mul(
                    sb_tmp[:, 0:512], x_ap, sb_abc[:, co:co + 512]))
                P.op("vector", lambda E_, SS, x_ap=x_ap, co=co: E_.tensor_add(
                    x_ap, sb_tmp[:, 0:512], sb_bbc[:, co:co + 512]))
            return P.op("vector", lambda E_, SS: E_.tensor_copy(sb_dum[:, 0:4], sb_dum[:, 4:8]),
                        inc=("VE", 1))

        try:
            P.op("vector", lambda E_, SS: E_.memset(sb_one[:, :], 1.0))
            P.op("vector", lambda E_, SS: E_.memset(sb_onr[:, :], 1.0))
            P.op("vector", lambda E_, SS: E_.memset(sb_dum[:, :], 0.0), inc=("VE", 1))
            P.op("sync", lambda E_, SS: E_.dma_start(out=sb_scl[:, :], in_=d_scl), inc=("SQ", 16))
            P.op("gpsimd", lambda E_, SS: E_.dma_start(
                out=sb_msk[:, :].rearrange("p (k m) -> p k m", k=16),
                in_=d_msk.rearrange("k p m -> p k m")), inc=("GS", 16))
            gs_msk = P.val("GS")

            P.op("gpsimd", lambda E_, SS: E_.dma_start(out=sb_wdt[:, :], in_=d_wdt), inc=("GS", 16))
            gs_wdt = P.val("GS")
            P.wait("tensor", "GS", gs_wdt)

            NCH = nchunks
            tm_down = [0] * NCH
            gq_ch = [0] * NCH
            ve_bcast_prev = 0

            def chunk_load(j):
                if j >= 2:
                    P.wait("gpsimd", "TM", tm_down[j - 2])
                buf = (j % 2) * 32 * 512
                P.op("gpsimd", lambda E_, SS, j=j, buf=buf: E_.dma_start(
                    out=sb_xt[:, buf:buf + 32 * 512], in_=d_xt[j]), inc=(f"CH{j % 2}", 16))
                gq_ch[j] = (f"CH{j % 2}", P.val(f"CH{j % 2}"))

            def emit_down(j, lnv):
                buf = (j % 2) * 32 * 512
                P.wait("tensor", "VE", lnv)
                for ct in range(4):
                    v = None
                    for t in range(32):
                        v = P.op("tensor", lambda E_, SS, ct=ct, t=t, buf=buf: E_.matmul(
                            ps[ct][:, 0:512],
                            sb_wdt[:, t * 512 + ct * 128:t * 512 + ct * 128 + 128],
                            sb_xt[:, buf + t * 512:buf + t * 512 + 512],
                            start=(t == 0), stop=(t == 31)),
                            inc=("TM", 1) if t == 31 else None)
                    P.wait("vector", "TM", v)
                    P.op("vector", lambda E_, SS, ct=ct, j=j: E_.tensor_copy(
                        sb_cf[:, ct * S + 512 * j:ct * S + 512 * j + 512], ps[ct][:, 0:512]),
                        inc=("VE", 1))
                tm_down[j] = P.val("TM")

            chunk_load(0)
            chunk_load(1)
            pend_ln = None
            for j in range(NCH):
                if pend_ln is not None:
                    emit_down(*pend_ln)
                    if j + 1 < NCH:
                        chunk_load(j + 1)
                buf = (j % 2) * 32 * 512
                co = (j % 2) * 512
                P.wait("scalar", *gq_ch[j])
                P.wait("tensor", *gq_ch[j])
                P.wait("tensor", "VE", ve_bcast_prev)
                srcs = [(sb_xt[:, buf + t * 512:buf + t * 512 + 512], 0) for t in range(32)]
                emit_stats(srcs, {0: (pc, 0)}, {0: (pd, 0)}, 512)
                ve_bcast_prev = ln_rows(512, co, [(pc, 0, 512)], [(pd, 0, 512)])
                lnv = emit_ln_apply([(sb_xt[:, buf + t * 512:buf + t * 512 + 512], co) for t in range(32)],
                                    ve_bcast_prev)
                pend_ln = (j, lnv)
            emit_down(*pend_ln)
            barrier()

            if "cfull" in d_dbg:
                P.op("sync", lambda E_, SS: E_.dma_start(out=d_dbg["cfull"], in_=sb_cf[:, :]), inc=("SQ", 16))
                P.wait("sync", "SQ", P.val("SQ"))
                barrier()
            if "ab" in d_dbg:
                P.op("sync", lambda E_, SS: E_.dma_start(out=d_dbg["ab"][:, 0:1024], in_=sb_abc[:, :]), inc=("SQ", 16))
                P.op("sync", lambda E_, SS: E_.dma_start(out=d_dbg["ab"][:, 1024:2048], in_=sb_bbc[:, :]), inc=("SQ", 16))
                P.op("sync", lambda E_, SS: E_.dma_start(out=d_dbg["rows"][0:1, 0:1024], in_=sb_ra[0:1, :]), inc=("SQ", 16))
                P.op("sync", lambda E_, SS: E_.dma_start(out=d_dbg["rows"][0:1, 1024:2048], in_=sb_rb[0:1, :]), inc=("SQ", 16))
                P.op("sync", lambda E_, SS: E_.dma_start(out=d_dbg["rows"][0:1, 2048:3072], in_=sb_rc[0:1, :]), inc=("SQ", 16))
                P.wait("sync", "SQ", P.val("SQ"))
                barrier()

            if max_phase < 2:
                _finish(P); raise _Done()
            P.op("gpsimd", lambda E_, SS: E_.dma_start(out=sb_xo[:, :], in_=d_xo2), inc=("GS", 16))
            gq_xo = P.val("GS")
            P.wait("scalar", "GS", gq_xo)
            P.wait("tensor", "GS", gq_xo)
            P.wait("tensor", "VE", ve_bcast_prev)
            srcs = []
            for t in range(32):
                for h in range(2):
                    srcs.append((sb_xo[:, t * 1024 + h * 512:t * 1024 + h * 512 + 512], h * 512))
            emit_stats(srcs, {0: (pc, 0), 512: (pc, 512)}, {0: (pd, 0), 512: (pd, 512)}, 1024)
            bve = ln_rows(1024, 0, [(pc, 0, 1024)], [(pd, 0, 1024)])
            tiles = []
            for t in range(32):
                for h in range(2):
                    tiles.append((sb_xo[:, t * 1024 + h * 512:t * 1024 + h * 512 + 512], h * 512))
            lnv = emit_ln_apply(tiles, bve)

            tm_q = [0] * 32
            sq_qt = [0] * 32
            ve_qt_evac = [0] * 32

            def wq_load(dt):
                if dt >= 2:
                    P.wait("gpsimd", "TM", tm_q[dt - 2])
                buf = (dt % 2) * 32 * 128
                P.op("gpsimd", lambda E_, SS, dt=dt, buf=buf: E_.dma_start(
                    out=sb_wq[:, buf:buf + 32 * 128], in_=d_wqt[dt]), inc=(f"WQ{dt % 2}", 16))
                return (f"WQ{dt % 2}", P.val(f"WQ{dt % 2}"))

            gq_wq = [0] * 32
            gq_wq[0] = wq_load(0)
            P.wait("tensor", "VE", lnv)
            for dt in range(32):
                if dt + 1 < 32:
                    gq_wq[dt + 1] = wq_load(dt + 1)
                buf = (dt % 2) * 32 * 128
                py = pc if dt % 2 == 0 else pd
                P.wait("tensor", *gq_wq[dt])
                if dt >= 2:
                    P.wait("tensor", "VE", ve_qt_evac[dt - 2])
                for t in range(32):
                    for h in range(2):
                        tm_q[dt] = P.op("tensor", lambda E_, SS, t=t, h=h, buf=buf, py=py: E_.matmul(
                            py[:, h * 512:h * 512 + 512],
                            sb_wq[:, buf + t * 128:buf + t * 128 + 128],
                            sb_xo[:, t * 1024 + h * 512:t * 1024 + h * 512 + 512],
                            start=(t == 0), stop=(t == 31)),
                            inc=("TM", 1) if (t == 31 and h == 1) else None)
                P.wait("vector", "TM", tm_q[dt])
                if dt >= 2:
                    P.wait("vector", *sq_qt[dt - 2])
                qb = (dt % 2) * 1024
                P.op("vector", lambda E_, SS, qb=qb, py=py: E_.tensor_scalar_mul(
                    sb_qt[:, qb:qb + 1024], py[:, 0:1024], QSCALE), inc=("VE", 1))
                ve_qt_evac[dt] = P.val("VE")
                P.wait("sync", "VE", ve_qt_evac[dt])
                P.op("sync", lambda E_, SS, dt=dt, qb=qb: E_.dma_start(
                    out=d_qt[dt], in_=sb_qt[:, qb:qb + 1024]), inc=(f"QS{dt % 2}", 16))
                sq_qt[dt] = (f"QS{dt % 2}", P.val(f"QS{dt % 2}"))
            sq_qt_all = (P.val("QS0"), P.val("QS1"))
            barrier()

            if "qt" in d_dbg:
                for dt in range(32):
                    P.op("sync", lambda E_, SS, dt=dt: E_.dma_start(out=d_dbg["qt"][dt], in_=d_qt[dt]),
                         inc=("SQ", 16))
                P.wait("sync", "SQ", P.val("SQ"))
                barrier()

            if max_phase < 3:
                _finish(P); raise _Done()
            P.wait("gpsimd", "QS0", sq_qt_all[0])
            P.wait("gpsimd", "QS1", sq_qt_all[1])
            P.op("gpsimd", lambda E_, SS: E_.dma_start(
                out=sb_wk2[:, :].rearrange("p (r m) -> p r m", r=4), in_=d_wk2), inc=("GS", 16))
            P.op("gpsimd", lambda E_, SS: E_.dma_start(
                out=sb_wv2[:, :].rearrange("p (r m) -> p r m", r=4), in_=d_wv2), inc=("GS", 16))
            gq_wkv = P.val("GS")

            P.wait("vector", "GS", gs_msk)
            for rset in (range(4), range(4, 8)):
                for h in range(8):
                    b = 64 * (h % 2)
                    cbase = (h // 2) * S
                    for r in rset:
                        dst = sb_kcm if r < 4 else sb_vcm
                        ro = (r % 4) * 1024
                        P.op("vector", lambda E_, SS, b=b, cbase=cbase, r=r, dst=dst, ro=ro, h=h: E_.tensor_copy(
                            dst[b:b + 64, (h // 2) * 4096 + ro:(h // 2) * 4096 + ro + 1024],
                            sb_cf[b:b + 64, cbase + r:cbase + r + 8 * 1023 + 1:8]))
                if rset.start == 0:
                    ve_matk = P.op("vector", lambda E_, SS: E_.tensor_copy(sb_dum[:, 0:4], sb_dum[:, 4:8]),
                                   inc=("VE", 1))
            ve_mat = P.op("vector", lambda E_, SS: E_.tensor_copy(sb_dum[:, 0:4], sb_dum[:, 4:8]),
                          inc=("VE", 1))

            g_i = [0]
            se_exp = {}
            tm_l = {}

            def emit_pvl(gi, sk_t, cs, nn, stb, msk, vem, hkL):
                if msk:
                    P.wait("tensor", "VE", vem)
                else:
                    P.wait("tensor", "SE", se_exp[gi])
                P.op("tensor", lambda E_, SS, sk_t=sk_t, cs=cs, nn=nn, stb=stb, hkL=hkL: E_.matmul(
                    pc[:, cs:cs + nn],
                    sb_vp[:, sk_t * 256 + hkL * 128:sk_t * 256 + hkL * 128 + 128],
                    sb_pt[:, stb * 512:stb * 512 + nn],
                    start=(sk_t == 0), stop=(sk_t == 63), skip_group_check=True))
                v = P.op("tensor", lambda E_, SS, cs=cs, nn=nn, stb=stb, sk_t=sk_t: E_.matmul(
                    pd[0:1, cs:cs + nn], sb_one[:, 0:1],
                    sb_pt[:, stb * 512:stb * 512 + nn],
                    start=(sk_t == 0), stop=(sk_t == 63), skip_group_check=True), inc=("TM", 1))
                tm_l[gi] = v
            tm_slast = {}
            ve_ot = 0
            sq_ot = [0] * 32
            last_pc_reader_ve = ve_mat
            last_pd_reader_ve = ve_mat
            ve_kt_evac = ve_mat
            se_last_exp = 0

            P.wait("tensor", "VE", ve_matk)
            P.wait("tensor", "GS", gq_wkv)

            for pair in range(4):
                hk0 = 2 * pair
                for hkL in range(2):
                    hk = hk0 + hkL
                    for h in range(8):
                        pp = pc if (h % 2 == 0) else pd
                        P.wait("tensor", "VE", last_pc_reader_ve if pp is pc else last_pd_reader_ve)
                        b = 64 * (h % 2)
                        v = None
                        for half in range(2):
                            for r in range(4):
                                v = P.op("tensor", lambda E_, SS, pp=pp, b=b, h=h, r=r, half=half, hk=hk: E_.matmul(
                                    pp[:, half * 512:half * 512 + 512],
                                    sb_wk2[b:b + 64, r * 1024 + hk * 128:r * 1024 + hk * 128 + 128],
                                    sb_kcm[b:b + 64, (h // 2) * 4096 + r * 1024 + half * 512:
                                           (h // 2) * 4096 + r * 1024 + half * 512 + 512],
                                    start=(r == 0), stop=(r == 3)),
                                    inc=("TM", 1) if (half == 1 and r == 3) else None)
                        P.wait("vector", "TM", v)
                        P.op("vector", lambda E_, SS, hkL=hkL, h=h, pp=pp: E_.tensor_copy(
                            sb_kt[:, hkL * 8192 + h * 1024:hkL * 8192 + h * 1024 + 1024], pp[:, 0:1024]),
                            inc=("VE", 1))
                        if pp is pc:
                            last_pc_reader_ve = P.val("VE")
                        else:
                            last_pd_reader_ve = P.val("VE")
                ve_v_evac = [0, 0]
                if pair == 0:
                    P.wait("tensor", "VE", ve_mat)
                for svt in range(64):
                    h = svt // 8
                    b = 64 * (h % 2)
                    pp = ps[svt % 2]
                    if pair > 0 and svt < 2:
                        P.wait("tensor", "SE", se_last_exp)
                    if svt >= 2:
                        P.wait("tensor", "VE", ve_v_evac[svt % 2])
                    v = None
                    for r in range(4):
                        v = P.op("tensor", lambda E_, SS, pp=pp, b=b, h=h, r=r, svt=svt, hk0=hk0: E_.matmul(
                            pp[:, 0:256],
                            sb_vcm[b:b + 64, (h // 2) * 4096 + r * 1024 + 128 * (svt % 8):
                                   (h // 2) * 4096 + r * 1024 + 128 * (svt % 8) + 128],
                            sb_wv2[b:b + 64, r * 1024 + hk0 * 128:r * 1024 + hk0 * 128 + 256],
                            start=(r == 0), stop=(r == 3)),
                            inc=("TM", 1) if r == 3 else None)
                    P.wait("vector", "TM", v)
                    P.op("vector", lambda E_, SS, svt=svt, pp=pp: E_.tensor_copy(
                        sb_vp[:, svt * 256:svt * 256 + 256], pp[:, 0:256]), inc=("VE", 1))
                    ve_v_evac[svt % 2] = P.val("VE")
                ve_kv = P.val("VE")
                P.wait("tensor", "VE", ve_kv)

                for hkL in range(2):
                    hk = hk0 + hkL
                    qbuf = (hk % 2) * 4096
                    P.wait("gpsimd", "QS0", sq_qt_all[0])
                    P.wait("gpsimd", "QS1", sq_qt_all[1])
                    if hk >= 2:
                        P.wait("gpsimd", "TM", tm_slast.get(hk - 2, 0))
                    P.op("gpsimd", lambda E_, SS, hk=hk, qbuf=qbuf: E_.dma_start(
                        out=sb_qg[:, qbuf:qbuf + 4096].rearrange("h p m -> p h m" if False else "p (h m) -> p h m", h=4),
                        in_=d_qt[4 * hk:4 * hk + 4].rearrange("h p m -> p h m")), inc=(f"QG{hk % 2}", 16))
                    P.wait("tensor", f"QG{hk % 2}", P.val(f"QG{hk % 2}"))

                    for ql in range(4):
                        qh = 4 * hk + ql
                        units = []
                        for sk_t in range(64):
                            k = sk_t // 8
                            c0 = 128 * k
                            if c0 < 512:
                                units.append((sk_t, c0, 512 - c0, True))
                                units.append((sk_t, 512, 512, False))
                            else:
                                units.append((sk_t, c0, 1024 - c0, True))
                        nu = len(units)
                        pend = []
                        for iu, (sk_t, cs, nn, msk) in enumerate(units):
                            gi = g_i[0] + iu
                            stb = gi % 4
                            if gi >= 4:
                                P.wait("tensor", "SE", se_exp[gi - 4])
                            if iu == 0:
                                P.wait("tensor", "VE", ve_ot if qh > 0 else ve_kv)
                            tms = P.op("tensor", lambda E_, SS, stb=stb, hkL=hkL, sk_t=sk_t, ql=ql,
                                 qbuf=qbuf, cs=cs, nn=nn: E_.matmul(
                                     ps[stb][:, 0:nn],
                                     sb_kt[:, hkL * 8192 + sk_t * 128:hkL * 8192 + sk_t * 128 + 128],
                                     sb_qg[:, qbuf + ql * 1024 + cs:qbuf + ql * 1024 + cs + nn],
                                     start=True, stop=True, skip_group_check=True), inc=("TM", 1))
                            if iu == nu - 1:
                                tm_slast[hk] = tms
                            P.wait("scalar", "TM", tms)
                            if gi >= 4:
                                P.wait("scalar", "TM", tm_l[gi - 4])
                            P.op("scalar", lambda E_, SS, stb=stb, nn=nn: E_.activation(
                                sb_pt[:, stb * 512:stb * 512 + nn], ps[stb][:, 0:nn], AF.Exp),
                                inc=("SE", 1))
                            se_exp[gi] = P.val("SE")
                            se_last_exp = se_exp[gi]
                            vem = None
                            if msk:
                                P.wait("vector", "SE", se_exp[gi])
                                k = sk_t // 8
                                mi = (k % 2) * 8 + (sk_t % 8)
                                P.op("vector", lambda E_, SS, stb=stb, mi=mi: E_.tensor_mul(
                                    sb_pt[:, stb * 512:stb * 512 + 128],
                                    sb_pt[:, stb * 512:stb * 512 + 128],
                                    sb_msk[:, mi * 128:mi * 128 + 128]), inc=("VE", 1))
                                vem = P.val("VE")
                            pend.append((gi, sk_t, cs, nn, stb, msk, vem, hkL))
                            if len(pend) > 3:
                                emit_pvl(*pend.pop(0))
                        for p_ in pend:
                            emit_pvl(*p_)
                        g_i[0] += nu

                        P.wait("vector", "TM", tm_l[g_i[0] - 1])
                        P.op("vector", lambda E_, SS: E_.tensor_copy(sb_ra[0:1, 0:1024], pd[0:1, 0:1024]))
                        P.op("vector", lambda E_, SS: E_.reciprocal(sb_rb[0:1, 0:1024], sb_ra[0:1, 0:1024]))
                        P.op("vector", lambda E_, SS: E_.tensor_copy(sb_rbf[0:1, 0:1024], sb_rb[0:1, 0:1024]),
                             inc=("VE", 1))
                        P.wait("tensor", "VE", P.val("VE"))
                        tv = None
                        for h2 in range(2):
                            tv = P.op("tensor", lambda E_, SS, h2=h2: E_.matmul(
                                pd[:, h2 * 512:h2 * 512 + 512], sb_onr[0:1, 0:128],
                                sb_rbf[0:1, h2 * 512:h2 * 512 + 512], start=True, stop=True),
                                inc=("TM", 1) if h2 == 1 else None)
                        P.wait("vector", "TM", tv)
                        if qh >= 2:
                            P.wait("vector", *sq_ot[qh - 2])
                        ob = (qh % 2) * 1024
                        P.op("vector", lambda E_, SS: E_.tensor_copy(sb_invl[:, :], pd[:, 0:1024]))
                        P.op("vector", lambda E_, SS, ob=ob: E_.tensor_mul(
                            sb_otsb[:, ob:ob + 1024], pc[:, 0:1024], sb_invl[:, :]), inc=("VE", 1))
                        ve_ot = P.val("VE")
                        P.wait("sync", "VE", ve_ot)
                        P.op("sync", lambda E_, SS, qh=qh, ob=ob: E_.dma_start(
                            out=d_ot[qh], in_=sb_otsb[:, ob:ob + 1024]), inc=(f"OS{qh % 2}", 16))
                        sq_ot[qh] = (f"OS{qh % 2}", P.val(f"OS{qh % 2}"))
                last_pc_reader_ve = ve_ot
                last_pd_reader_ve = ve_ot

            tm_c_end = P.val("TM")
            for qh in range(32):
                P.wait("sync", "TM", tm_c_end)
                P.wait("sync", *sq_ot[qh])
                P.op("sync", lambda E_, SS, qh=qh: E_.dma_start(
                    out=sb_ota[:, qh * 1024:qh * 1024 + 1024], in_=d_ot[qh]), inc=("OP", 16))
            sq_ota_pre = P.val("OP")
            barrier()

            if "ot" in d_dbg:
                for qh in range(32):
                    P.op("sync", lambda E_, SS, qh=qh: E_.dma_start(out=d_dbg["ot"][qh], in_=d_ot[qh]),
                         inc=("SQ", 16))
                P.wait("sync", "SQ", P.val("SQ"))
                barrier()

            if max_phase < 4:
                _finish(P); raise _Done()
            P.wait("tensor", "OP", sq_ota_pre)

            tm_wo = [0] * 32
            ve_x1 = [0] * 32
            sq_x1 = [0] * 32
            se_x1b = [0] * 32
            tm_st = [0] * 32
            gq_wo = [0] * 32
            sq_xr = [0] * 32

            def wo_load(ot):
                if ot >= 2:
                    P.wait("gpsimd", "TM", tm_wo[ot - 2])
                buf = (ot % 2) * 4096
                P.op("gpsimd", lambda E_, SS, ot=ot, buf=buf: E_.dma_start(
                    out=sb_wo[:, buf:buf + 4096], in_=d_wot[ot]), inc=(f"WO{ot % 2}", 16))
                gq_wo[ot] = (f"WO{ot % 2}", P.val(f"WO{ot % 2}"))

            def xr_load(ot):
                if ot >= 2:
                    P.wait("sync", "VE", ve_x1[ot - 2])
                buf = (ot % 2) * 1024
                P.op("sync", lambda E_, SS, ot=ot, buf=buf: E_.dma_start(
                    out=sb_xr[:, buf:buf + 1024], in_=d_xo[128 * ot:128 * ot + 128, :]), inc=(f"XR{ot % 2}", 16))
                sq_xr[ot] = (f"XR{ot % 2}", P.val(f"XR{ot % 2}"))

            wo_load(0)
            xr_load(0)
            pend_stats = []

            def emit_stats_d(ot):
                sq_slot = sb_xsq if ot % 2 == 0 else sb_tmp
                P.wait("tensor", "SE", se_x1b[ot])
                for h in range(2):
                    P.op("tensor", lambda E_, SS, ot=ot, h=h: E_.matmul(
                        ps[h][0:1, 0:512], sb_one[:, 0:1],
                        sb_x1b[:, ot * 1024 + h * 512:ot * 1024 + h * 512 + 512],
                        start=(ot == 0), stop=(ot == 31)))
                    tm_st[ot] = P.op("tensor", lambda E_, SS, ot=ot, h=h, sq_slot=sq_slot: E_.matmul(
                        ps[2 + h][0:1, 0:512], sb_one[:, 0:1], sq_slot[:, h * 512:h * 512 + 512],
                        start=(ot == 0), stop=(ot == 31)),
                        inc=("TM", 1) if h == 1 else None)

            for ot in range(32):
                if ot + 1 < 32:
                    wo_load(ot + 1)
                    xr_load(ot + 1)
                buf = (ot % 2) * 4096
                py = pc if ot % 2 == 0 else pd
                P.wait("tensor", *gq_wo[ot])
                if ot >= 2:
                    P.wait("tensor", "VE", ve_x1[ot - 2])
                for t in range(32):
                    for h in range(2):
                        tm_wo[ot] = P.op("tensor", lambda E_, SS, t=t, h=h, buf=buf, py=py: E_.matmul(
                            py[:, h * 512:h * 512 + 512],
                            sb_wo[:, buf + t * 128:buf + t * 128 + 128],
                            sb_ota[:, t * 1024 + h * 512:t * 1024 + h * 512 + 512],
                            start=(t == 0), stop=(t == 31)),
                            inc=("TM", 1) if (t == 31 and h == 1) else None)
                P.wait("vector", "TM", tm_wo[ot])
                P.wait("vector", *sq_xr[ot])
                if ot >= 2:
                    P.wait("vector", *sq_x1[ot - 2])
                xb = (ot % 2) * 1024
                P.op("vector", lambda E_, SS, xb=xb, py=py: E_.tensor_add(
                    sb_x1s[:, xb:xb + 1024], py[:, 0:1024], sb_xr[:, xb:xb + 1024]), inc=("VE", 1))
                ve_x1[ot] = P.val("VE")
                P.wait("scalar", "VE", ve_x1[ot])
                if ot >= 2:
                    P.wait("scalar", "TM", tm_st[ot - 2])
                sq_slot = sb_xsq if ot % 2 == 0 else sb_tmp
                P.op("scalar", lambda E_, SS, ot=ot, xb=xb: E_.copy(
                    sb_x1b[:, ot * 1024:ot * 1024 + 1024], sb_x1s[:, xb:xb + 1024]))
                P.op("scalar", lambda E_, SS, ot=ot, sq_slot=sq_slot: E_.activation(
                    sq_slot[:, 0:1024], sb_x1b[:, ot * 1024:ot * 1024 + 1024], AF.Square),
                    inc=("SE", 1))
                se_x1b[ot] = P.val("SE")
                if ot >= 1:
                    emit_stats_d(ot - 1)
                P.wait("sync", "VE", ve_x1[ot])
                P.op("sync", lambda E_, SS, ot=ot, xb=xb: E_.dma_start(
                    out=d_x1[ot], in_=sb_x1s[:, xb:xb + 1024]), inc=(f"XS{ot % 2}", 16))
                sq_x1[ot] = (f"XS{ot % 2}", P.val(f"XS{ot % 2}"))
            emit_stats_d(31)
            sq_x1_all = (P.val("XS0"), P.val("XS1"))

            bve2 = ln_rows(1024, 0,
                           [(ps[0], 0, 512), (ps[1], 0, 512)],
                           [(ps[2], 0, 512), (ps[3], 0, 512)])
            barrier()

            if "x1" in d_dbg:
                for ot in range(32):
                    P.op("sync", lambda E_, SS, ot=ot: E_.dma_start(out=d_dbg["x1"][ot], in_=d_x1[ot]),
                         inc=("SQ", 16))
                P.wait("sync", "SQ", P.val("SQ"))
                barrier()

            if max_phase < 5:
                _finish(P); raise _Done()
            tiles = []
            for t in range(32):
                for h in range(2):
                    tiles.append((sb_x1b[:, t * 1024 + h * 512:t * 1024 + h * 512 + 512], h * 512))
            lnv2 = emit_ln_apply(tiles, bve2)
            P.wait("tensor", "VE", lnv2)
            if "ab2" in d_dbg:
                P.wait("sync", "VE", lnv2)
                P.op("sync", lambda E_, SS: E_.dma_start(out=d_dbg["ab2"][:, 0:1024], in_=sb_abc[:, :]), inc=("SQ", 16))
                P.op("sync", lambda E_, SS: E_.dma_start(out=d_dbg["ab2"][:, 1024:2048], in_=sb_bbc[:, :]), inc=("SQ", 16))
                P.op("sync", lambda E_, SS: E_.dma_start(out=d_dbg["x1n"], in_=sb_x1b[:, :]), inc=("SQ", 16))
                P.wait("sync", "SQ", P.val("SQ"))
                barrier()

            tm_ffn = [0] * 32
            se_wb = [0] * 32
            ve_os = [0] * 32
            sq_os = [0] * 32
            sq_wf = [0] * 32
            sq_x1r = [0] * 32

            def wf_load(ot):
                if ot < 2:
                    P.wait("sync", "XS0", sq_x1_all[0])
                    P.wait("sync", "XS1", sq_x1_all[1])
                if ot >= 2:
                    P.wait("sync", "SE", se_wb[ot - 2])
                buf = (ot % 2) * 4096
                P.op("sync", lambda E_, SS, ot=ot, buf=buf: E_.dma_start(
                    out=sb_wf[:, buf:buf + 4096], in_=d_wft[ot]), inc=(f"WF{ot % 2}", 16))
                sq_wf[ot] = (f"WF{ot % 2}", P.val(f"WF{ot % 2}"))

            def x1r_load(ot):
                P.wait("sync", "XS0", sq_x1_all[0])
                P.wait("sync", "XS1", sq_x1_all[1])
                if ot >= 2:
                    P.wait("sync", "VE", ve_os[ot - 2])
                buf = (ot % 2) * 1024
                P.op("sync", lambda E_, SS, ot=ot, buf=buf: E_.dma_start(
                    out=sb_x1r[:, buf:buf + 1024], in_=d_x1[ot]), inc=(f"XB{ot % 2}", 16))
                sq_x1r[ot] = (f"XB{ot % 2}", P.val(f"XB{ot % 2}"))

            P.wait("scalar", "SQ", 16)
            wf_load(0)
            x1r_load(0)
            for ot in range(32):
                if ot + 1 < 32:
                    wf_load(ot + 1)
                    x1r_load(ot + 1)
                buf = (ot % 2) * 4096
                py = pc if ot % 2 == 0 else pd
                P.wait("scalar", *sq_wf[ot])
                if ot >= 2:
                    P.wait("scalar", "TM", tm_ffn[ot - 2])
                for it in range(32):
                    P.op("scalar", lambda E_, SS, ot=ot, it=it, buf=buf: E_.mul(
                        sb_wb[:, buf + it * 128:buf + it * 128 + 128],
                        sb_wf[:, buf + it * 128:buf + it * 128 + 128],
                        sb_scl[:, ot * 32 + it:ot * 32 + it + 1]))
                P.op("scalar", lambda E_, SS: E_.copy(sb_dum[:, 8:12], sb_dum[:, 12:16]), inc=("SE", 1))
                se_wb[ot] = P.val("SE")
                P.wait("tensor", "SE", se_wb[ot])
                if ot >= 2:
                    P.wait("tensor", "VE", ve_os[ot - 2])
                for it in range(32):
                    for h in range(2):
                        tm_ffn[ot] = P.op("tensor", lambda E_, SS, it=it, h=h, buf=buf, py=py: E_.matmul(
                            py[:, h * 512:h * 512 + 512],
                            sb_wb[:, buf + it * 128:buf + it * 128 + 128],
                            sb_x1b[:, it * 1024 + h * 512:it * 1024 + h * 512 + 512],
                            start=(it == 0), stop=(it == 31)),
                            inc=("TM", 1) if (it == 31 and h == 1) else None)
                P.wait("vector", "TM", tm_ffn[ot])
                P.wait("vector", *sq_x1r[ot])
                if ot >= 2:
                    P.wait("vector", *sq_os[ot - 2])
                xb = (ot % 2) * 1024
                P.op("vector", lambda E_, SS, xb=xb, py=py: E_.tensor_add(
                    sb_os[:, xb:xb + 1024], py[:, 0:1024], sb_x1r[:, xb:xb + 1024]), inc=("VE", 1))
                ve_os[ot] = P.val("VE")
                P.wait("sync", "VE", ve_os[ot])
                P.op("sync", lambda E_, SS, ot=ot, xb=xb: E_.dma_start(
                    out=d_out[ot], in_=sb_os[:, xb:xb + 1024]), inc=(f"OU{ot % 2}", 16))
                sq_os[ot] = (f"OU{ot % 2}", P.val(f"OU{ot % 2}"))

            P.wait("sync", "OU0", P.val("OU0"))
            P.wait("sync", "OU1", P.val("OU1"))
            P.wait("gpsimd", "OU0", P.val("OU0"))
            P.wait("gpsimd", "OU1", P.val("OU1"))


        except _Done:
            pass
        _dispatch(P, blk, sems)
    return nc


def _finish(P):
    pass


def _dispatch(P, blk, sems):
        @blk.tensor
        def _(e):
            for f in P.q["tensor"]:
                f(e, sems)

        @blk.vector
        def _(e):
            for f in P.q["vector"]:
                f(e, sems)

        @blk.scalar
        def _(e):
            for f in P.q["scalar"]:
                f(e, sems)

        @blk.gpsimd
        def _(e):
            for f in P.q["gpsimd"]:
                f(e, sems)

        @blk.sync
        def _(e):
            for f in P.q["sync"]:
                f(e, sems)


def _stage(inputs):
    x = np.asarray(inputs["L0_attn_hidden_states"], dtype=np.float32)[0]
    Wq = np.asarray(inputs["L0_attn_Wq"], dtype=np.float32)
    Wkd = np.asarray(inputs["L0_attn_Wk_down"], dtype=np.float32)
    Wvd = np.asarray(inputs["L0_attn_Wv_down"], dtype=np.float32)
    Wku = np.asarray(inputs["L0_attn_Wk_up"], dtype=np.float32)
    Wvu = np.asarray(inputs["L0_attn_Wv_up"], dtype=np.float32)
    Wo = np.asarray(inputs["L0_attn_Wo"], dtype=np.float32)
    Wf = np.asarray(inputs["L0_ffn_weight"], dtype=np.float32)
    scl = np.asarray(inputs["L0_ffn_weight_scale"], dtype=np.float32)

    wd = np.concatenate(
        [np.concatenate([Wkd[32 * h:32 * h + 32], Wvd[32 * h:32 * h + 32]], axis=0)
         for h in range(8)], axis=0)
    wdt = np.ascontiguousarray(
        wd.T.reshape(32, 128, 512).transpose(1, 0, 2).reshape(128, 32 * 512))

    xt_sw = np.ascontiguousarray(
        x.T.reshape(32, 128, 16, 512).transpose(2, 1, 0, 3).reshape(16, 128, 32 * 512))

    def tiled_T(w):
        return np.ascontiguousarray(
            w.T.reshape(32, 128, 32, 128).transpose(2, 1, 0, 3).reshape(32, 128, 32 * 128))

    wqt = tiled_T(Wq)
    wot = tiled_T(Wo)
    wft = tiled_T(Wf)

    def dup2(wu):
        wt = wu.T
        out = np.empty((128, 4, 1024), np.float32)
        for r in range(4):
            for b in range(2):
                out[64 * b:64 * b + 64, r] = wt[64 * r:64 * r + 64]
        return np.ascontiguousarray(out)

    wk2 = dup2(Wku)
    wv2 = dup2(Wvu)
    scale_pp = np.ascontiguousarray(
        np.broadcast_to(scl.reshape(1, 1024), (128, 1024)).astype(np.float32))

    shared = dict(xt_sw=xt_sw, wdt_sw=wdt, wqt_sw=wqt, wot_sw=wot, wft_sw=wft,
                  wk2=wk2, wv2=wv2, scale_pp=scale_pp)

    tri = np.tril(np.ones((128, 128), np.float32)).T
    in_maps = []
    for c in range(NCORE):
        blocks = zig_blocks(c)
        rows = np.concatenate([np.arange(b * 128, b * 128 + 128) for b in blocks])
        xt_own = np.ascontiguousarray(x[rows].T)
        mask = np.empty((16, 128, 128), np.float32)
        for kpar in range(2):
            bpos = c if kpar == 0 else 7 - c
            for m in range(8):
                if m < bpos:
                    mask[kpar * 8 + m] = 1.0
                elif m == bpos:
                    mask[kpar * 8 + m] = tri
                else:
                    mask[kpar * 8 + m] = 0.0
        im = dict(shared)
        im["xt_own"] = xt_own
        im["xo_sw"] = np.ascontiguousarray(
            xt_own.reshape(32, 128, 1024).transpose(1, 0, 2).reshape(128, 32 * 1024))
        im["mask01"] = np.ascontiguousarray(mask)
        in_maps.append(im)
    return in_maps


def kernel(**inputs):
    in_maps = _stage(inputs)
    nc = build_program(tuple(DEBUG_TAPS), MAX_PHASE, NCHUNKS, A_LEVEL, LR_LEVEL, NCORES_RUN)
    res = run_bass_kernel_spmd(nc, in_maps[:NCORES_RUN], core_ids=list(range(NCORES_RUN)))
    out = np.zeros((1, S, HID), np.float32)
    for c in range(NCORES_RUN):
        x2t = np.asarray(res.results[c]["o"]).reshape(HID, 1024)
        for j, b in enumerate(zig_blocks(c)):
            out[0, b * 128:(b + 1) * 128, :] = x2t[:, j * 128:(j + 1) * 128].T
    if DEBUG_TAPS:
        kernel._dbg = [{k: np.asarray(v) for k, v in r.items() if k.startswith("dbg_")}
                       for r in res.results]
    return out


# version 15
# speedup vs baseline: 1.1123x; 1.1123x over previous
import math
import numpy as np
from contextlib import ExitStack

import concourse.bass as bass
import concourse.mybir as mybir
from concourse.bass_utils import run_bass_kernel_spmd

F32 = mybir.dt.float32
BF16 = mybir.dt.bfloat16
AF = mybir.ActivationFunctionType

NCORE = 8
S = 8192
HID = 4096
QSCALE = 1.0 / math.sqrt(128.0)
EPS = 1e-5

DEBUG_TAPS = []
MAX_PHASE = 5
NCORES_RUN = 8
NCHUNKS = 16
A_LEVEL = 4
LR_LEVEL = 4


def zig_blocks(c):
    return sorted([c, 15 - c, 16 + c, 31 - c, 32 + c, 47 - c, 48 + c, 63 - c])


BIG0 = 0
BIG1 = 65536
MID = 131072
SM_PT = 163840
SM_OTSB = 167936
SM_INVL = 172032
SM_ABC = 176128
SM_BBC = 178176
SM_TMP = 180224
SM_SCL = 182272
SM_MSK = 186368
SM_XSQ = 190464
SM_RA = 192512
SM_RB = 196608
SM_RC = 200704
SM_RBF = 204800
SM_ONE = 208896
SM_ONR = 208928
SM_DUM = 209216

ENGS = ("tensor", "vector", "scalar", "gpsimd", "sync")


class _Done(Exception):
    pass


class Prog:
    def __init__(self):
        self.q = {e: [] for e in ENGS}
        self.cnt = {}
        self.nbar = 0

    def bump(self, sem, n):
        self.cnt[sem] = self.cnt.get(sem, 0) + n
        return self.cnt[sem]

    def val(self, sem):
        return self.cnt.get(sem, 0)

    def wait(self, eng, sem, v):
        if v <= 0:
            return
        assert v <= self.val(sem), f"{eng} waits {sem}>={v} but only {self.val(sem)} emitted"
        self.q[eng].append(lambda E, SS, sem=sem, v=v: E.wait_ge(SS[sem], v))

    def op(self, eng, builder, inc=None):
        if inc is not None:
            sem, n = inc
            v = self.bump(sem, n)
            self.q[eng].append(
                lambda E, SS, b=builder, sem=sem, n=n: b(E, SS).then_inc(SS[sem], n))
            return v
        self.q[eng].append(lambda E, SS, b=builder: b(E, SS))
        return None


def build_program(debug_taps=(), max_phase=5, nchunks=16, a_level=4, lr_level=4, ncores=8):
    nc = bass.Bass("TRN2", target_bir_lowering=False)

    d_xt = nc.dram_tensor("xt_sw", [32, 128, 32 * 256], F32, kind="ExternalInput").ap()
    d_xo = nc.dram_tensor("xt_own", [HID, 1024], F32, kind="ExternalInput").ap()
    d_xo2 = nc.dram_tensor("xo_sw", [128, 32 * 1024], F32, kind="ExternalInput").ap()
    d_wdt = nc.dram_tensor("wdt_sw", [128, 32 * 512], F32, kind="ExternalInput").ap()
    d_wqt = nc.dram_tensor("wqt_sw", [32, 128, 32 * 128], F32, kind="ExternalInput").ap()
    d_wot = nc.dram_tensor("wot_sw", [32, 128, 32 * 128], F32, kind="ExternalInput").ap()
    d_wft = nc.dram_tensor("wft_sw", [32, 128, 32 * 128], F32, kind="ExternalInput").ap()
    d_wk2 = nc.dram_tensor("wk2", [128, 4, 1024], F32, kind="ExternalInput").ap()
    d_wv2 = nc.dram_tensor("wv2", [128, 4, 1024], F32, kind="ExternalInput").ap()
    d_scl = nc.dram_tensor("scale_pp", [128, 1024], F32, kind="ExternalInput").ap()
    d_msk = nc.dram_tensor("mask01", [16, 128, 128], F32, kind="ExternalInput").ap()

    d_qt = nc.dram_tensor("qt_spill", [32, 128, 1024], BF16, kind="Internal").ap()
    d_ot = nc.dram_tensor("ot_spill", [32, 128, 1024], BF16, kind="Internal").ap()
    d_x1 = nc.dram_tensor("x1_spill", [32, 128, 1024], F32, kind="Internal").ap()
    d_out = nc.dram_tensor("o", [32, 128, 1024], F32, kind="ExternalOutput").ap()

    d_dbg = {}
    if "cfull" in debug_taps:
        d_dbg["cfull"] = nc.dram_tensor("dbg_cfull", [128, 4 * S], BF16, kind="ExternalOutput").ap()
    if "ab" in debug_taps:
        d_dbg["ab"] = nc.dram_tensor("dbg_ab", [128, 2048], BF16, kind="ExternalOutput").ap()
        d_dbg["rows"] = nc.dram_tensor("dbg_rows", [1, 3072], F32, kind="ExternalOutput").ap()
    if "ab2" in debug_taps:
        d_dbg["ab2"] = nc.dram_tensor("dbg_ab2", [128, 2048], BF16, kind="ExternalOutput").ap()
        d_dbg["x1n"] = nc.dram_tensor("dbg_x1n", [128, 32 * 1024], BF16, kind="ExternalOutput").ap()
    if "qt" in debug_taps:
        d_dbg["qt"] = nc.dram_tensor("dbg_qt", [32, 128, 1024], BF16, kind="ExternalOutput").ap()
    if "ot" in debug_taps:
        d_dbg["ot"] = nc.dram_tensor("dbg_ot", [32, 128, 1024], BF16, kind="ExternalOutput").ap()
    if "x1" in debug_taps:
        d_dbg["x1"] = nc.dram_tensor("dbg_x1", [32, 128, 1024], F32, kind="ExternalOutput").ap()

    P = Prog()

    with ExitStack() as ctx:
        E = ctx.enter_context
        _SBUF_BASE = 16512

        def A(name, shape, dtype, offset):
            return nc.alloc_sbuf_tensor_at(name, shape, dtype, offset=offset + _SBUF_BASE)

        sb_pt = A("pt", [128, 4 * 512], BF16, offset=SM_PT)
        sb_otsb = A("otsb", [128, 2 * 1024], BF16, offset=SM_OTSB)
        sb_invl = A("invl", [128, 1024], F32, offset=SM_INVL)
        sb_abc = A("abc", [128, 1024], BF16, offset=SM_ABC)
        sb_bbc = A("bbc", [128, 1024], BF16, offset=SM_BBC)
        sb_tmp = A("tmpb", [128, 1024], BF16, offset=SM_TMP)
        sb_scl = A("sclpp", [128, 1024], F32, offset=SM_SCL)
        sb_msk = A("maskb", [128, 16 * 128], BF16, offset=SM_MSK)
        sb_xsq = A("xsq", [128, 1024], BF16, offset=SM_XSQ)
        sb_ra = A("rowa", [1, 1024], F32, offset=SM_RA)
        sb_rb = A("rowb", [1, 1024], F32, offset=SM_RB)
        sb_rc = A("rowc", [1, 1024], F32, offset=SM_RC)
        sb_rbf = A("rowbf", [1, 2048], BF16, offset=SM_RBF)
        sb_one = A("onec", [128, 8], BF16, offset=SM_ONE)
        sb_onr = A("oner", [1, 128], BF16, offset=SM_ONR)
        sb_dum = A("dum", [128, 16], F32, offset=SM_DUM)

        sb_xt = A("xt", [128, 2 * 32 * 512], BF16, offset=BIG0)
        sb_xo = A("xo", [128, 32 * 1024], BF16, offset=BIG0)
        sb_cf = A("cfull", [128, 4 * S], BF16, offset=BIG1)
        sb_wdt = A("wdtb", [128, 32 * 512], BF16, offset=MID)
        sb_wq = A("wqb", [128, 2 * 32 * 128], BF16, offset=MID)
        sb_qt = A("qtb", [128, 2 * 1024], BF16, offset=MID + 16384)
        sb_kcm = A("kcmp", [128, 16384], BF16, offset=BIG0)
        sb_vcm = A("vcmp", [128, 16384], BF16, offset=BIG0 + 32768)
        sb_kt = A("ktp", [128, 2 * 8192], BF16, offset=BIG1)
        sb_vp = A("vp", [128, 64 * 256], BF16, offset=BIG1 + 32768)
        sb_qg = A("qg", [128, 2 * 4 * 1024], BF16, offset=MID)
        sb_wk2 = A("wk2b", [128, 4 * 1024], BF16, offset=MID + 16384)
        sb_wv2 = A("wv2b", [128, 4 * 1024], BF16, offset=MID + 24576)
        sb_ota = A("otall", [128, 32 * 1024], BF16, offset=BIG0)
        sb_x1b = A("x1b", [128, 32 * 1024], BF16, offset=BIG1)
        sb_wo = A("wob", [128, 2 * 32 * 128], BF16, offset=MID)
        sb_xr = A("xr", [128, 2 * 1024], F32, offset=MID + 16384)
        sb_x1s = A("x1s", [128, 2 * 1024], F32, offset=MID + 24576)
        sb_wf = A("wfb", [128, 2 * 32 * 128], F32, offset=MID)
        sb_wb = A("wbb", [128, 2 * 32 * 128], BF16, offset=BIG0)
        sb_x1r = A("x1r", [128, 2 * 1024], F32, offset=BIG0 + 16384)
        sb_os = A("osb", [128, 2 * 1024], F32, offset=BIG0 + 24576)

        ps = [E(nc.psum_tensor(f"ps{i}", [128, 512], F32)) for i in range(4)]
        pc = E(nc.psum_tensor("pc", [128, 1024], F32))
        pd = E(nc.psum_tensor("pd", [128, 1024], F32))

        sems = {}
        for s_ in ("GQ", "SQ", "TM", "VE", "SE", "PB",
                   "CH0", "CH1", "CH2", "CH3", "WQ0", "WQ1", "QG0", "QG1", "WO0", "WO1",
                   "QS0", "QS1", "OS0", "OS1", "OP", "XR0", "XR1",
                   "XS0", "XS1", "WF0", "WF1", "XB0", "XB1", "OU0", "OU1", "GS"):
            sems[s_] = E(nc.semaphore(s_))

        blk = E(nc.Block())

        def barrier():
            P.nbar += 1
            tgt = 5 * P.nbar
            P.bump("PB", 5)
            P.q["tensor"].append(lambda E_, SS: E_.matmul(
                ps[0][0:8, 0:4], sb_onr[0:1, 0:8], sb_onr[0:1, 4:8],
                start=True, stop=True, skip_group_check=True).then_inc(SS["PB"], 1))
            P.q["sync"].append(lambda E_, SS: E_.nop().then_inc(SS["PB"], 1))
            P.q["gpsimd"].append(lambda E_, SS: E_.nop().then_inc(SS["PB"], 1))
            P.q["vector"].append(
                lambda E_, SS: E_.tensor_copy(sb_dum[:, 0:4], sb_dum[:, 4:8]).then_inc(SS["PB"], 1))
            P.q["scalar"].append(
                lambda E_, SS: E_.copy(sb_dum[:, 8:12], sb_dum[:, 12:16]).then_inc(SS["PB"], 1))
            for e in ENGS:
                P.wait(e, "PB", tgt)

        def emit_stats(src_aps, sumA, sumB, n_si):
            tm_sq = {}
            occ = {}
            lastocc = {}
            for i, (_, co) in enumerate(src_aps):
                lastocc[co] = i
            for i, (src, co) in enumerate(src_aps):
                nw = src.shape[-1]
                slot = (i % 2) * 512
                if i % 2 == 0:
                    if i >= 2:
                        P.wait("vector", "TM", tm_sq[i - 2])
                    P.op("vector", lambda E_, SS, src=src, slot=slot, nw=nw: E_.tensor_mul(
                        sb_xsq[:, slot:slot + nw], src, src), inc=("VE", 1))
                    P.wait("tensor", "VE", P.val("VE"))
                else:
                    if i >= 2:
                        P.wait("scalar", "TM", tm_sq[i - 2])
                    P.op("scalar", lambda E_, SS, src=src, slot=slot, nw=nw: E_.activation(
                        sb_xsq[:, slot:slot + nw], src, AF.Square), inc=("SE", 1))
                    P.wait("tensor", "SE", P.val("SE"))
                pa, ca = sumA[co]
                pb, cb = sumB[co]
                stA = co not in occ
                occ[co] = True
                last = lastocc[co] == i
                P.op("tensor", lambda E_, SS, src=src, pa=pa, ca=ca, stA=stA, last=last, nw=nw: E_.matmul(
                    pa[0:1, ca:ca + nw], sb_one[:, 0:1], src, start=stA, stop=last))
                v = P.op("tensor", lambda E_, SS, pb=pb, cb=cb, slot=slot, stA=stA, last=last, nw=nw: E_.matmul(
                    pb[0:1, cb:cb + nw], sb_one[:, 0:1], sb_xsq[:, slot:slot + nw],
                    start=stA, stop=last), inc=("TM", 1))
                tm_sq[i] = v

        def ln_rows(n_si, out_co, rowA_srcs, rowB_srcs):
            P.wait("vector", "TM", P.val("TM"))
            off = 0
            for p_, c_, n_ in rowA_srcs:
                P.op("vector", lambda E_, SS, p_=p_, c_=c_, n_=n_, off=off: E_.tensor_copy(
                    sb_ra[0:1, off:off + n_], p_[0:1, c_:c_ + n_]))
                off += n_
            off = 0
            for p_, c_, n_ in rowB_srcs:
                P.op("vector", lambda E_, SS, p_=p_, c_=c_, n_=n_, off=off: E_.tensor_copy(
                    sb_rb[0:1, off:off + n_], p_[0:1, c_:c_ + n_]))
                off += n_
            n = n_si
            P.op("vector", lambda E_, SS, n=n: E_.tensor_scalar_mul(sb_ra[0:1, 0:n], sb_ra[0:1, 0:n], 1.0 / HID))
            P.op("vector", lambda E_, SS, n=n: E_.tensor_scalar_mul(sb_rb[0:1, 0:n], sb_rb[0:1, 0:n], 1.0 / HID))
            P.op("vector", lambda E_, SS, n=n: E_.tensor_mul(sb_rc[0:1, 0:n], sb_ra[0:1, 0:n], sb_ra[0:1, 0:n]))
            P.op("vector", lambda E_, SS, n=n: E_.tensor_sub(sb_rb[0:1, 0:n], sb_rb[0:1, 0:n], sb_rc[0:1, 0:n]))
            P.op("vector", lambda E_, SS, n=n: E_.tensor_scalar_add(sb_rb[0:1, 0:n], sb_rb[0:1, 0:n], EPS),
                 inc=("VE", 1))
            P.wait("scalar", "VE", P.val("VE"))
            P.op("scalar", lambda E_, SS, n=n: E_.activation(sb_rc[0:1, 0:n], sb_rb[0:1, 0:n], AF.Ln))
            P.op("scalar", lambda E_, SS, n=n: E_.activation(sb_rc[0:1, 0:n], sb_rc[0:1, 0:n], AF.Exp, scale=-0.5),
                 inc=("SE", 1))
            P.wait("vector", "SE", P.val("SE"))
            P.op("vector", lambda E_, SS, n=n: E_.tensor_copy(sb_rbf[0:1, 0:n], sb_rc[0:1, 0:n]))
            P.op("vector", lambda E_, SS, n=n: E_.tensor_mul(sb_ra[0:1, 0:n], sb_ra[0:1, 0:n], sb_rc[0:1, 0:n]))
            P.op("vector", lambda E_, SS, n=n: E_.tensor_scalar_mul(sb_ra[0:1, 0:n], sb_ra[0:1, 0:n], -1.0))
            P.op("vector", lambda E_, SS, n=n: E_.tensor_copy(sb_rbf[0:1, 1024:1024 + n], sb_ra[0:1, 0:n]),
                 inc=("VE", 1))
            P.wait("tensor", "VE", P.val("VE"))
            for h in range(0, n, 512):
                nn = min(512, n - h)
                P.op("tensor", lambda E_, SS, h=h, nn=nn: E_.matmul(
                    pc[:, h:h + nn], sb_onr[0:1, 0:128], sb_rbf[0:1, h:h + nn], start=True, stop=True))
            hs = list(range(0, n, 512))
            for h in hs[:-1]:
                P.op("tensor", lambda E_, SS, h=h: E_.matmul(
                    pd[:, h:h + 512], sb_onr[0:1, 0:128], sb_rbf[0:1, 1024 + h:1024 + h + 512],
                    start=True, stop=True))
            h = hs[-1]
            nn = min(512, n - h)
            tv = P.op("tensor", lambda E_, SS, h=h, nn=nn: E_.matmul(
                pd[:, h:h + nn], sb_onr[0:1, 0:128], sb_rbf[0:1, 1024 + h:1024 + h + nn],
                start=True, stop=True), inc=("TM", 1))
            P.wait("vector", "TM", tv)
            P.op("vector", lambda E_, SS, n=n, co=out_co: E_.tensor_copy(sb_abc[:, co:co + n], pc[:, 0:n]))
            P.op("vector", lambda E_, SS, n=n, co=out_co: E_.tensor_copy(sb_bbc[:, co:co + n], pd[:, 0:n]),
                 inc=("VE", 1))
            return P.val("VE")

        def emit_ln_apply(tiles, bcast_ve):
            P.wait("vector", "VE", bcast_ve)
            for x_ap, co in tiles:
                nw = x_ap.shape[-1]
                P.op("vector", lambda E_, SS, x_ap=x_ap, co=co, nw=nw: E_.tensor_mul(
                    sb_tmp[:, 0:nw], x_ap, sb_abc[:, co:co + nw]))
                P.op("vector", lambda E_, SS, x_ap=x_ap, co=co, nw=nw: E_.tensor_add(
                    x_ap, sb_tmp[:, 0:nw], sb_bbc[:, co:co + nw]))
            return P.op("vector", lambda E_, SS: E_.tensor_copy(sb_dum[:, 0:4], sb_dum[:, 4:8]),
                        inc=("VE", 1))

        try:
            P.op("vector", lambda E_, SS: E_.memset(sb_one[:, :], 1.0))
            P.op("vector", lambda E_, SS: E_.memset(sb_onr[:, :], 1.0))
            P.op("vector", lambda E_, SS: E_.memset(sb_dum[:, :], 0.0), inc=("VE", 1))
            P.op("sync", lambda E_, SS: E_.dma_start(out=sb_scl[:, :], in_=d_scl), inc=("SQ", 16))
            P.op("gpsimd", lambda E_, SS: E_.dma_start(
                out=sb_msk[:, :].rearrange("p (k m) -> p k m", k=16),
                in_=d_msk.rearrange("k p m -> p k m")), inc=("GS", 16))
            gs_msk = P.val("GS")

            P.op("gpsimd", lambda E_, SS: E_.dma_start(out=sb_wdt[:, :], in_=d_wdt), inc=("GS", 16))
            gs_wdt = P.val("GS")
            P.wait("tensor", "GS", gs_wdt)

            NCH = 32
            CW = 256
            tm_down = [0] * NCH
            gq_ch = [0] * NCH
            ve_bcast_prev = 0
            se_cast = {}

            def chunk_load(j):
                if j >= 4:
                    P.wait("gpsimd", "TM", tm_down[j - 4])
                buf = (j % 4) * 32 * CW
                P.op("gpsimd", lambda E_, SS, j=j, buf=buf: E_.dma_start(
                    out=sb_xt[:, buf:buf + 32 * CW], in_=d_xt[j]),
                    inc=(f"CH{j % 4}", 16))
                gq_ch[j] = (f"CH{j % 4}", P.val(f"CH{j % 4}"))

            def emit_down(j, lnv):
                buf = (j % 4) * 32 * CW
                P.wait("tensor", "VE", lnv)
                for ct in range(4):
                    v = None
                    for t in range(32):
                        v = P.op("tensor", lambda E_, SS, ct=ct, t=t, buf=buf: E_.matmul(
                            ps[ct][:, 0:CW],
                            sb_wdt[:, t * 512 + ct * 128:t * 512 + ct * 128 + 128],
                            sb_xt[:, buf + t * CW:buf + t * CW + CW],
                            start=(t == 0), stop=(t == 31)),
                            inc=("TM", 1) if t == 31 else None)
                    P.wait("vector", "TM", v)
                    P.op("vector", lambda E_, SS, ct=ct, j=j: E_.tensor_copy(
                        sb_cf[:, ct * S + CW * j:ct * S + CW * j + CW], ps[ct][:, 0:CW]),
                        inc=("VE", 1))
                tm_down[j] = P.val("TM")

            for j in range(3):
                chunk_load(j)
            pend_ln = None
            for j in range(NCH):
                if pend_ln is not None:
                    emit_down(*pend_ln)
                if j + 3 < NCH:
                    chunk_load(j + 3)
                buf = (j % 4) * 32 * CW
                co = (j % 4) * CW
                P.wait("scalar", *gq_ch[j])
                P.wait("tensor", *gq_ch[j])
                P.wait("vector", *gq_ch[j])
                P.wait("tensor", "VE", ve_bcast_prev)
                srcs = [(sb_xt[:, buf + t * CW:buf + t * CW + CW], 0) for t in range(32)]
                emit_stats(srcs, {0: (pc, 0)}, {0: (pd, 0)}, CW)
                ve_bcast_prev = ln_rows(CW, co, [(pc, 0, CW)], [(pd, 0, CW)])
                lnv = emit_ln_apply([(sb_xt[:, buf + t * CW:buf + t * CW + CW], co) for t in range(32)],
                                    ve_bcast_prev)
                pend_ln = (j, lnv)
            emit_down(*pend_ln)
            barrier()

            if "cfull" in d_dbg:
                P.op("sync", lambda E_, SS: E_.dma_start(out=d_dbg["cfull"], in_=sb_cf[:, :]), inc=("SQ", 16))
                P.wait("sync", "SQ", P.val("SQ"))
                barrier()
            if "ab" in d_dbg:
                P.op("sync", lambda E_, SS: E_.dma_start(out=d_dbg["ab"][:, 0:1024], in_=sb_abc[:, :]), inc=("SQ", 16))
                P.op("sync", lambda E_, SS: E_.dma_start(out=d_dbg["ab"][:, 1024:2048], in_=sb_bbc[:, :]), inc=("SQ", 16))
                P.op("sync", lambda E_, SS: E_.dma_start(out=d_dbg["rows"][0:1, 0:1024], in_=sb_ra[0:1, :]), inc=("SQ", 16))
                P.op("sync", lambda E_, SS: E_.dma_start(out=d_dbg["rows"][0:1, 1024:2048], in_=sb_rb[0:1, :]), inc=("SQ", 16))
                P.op("sync", lambda E_, SS: E_.dma_start(out=d_dbg["rows"][0:1, 2048:3072], in_=sb_rc[0:1, :]), inc=("SQ", 16))
                P.wait("sync", "SQ", P.val("SQ"))
                barrier()

            if max_phase < 2:
                _finish(P); raise _Done()
            P.op("gpsimd", lambda E_, SS: E_.dma_start(out=sb_xo[:, :], in_=d_xo2), inc=("GS", 16))
            gq_xo = P.val("GS")
            P.wait("scalar", "GS", gq_xo)
            P.wait("tensor", "GS", gq_xo)
            P.wait("tensor", "VE", ve_bcast_prev)
            srcs = []
            for t in range(32):
                for h in range(2):
                    srcs.append((sb_xo[:, t * 1024 + h * 512:t * 1024 + h * 512 + 512], h * 512))
            emit_stats(srcs, {0: (pc, 0), 512: (pc, 512)}, {0: (pd, 0), 512: (pd, 512)}, 1024)
            bve = ln_rows(1024, 0, [(pc, 0, 1024)], [(pd, 0, 1024)])
            tiles = []
            for t in range(32):
                for h in range(2):
                    tiles.append((sb_xo[:, t * 1024 + h * 512:t * 1024 + h * 512 + 512], h * 512))
            lnv = emit_ln_apply(tiles, bve)

            tm_q = [0] * 32
            sq_qt = [0] * 32
            ve_qt_evac = [0] * 32

            def wq_load(dt):
                if dt >= 2:
                    P.wait("gpsimd", "TM", tm_q[dt - 2])
                buf = (dt % 2) * 32 * 128
                P.op("gpsimd", lambda E_, SS, dt=dt, buf=buf: E_.dma_start(
                    out=sb_wq[:, buf:buf + 32 * 128], in_=d_wqt[dt]), inc=(f"WQ{dt % 2}", 16))
                return (f"WQ{dt % 2}", P.val(f"WQ{dt % 2}"))

            gq_wq = [0] * 32
            gq_wq[0] = wq_load(0)
            P.wait("tensor", "VE", lnv)
            for dt in range(32):
                if dt + 1 < 32:
                    gq_wq[dt + 1] = wq_load(dt + 1)
                buf = (dt % 2) * 32 * 128
                py = pc if dt % 2 == 0 else pd
                P.wait("tensor", *gq_wq[dt])
                if dt >= 2:
                    P.wait("tensor", "VE", ve_qt_evac[dt - 2])
                for t in range(32):
                    for h in range(2):
                        tm_q[dt] = P.op("tensor", lambda E_, SS, t=t, h=h, buf=buf, py=py: E_.matmul(
                            py[:, h * 512:h * 512 + 512],
                            sb_wq[:, buf + t * 128:buf + t * 128 + 128],
                            sb_xo[:, t * 1024 + h * 512:t * 1024 + h * 512 + 512],
                            start=(t == 0), stop=(t == 31)),
                            inc=("TM", 1) if (t == 31 and h == 1) else None)
                P.wait("vector", "TM", tm_q[dt])
                if dt >= 2:
                    P.wait("vector", *sq_qt[dt - 2])
                qb = (dt % 2) * 1024
                P.op("vector", lambda E_, SS, qb=qb, py=py: E_.tensor_scalar_mul(
                    sb_qt[:, qb:qb + 1024], py[:, 0:1024], QSCALE), inc=("VE", 1))
                ve_qt_evac[dt] = P.val("VE")
                P.wait("sync", "VE", ve_qt_evac[dt])
                P.op("sync", lambda E_, SS, dt=dt, qb=qb: E_.dma_start(
                    out=d_qt[dt], in_=sb_qt[:, qb:qb + 1024]), inc=(f"QS{dt % 2}", 16))
                sq_qt[dt] = (f"QS{dt % 2}", P.val(f"QS{dt % 2}"))
            sq_qt_all = (P.val("QS0"), P.val("QS1"))
            barrier()

            if "qt" in d_dbg:
                for dt in range(32):
                    P.op("sync", lambda E_, SS, dt=dt: E_.dma_start(out=d_dbg["qt"][dt], in_=d_qt[dt]),
                         inc=("SQ", 16))
                P.wait("sync", "SQ", P.val("SQ"))
                barrier()

            if max_phase < 3:
                _finish(P); raise _Done()
            P.wait("gpsimd", "QS0", sq_qt_all[0])
            P.wait("gpsimd", "QS1", sq_qt_all[1])
            P.op("gpsimd", lambda E_, SS: E_.dma_start(
                out=sb_wk2[:, :].rearrange("p (r m) -> p r m", r=4), in_=d_wk2), inc=("GS", 16))
            P.op("gpsimd", lambda E_, SS: E_.dma_start(
                out=sb_wv2[:, :].rearrange("p (r m) -> p r m", r=4), in_=d_wv2), inc=("GS", 16))
            gq_wkv = P.val("GS")

            P.wait("vector", "GS", gs_msk)
            for rset in (range(4), range(4, 8)):
                for h in range(8):
                    b = 64 * (h % 2)
                    cbase = (h // 2) * S
                    for r in rset:
                        dst = sb_kcm if r < 4 else sb_vcm
                        ro = (r % 4) * 1024
                        P.op("vector", lambda E_, SS, b=b, cbase=cbase, r=r, dst=dst, ro=ro, h=h: E_.tensor_copy(
                            dst[b:b + 64, (h // 2) * 4096 + ro:(h // 2) * 4096 + ro + 1024],
                            sb_cf[b:b + 64, cbase + r:cbase + r + 8 * 1023 + 1:8]))
                if rset.start == 0:
                    ve_matk = P.op("vector", lambda E_, SS: E_.tensor_copy(sb_dum[:, 0:4], sb_dum[:, 4:8]),
                                   inc=("VE", 1))
            ve_mat = P.op("vector", lambda E_, SS: E_.tensor_copy(sb_dum[:, 0:4], sb_dum[:, 4:8]),
                          inc=("VE", 1))

            g_i = [0]
            se_exp = {}
            tm_l = {}

            def emit_pvl(gi, sk_t, cs, nn, stb, msk, vem, hkL):
                if msk:
                    P.wait("tensor", "VE", vem)
                else:
                    P.wait("tensor", "SE", se_exp[gi])
                P.op("tensor", lambda E_, SS, sk_t=sk_t, cs=cs, nn=nn, stb=stb, hkL=hkL: E_.matmul(
                    pc[:, cs:cs + nn],
                    sb_vp[:, sk_t * 256 + hkL * 128:sk_t * 256 + hkL * 128 + 128],
                    sb_pt[:, stb * 512:stb * 512 + nn],
                    start=(sk_t == 0), stop=(sk_t == 63), skip_group_check=True))
                v = P.op("tensor", lambda E_, SS, cs=cs, nn=nn, stb=stb, sk_t=sk_t: E_.matmul(
                    pd[0:1, cs:cs + nn], sb_one[:, 0:1],
                    sb_pt[:, stb * 512:stb * 512 + nn],
                    start=(sk_t == 0), stop=(sk_t == 63), skip_group_check=True), inc=("TM", 1))
                tm_l[gi] = v
            tm_slast = {}
            ve_ot = 0
            sq_ot = [0] * 32
            last_pc_reader_ve = ve_mat
            last_pd_reader_ve = ve_mat
            ve_kt_evac = ve_mat
            se_last_exp = 0

            P.wait("tensor", "VE", ve_matk)
            P.wait("tensor", "GS", gq_wkv)

            for pair in range(4):
                hk0 = 2 * pair
                for hkL in range(2):
                    hk = hk0 + hkL
                    for h in range(8):
                        pp = pc if (h % 2 == 0) else pd
                        P.wait("tensor", "VE", last_pc_reader_ve if pp is pc else last_pd_reader_ve)
                        b = 64 * (h % 2)
                        v = None
                        for half in range(2):
                            for r in range(4):
                                v = P.op("tensor", lambda E_, SS, pp=pp, b=b, h=h, r=r, half=half, hk=hk: E_.matmul(
                                    pp[:, half * 512:half * 512 + 512],
                                    sb_wk2[b:b + 64, r * 1024 + hk * 128:r * 1024 + hk * 128 + 128],
                                    sb_kcm[b:b + 64, (h // 2) * 4096 + r * 1024 + half * 512:
                                           (h // 2) * 4096 + r * 1024 + half * 512 + 512],
                                    start=(r == 0), stop=(r == 3)),
                                    inc=("TM", 1) if (half == 1 and r == 3) else None)
                        P.wait("vector", "TM", v)
                        P.op("vector", lambda E_, SS, hkL=hkL, h=h, pp=pp: E_.tensor_copy(
                            sb_kt[:, hkL * 8192 + h * 1024:hkL * 8192 + h * 1024 + 1024], pp[:, 0:1024]),
                            inc=("VE", 1))
                        if pp is pc:
                            last_pc_reader_ve = P.val("VE")
                        else:
                            last_pd_reader_ve = P.val("VE")
                ve_v_evac = [0, 0]
                if pair == 0:
                    P.wait("tensor", "VE", ve_mat)
                for svt in range(64):
                    h = svt // 8
                    b = 64 * (h % 2)
                    pp = ps[svt % 2]
                    if pair > 0 and svt < 2:
                        P.wait("tensor", "SE", se_last_exp)
                    if svt >= 2:
                        P.wait("tensor", "VE", ve_v_evac[svt % 2])
                    v = None
                    for r in range(4):
                        v = P.op("tensor", lambda E_, SS, pp=pp, b=b, h=h, r=r, svt=svt, hk0=hk0: E_.matmul(
                            pp[:, 0:256],
                            sb_vcm[b:b + 64, (h // 2) * 4096 + r * 1024 + 128 * (svt % 8):
                                   (h // 2) * 4096 + r * 1024 + 128 * (svt % 8) + 128],
                            sb_wv2[b:b + 64, r * 1024 + hk0 * 128:r * 1024 + hk0 * 128 + 256],
                            start=(r == 0), stop=(r == 3)),
                            inc=("TM", 1) if r == 3 else None)
                    P.wait("vector", "TM", v)
                    P.op("vector", lambda E_, SS, svt=svt, pp=pp: E_.tensor_copy(
                        sb_vp[:, svt * 256:svt * 256 + 256], pp[:, 0:256]), inc=("VE", 1))
                    ve_v_evac[svt % 2] = P.val("VE")
                ve_kv = P.val("VE")
                P.wait("tensor", "VE", ve_kv)

                for hkL in range(2):
                    hk = hk0 + hkL
                    qbuf = (hk % 2) * 4096
                    P.wait("gpsimd", "QS0", sq_qt_all[0])
                    P.wait("gpsimd", "QS1", sq_qt_all[1])
                    if hk >= 2:
                        P.wait("gpsimd", "TM", tm_slast.get(hk - 2, 0))
                    P.op("gpsimd", lambda E_, SS, hk=hk, qbuf=qbuf: E_.dma_start(
                        out=sb_qg[:, qbuf:qbuf + 4096].rearrange("h p m -> p h m" if False else "p (h m) -> p h m", h=4),
                        in_=d_qt[4 * hk:4 * hk + 4].rearrange("h p m -> p h m")), inc=(f"QG{hk % 2}", 16))
                    P.wait("tensor", f"QG{hk % 2}", P.val(f"QG{hk % 2}"))

                    for ql in range(4):
                        qh = 4 * hk + ql
                        units = []
                        for sk_t in range(64):
                            k = sk_t // 8
                            c0 = 128 * k
                            if c0 < 512:
                                units.append((sk_t, c0, 512 - c0, True))
                                units.append((sk_t, 512, 512, False))
                            else:
                                units.append((sk_t, c0, 1024 - c0, True))
                        nu = len(units)
                        pend = []
                        for iu, (sk_t, cs, nn, msk) in enumerate(units):
                            gi = g_i[0] + iu
                            stb = gi % 4
                            if gi >= 4:
                                P.wait("tensor", "SE", se_exp[gi - 4])
                            if iu == 0:
                                P.wait("tensor", "VE", ve_ot if qh > 0 else ve_kv)
                            tms = P.op("tensor", lambda E_, SS, stb=stb, hkL=hkL, sk_t=sk_t, ql=ql,
                                 qbuf=qbuf, cs=cs, nn=nn: E_.matmul(
                                     ps[stb][:, 0:nn],
                                     sb_kt[:, hkL * 8192 + sk_t * 128:hkL * 8192 + sk_t * 128 + 128],
                                     sb_qg[:, qbuf + ql * 1024 + cs:qbuf + ql * 1024 + cs + nn],
                                     start=True, stop=True, skip_group_check=True), inc=("TM", 1))
                            if iu == nu - 1:
                                tm_slast[hk] = tms
                            P.wait("scalar", "TM", tms)
                            if gi >= 4:
                                P.wait("scalar", "TM", tm_l[gi - 4])
                            P.op("scalar", lambda E_, SS, stb=stb, nn=nn: E_.activation(
                                sb_pt[:, stb * 512:stb * 512 + nn], ps[stb][:, 0:nn], AF.Exp),
                                inc=("SE", 1))
                            se_exp[gi] = P.val("SE")
                            se_last_exp = se_exp[gi]
                            vem = None
                            if msk:
                                P.wait("vector", "SE", se_exp[gi])
                                k = sk_t // 8
                                mi = (k % 2) * 8 + (sk_t % 8)
                                P.op("vector", lambda E_, SS, stb=stb, mi=mi: E_.tensor_mul(
                                    sb_pt[:, stb * 512:stb * 512 + 128],
                                    sb_pt[:, stb * 512:stb * 512 + 128],
                                    sb_msk[:, mi * 128:mi * 128 + 128]), inc=("VE", 1))
                                vem = P.val("VE")
                            pend.append((gi, sk_t, cs, nn, stb, msk, vem, hkL))
                            if len(pend) > 3:
                                emit_pvl(*pend.pop(0))
                        for p_ in pend:
                            emit_pvl(*p_)
                        g_i[0] += nu

                        P.wait("vector", "TM", tm_l[g_i[0] - 1])
                        P.op("vector", lambda E_, SS: E_.tensor_copy(sb_ra[0:1, 0:1024], pd[0:1, 0:1024]))
                        P.op("vector", lambda E_, SS: E_.reciprocal(sb_rb[0:1, 0:1024], sb_ra[0:1, 0:1024]))
                        P.op("vector", lambda E_, SS: E_.tensor_copy(sb_rbf[0:1, 0:1024], sb_rb[0:1, 0:1024]),
                             inc=("VE", 1))
                        P.wait("tensor", "VE", P.val("VE"))
                        tv = None
                        for h2 in range(2):
                            tv = P.op("tensor", lambda E_, SS, h2=h2: E_.matmul(
                                pd[:, h2 * 512:h2 * 512 + 512], sb_onr[0:1, 0:128],
                                sb_rbf[0:1, h2 * 512:h2 * 512 + 512], start=True, stop=True),
                                inc=("TM", 1) if h2 == 1 else None)
                        P.wait("vector", "TM", tv)
                        if qh >= 2:
                            P.wait("vector", *sq_ot[qh - 2])
                        ob = (qh % 2) * 1024
                        P.op("vector", lambda E_, SS: E_.tensor_copy(sb_invl[:, :], pd[:, 0:1024]))
                        P.op("vector", lambda E_, SS, ob=ob: E_.tensor_mul(
                            sb_otsb[:, ob:ob + 1024], pc[:, 0:1024], sb_invl[:, :]), inc=("VE", 1))
                        ve_ot = P.val("VE")
                        P.wait("sync", "VE", ve_ot)
                        P.op("sync", lambda E_, SS, qh=qh, ob=ob: E_.dma_start(
                            out=d_ot[qh], in_=sb_otsb[:, ob:ob + 1024]), inc=(f"OS{qh % 2}", 16))
                        sq_ot[qh] = (f"OS{qh % 2}", P.val(f"OS{qh % 2}"))
                last_pc_reader_ve = ve_ot
                last_pd_reader_ve = ve_ot

            tm_c_end = P.val("TM")
            for qh in range(32):
                P.wait("sync", "TM", tm_c_end)
                P.wait("sync", *sq_ot[qh])
                P.op("sync", lambda E_, SS, qh=qh: E_.dma_start(
                    out=sb_ota[:, qh * 1024:qh * 1024 + 1024], in_=d_ot[qh]), inc=("OP", 16))
            sq_ota_pre = P.val("OP")
            barrier()

            if "ot" in d_dbg:
                for qh in range(32):
                    P.op("sync", lambda E_, SS, qh=qh: E_.dma_start(out=d_dbg["ot"][qh], in_=d_ot[qh]),
                         inc=("SQ", 16))
                P.wait("sync", "SQ", P.val("SQ"))
                barrier()

            if max_phase < 4:
                _finish(P); raise _Done()
            P.wait("tensor", "OP", sq_ota_pre)

            tm_wo = [0] * 32
            ve_x1 = [0] * 32
            sq_x1 = [0] * 32
            se_x1b = [0] * 32
            tm_st = [0] * 32
            gq_wo = [0] * 32
            sq_xr = [0] * 32

            def wo_load(ot):
                if ot >= 2:
                    P.wait("gpsimd", "TM", tm_wo[ot - 2])
                buf = (ot % 2) * 4096
                P.op("gpsimd", lambda E_, SS, ot=ot, buf=buf: E_.dma_start(
                    out=sb_wo[:, buf:buf + 4096], in_=d_wot[ot]), inc=(f"WO{ot % 2}", 16))
                gq_wo[ot] = (f"WO{ot % 2}", P.val(f"WO{ot % 2}"))

            def xr_load(ot):
                if ot >= 2:
                    P.wait("sync", "VE", ve_x1[ot - 2])
                buf = (ot % 2) * 1024
                P.op("sync", lambda E_, SS, ot=ot, buf=buf: E_.dma_start(
                    out=sb_xr[:, buf:buf + 1024], in_=d_xo[128 * ot:128 * ot + 128, :]), inc=(f"XR{ot % 2}", 16))
                sq_xr[ot] = (f"XR{ot % 2}", P.val(f"XR{ot % 2}"))

            wo_load(0)
            xr_load(0)
            pend_stats = []

            def emit_stats_d(ot):
                sq_slot = sb_xsq if ot % 2 == 0 else sb_tmp
                P.wait("tensor", "SE", se_x1b[ot])
                for h in range(2):
                    P.op("tensor", lambda E_, SS, ot=ot, h=h: E_.matmul(
                        ps[h][0:1, 0:512], sb_one[:, 0:1],
                        sb_x1b[:, ot * 1024 + h * 512:ot * 1024 + h * 512 + 512],
                        start=(ot == 0), stop=(ot == 31)))
                    tm_st[ot] = P.op("tensor", lambda E_, SS, ot=ot, h=h, sq_slot=sq_slot: E_.matmul(
                        ps[2 + h][0:1, 0:512], sb_one[:, 0:1], sq_slot[:, h * 512:h * 512 + 512],
                        start=(ot == 0), stop=(ot == 31)),
                        inc=("TM", 1) if h == 1 else None)

            for ot in range(32):
                if ot + 1 < 32:
                    wo_load(ot + 1)
                    xr_load(ot + 1)
                buf = (ot % 2) * 4096
                py = pc if ot % 2 == 0 else pd
                P.wait("tensor", *gq_wo[ot])
                if ot >= 2:
                    P.wait("tensor", "VE", ve_x1[ot - 2])
                for t in range(32):
                    for h in range(2):
                        tm_wo[ot] = P.op("tensor", lambda E_, SS, t=t, h=h, buf=buf, py=py: E_.matmul(
                            py[:, h * 512:h * 512 + 512],
                            sb_wo[:, buf + t * 128:buf + t * 128 + 128],
                            sb_ota[:, t * 1024 + h * 512:t * 1024 + h * 512 + 512],
                            start=(t == 0), stop=(t == 31)),
                            inc=("TM", 1) if (t == 31 and h == 1) else None)
                P.wait("vector", "TM", tm_wo[ot])
                P.wait("vector", *sq_xr[ot])
                if ot >= 2:
                    P.wait("vector", *sq_x1[ot - 2])
                xb = (ot % 2) * 1024
                P.op("vector", lambda E_, SS, xb=xb, py=py: E_.tensor_add(
                    sb_x1s[:, xb:xb + 1024], py[:, 0:1024], sb_xr[:, xb:xb + 1024]), inc=("VE", 1))
                ve_x1[ot] = P.val("VE")
                P.wait("scalar", "VE", ve_x1[ot])
                if ot >= 2:
                    P.wait("scalar", "TM", tm_st[ot - 2])
                sq_slot = sb_xsq if ot % 2 == 0 else sb_tmp
                P.op("scalar", lambda E_, SS, ot=ot, xb=xb: E_.copy(
                    sb_x1b[:, ot * 1024:ot * 1024 + 1024], sb_x1s[:, xb:xb + 1024]))
                P.op("scalar", lambda E_, SS, ot=ot, sq_slot=sq_slot: E_.activation(
                    sq_slot[:, 0:1024], sb_x1b[:, ot * 1024:ot * 1024 + 1024], AF.Square),
                    inc=("SE", 1))
                se_x1b[ot] = P.val("SE")
                if ot >= 1:
                    emit_stats_d(ot - 1)
                P.wait("sync", "VE", ve_x1[ot])
                P.op("sync", lambda E_, SS, ot=ot, xb=xb: E_.dma_start(
                    out=d_x1[ot], in_=sb_x1s[:, xb:xb + 1024]), inc=(f"XS{ot % 2}", 16))
                sq_x1[ot] = (f"XS{ot % 2}", P.val(f"XS{ot % 2}"))
            emit_stats_d(31)
            sq_x1_all = (P.val("XS0"), P.val("XS1"))

            bve2 = ln_rows(1024, 0,
                           [(ps[0], 0, 512), (ps[1], 0, 512)],
                           [(ps[2], 0, 512), (ps[3], 0, 512)])
            barrier()

            if "x1" in d_dbg:
                for ot in range(32):
                    P.op("sync", lambda E_, SS, ot=ot: E_.dma_start(out=d_dbg["x1"][ot], in_=d_x1[ot]),
                         inc=("SQ", 16))
                P.wait("sync", "SQ", P.val("SQ"))
                barrier()

            if max_phase < 5:
                _finish(P); raise _Done()
            tiles = []
            for t in range(32):
                for h in range(2):
                    tiles.append((sb_x1b[:, t * 1024 + h * 512:t * 1024 + h * 512 + 512], h * 512))
            lnv2 = emit_ln_apply(tiles, bve2)
            P.wait("tensor", "VE", lnv2)
            if "ab2" in d_dbg:
                P.wait("sync", "VE", lnv2)
                P.op("sync", lambda E_, SS: E_.dma_start(out=d_dbg["ab2"][:, 0:1024], in_=sb_abc[:, :]), inc=("SQ", 16))
                P.op("sync", lambda E_, SS: E_.dma_start(out=d_dbg["ab2"][:, 1024:2048], in_=sb_bbc[:, :]), inc=("SQ", 16))
                P.op("sync", lambda E_, SS: E_.dma_start(out=d_dbg["x1n"], in_=sb_x1b[:, :]), inc=("SQ", 16))
                P.wait("sync", "SQ", P.val("SQ"))
                barrier()

            tm_ffn = [0] * 32
            se_wb = [0] * 32
            ve_os = [0] * 32
            sq_os = [0] * 32
            sq_wf = [0] * 32
            sq_x1r = [0] * 32

            def wf_load(ot):
                if ot < 2:
                    P.wait("sync", "XS0", sq_x1_all[0])
                    P.wait("sync", "XS1", sq_x1_all[1])
                if ot >= 2:
                    P.wait("sync", "SE", se_wb[ot - 2])
                buf = (ot % 2) * 4096
                P.op("sync", lambda E_, SS, ot=ot, buf=buf: E_.dma_start(
                    out=sb_wf[:, buf:buf + 4096], in_=d_wft[ot]), inc=(f"WF{ot % 2}", 16))
                sq_wf[ot] = (f"WF{ot % 2}", P.val(f"WF{ot % 2}"))

            def x1r_load(ot):
                P.wait("sync", "XS0", sq_x1_all[0])
                P.wait("sync", "XS1", sq_x1_all[1])
                if ot >= 2:
                    P.wait("sync", "VE", ve_os[ot - 2])
                buf = (ot % 2) * 1024
                P.op("sync", lambda E_, SS, ot=ot, buf=buf: E_.dma_start(
                    out=sb_x1r[:, buf:buf + 1024], in_=d_x1[ot]), inc=(f"XB{ot % 2}", 16))
                sq_x1r[ot] = (f"XB{ot % 2}", P.val(f"XB{ot % 2}"))

            P.wait("scalar", "SQ", 16)
            wf_load(0)
            x1r_load(0)
            for ot in range(32):
                if ot + 1 < 32:
                    wf_load(ot + 1)
                    x1r_load(ot + 1)
                buf = (ot % 2) * 4096
                py = pc if ot % 2 == 0 else pd
                P.wait("scalar", *sq_wf[ot])
                if ot >= 2:
                    P.wait("scalar", "TM", tm_ffn[ot - 2])
                for it in range(32):
                    P.op("scalar", lambda E_, SS, ot=ot, it=it, buf=buf: E_.mul(
                        sb_wb[:, buf + it * 128:buf + it * 128 + 128],
                        sb_wf[:, buf + it * 128:buf + it * 128 + 128],
                        sb_scl[:, ot * 32 + it:ot * 32 + it + 1]))
                P.op("scalar", lambda E_, SS: E_.copy(sb_dum[:, 8:12], sb_dum[:, 12:16]), inc=("SE", 1))
                se_wb[ot] = P.val("SE")
                P.wait("tensor", "SE", se_wb[ot])
                if ot >= 2:
                    P.wait("tensor", "VE", ve_os[ot - 2])
                for it in range(32):
                    for h in range(2):
                        tm_ffn[ot] = P.op("tensor", lambda E_, SS, it=it, h=h, buf=buf, py=py: E_.matmul(
                            py[:, h * 512:h * 512 + 512],
                            sb_wb[:, buf + it * 128:buf + it * 128 + 128],
                            sb_x1b[:, it * 1024 + h * 512:it * 1024 + h * 512 + 512],
                            start=(it == 0), stop=(it == 31)),
                            inc=("TM", 1) if (it == 31 and h == 1) else None)
                P.wait("vector", "TM", tm_ffn[ot])
                P.wait("vector", *sq_x1r[ot])
                if ot >= 2:
                    P.wait("vector", *sq_os[ot - 2])
                xb = (ot % 2) * 1024
                P.op("vector", lambda E_, SS, xb=xb, py=py: E_.tensor_add(
                    sb_os[:, xb:xb + 1024], py[:, 0:1024], sb_x1r[:, xb:xb + 1024]), inc=("VE", 1))
                ve_os[ot] = P.val("VE")
                P.wait("sync", "VE", ve_os[ot])
                P.op("sync", lambda E_, SS, ot=ot, xb=xb: E_.dma_start(
                    out=d_out[ot], in_=sb_os[:, xb:xb + 1024]), inc=(f"OU{ot % 2}", 16))
                sq_os[ot] = (f"OU{ot % 2}", P.val(f"OU{ot % 2}"))

            P.wait("sync", "OU0", P.val("OU0"))
            P.wait("sync", "OU1", P.val("OU1"))
            P.wait("gpsimd", "OU0", P.val("OU0"))
            P.wait("gpsimd", "OU1", P.val("OU1"))


        except _Done:
            pass
        _dispatch(P, blk, sems)
    return nc


def _finish(P):
    pass


def _dispatch(P, blk, sems):
        @blk.tensor
        def _(e):
            for f in P.q["tensor"]:
                f(e, sems)

        @blk.vector
        def _(e):
            for f in P.q["vector"]:
                f(e, sems)

        @blk.scalar
        def _(e):
            for f in P.q["scalar"]:
                f(e, sems)

        @blk.gpsimd
        def _(e):
            for f in P.q["gpsimd"]:
                f(e, sems)

        @blk.sync
        def _(e):
            for f in P.q["sync"]:
                f(e, sems)


def _stage(inputs):
    x = np.asarray(inputs["L0_attn_hidden_states"], dtype=np.float32)[0]
    Wq = np.asarray(inputs["L0_attn_Wq"], dtype=np.float32)
    Wkd = np.asarray(inputs["L0_attn_Wk_down"], dtype=np.float32)
    Wvd = np.asarray(inputs["L0_attn_Wv_down"], dtype=np.float32)
    Wku = np.asarray(inputs["L0_attn_Wk_up"], dtype=np.float32)
    Wvu = np.asarray(inputs["L0_attn_Wv_up"], dtype=np.float32)
    Wo = np.asarray(inputs["L0_attn_Wo"], dtype=np.float32)
    Wf = np.asarray(inputs["L0_ffn_weight"], dtype=np.float32)
    scl = np.asarray(inputs["L0_ffn_weight_scale"], dtype=np.float32)

    wd = np.concatenate(
        [np.concatenate([Wkd[32 * h:32 * h + 32], Wvd[32 * h:32 * h + 32]], axis=0)
         for h in range(8)], axis=0)
    wdt = np.ascontiguousarray(
        wd.T.reshape(32, 128, 512).transpose(1, 0, 2).reshape(128, 32 * 512))

    xt_sw = np.ascontiguousarray(
        x.T.reshape(32, 128, 32, 256).transpose(2, 1, 0, 3).reshape(32, 128, 32 * 256))

    def tiled_T(w):
        return np.ascontiguousarray(
            w.T.reshape(32, 128, 32, 128).transpose(2, 1, 0, 3).reshape(32, 128, 32 * 128))

    wqt = tiled_T(Wq)
    wot = tiled_T(Wo)
    wft = tiled_T(Wf)

    def dup2(wu):
        wt = wu.T
        out = np.empty((128, 4, 1024), np.float32)
        for r in range(4):
            for b in range(2):
                out[64 * b:64 * b + 64, r] = wt[64 * r:64 * r + 64]
        return np.ascontiguousarray(out)

    wk2 = dup2(Wku)
    wv2 = dup2(Wvu)
    scale_pp = np.ascontiguousarray(
        np.broadcast_to(scl.reshape(1, 1024), (128, 1024)).astype(np.float32))

    shared = dict(xt_sw=xt_sw, wdt_sw=wdt, wqt_sw=wqt, wot_sw=wot, wft_sw=wft,
                  wk2=wk2, wv2=wv2, scale_pp=scale_pp)

    tri = np.tril(np.ones((128, 128), np.float32)).T
    in_maps = []
    for c in range(NCORE):
        blocks = zig_blocks(c)
        rows = np.concatenate([np.arange(b * 128, b * 128 + 128) for b in blocks])
        xt_own = np.ascontiguousarray(x[rows].T)
        mask = np.empty((16, 128, 128), np.float32)
        for kpar in range(2):
            bpos = c if kpar == 0 else 7 - c
            for m in range(8):
                if m < bpos:
                    mask[kpar * 8 + m] = 1.0
                elif m == bpos:
                    mask[kpar * 8 + m] = tri
                else:
                    mask[kpar * 8 + m] = 0.0
        im = dict(shared)
        im["xt_own"] = xt_own
        im["xo_sw"] = np.ascontiguousarray(
            xt_own.reshape(32, 128, 1024).transpose(1, 0, 2).reshape(128, 32 * 1024))
        im["mask01"] = np.ascontiguousarray(mask)
        in_maps.append(im)
    return in_maps


def kernel(**inputs):
    in_maps = _stage(inputs)
    nc = build_program(tuple(DEBUG_TAPS), MAX_PHASE, NCHUNKS, A_LEVEL, LR_LEVEL, NCORES_RUN)
    res = run_bass_kernel_spmd(nc, in_maps[:NCORES_RUN], core_ids=list(range(NCORES_RUN)))
    out = np.zeros((1, S, HID), np.float32)
    for c in range(NCORES_RUN):
        x2t = np.asarray(res.results[c]["o"]).reshape(HID, 1024)
        for j, b in enumerate(zig_blocks(c)):
            out[0, b * 128:(b + 1) * 128, :] = x2t[:, j * 128:(j + 1) * 128].T
    if DEBUG_TAPS:
        kernel._dbg = [{k: np.asarray(v) for k, v in r.items() if k.startswith("dbg_")}
                       for r in res.results]
    return out


# version 16 (final)
# speedup vs baseline: 1.2004x; 1.0792x over previous
import math
import numpy as np
from contextlib import ExitStack

import concourse.bass as bass
import concourse.mybir as mybir
from concourse.bass_utils import run_bass_kernel_spmd

F32 = mybir.dt.float32
BF16 = mybir.dt.bfloat16
AF = mybir.ActivationFunctionType

NCORE = 8
S = 8192
HID = 4096
QSCALE = 1.0 / math.sqrt(128.0)
EPS = 1e-5

DEBUG_TAPS = []
MAX_PHASE = 5
NCORES_RUN = 8
NCHUNKS = 16
A_LEVEL = 4
LR_LEVEL = 4


def zig_blocks(c):
    return sorted([c, 15 - c, 16 + c, 31 - c, 32 + c, 47 - c, 48 + c, 63 - c])


BIG0 = 0
BIG1 = 65536
MID = 131072
SM_PT = 163840
SM_OTSB = 167936
SM_INVL = 172032
SM_ABC = 176128
SM_BBC = 178176
SM_TMP = 180224
SM_SCL = 182272
SM_MSK = 186368
SM_XSQ = 190464
SM_RA = 192512
SM_RB = 196608
SM_RC = 200704
SM_RBF = 204800
SM_ONE = 208896
SM_ONR = 208928
SM_DUM = 209216

ENGS = ("tensor", "vector", "scalar", "gpsimd", "sync")


class _Done(Exception):
    pass


class Prog:
    def __init__(self):
        self.q = {e: [] for e in ENGS}
        self.cnt = {}
        self.nbar = 0

    def bump(self, sem, n):
        self.cnt[sem] = self.cnt.get(sem, 0) + n
        return self.cnt[sem]

    def val(self, sem):
        return self.cnt.get(sem, 0)

    def wait(self, eng, sem, v):
        if v <= 0:
            return
        assert v <= self.val(sem), f"{eng} waits {sem}>={v} but only {self.val(sem)} emitted"
        self.q[eng].append(lambda E, SS, sem=sem, v=v: E.wait_ge(SS[sem], v))

    def op(self, eng, builder, inc=None):
        if inc is not None:
            sem, n = inc
            v = self.bump(sem, n)
            self.q[eng].append(
                lambda E, SS, b=builder, sem=sem, n=n: b(E, SS).then_inc(SS[sem], n))
            return v
        self.q[eng].append(lambda E, SS, b=builder: b(E, SS))
        return None


def build_program(debug_taps=(), max_phase=5, nchunks=16, a_level=4, lr_level=4, ncores=8):
    nc = bass.Bass("TRN2", target_bir_lowering=False)

    d_xt = nc.dram_tensor("xt_sw", [16, 128, 32 * 512], F32, kind="ExternalInput").ap()
    d_xo = nc.dram_tensor("xt_own", [HID, 1024], F32, kind="ExternalInput").ap()
    d_xo2 = nc.dram_tensor("xo_sw", [128, 32 * 1024], F32, kind="ExternalInput").ap()
    d_wdt = nc.dram_tensor("wdt_sw", [128, 32 * 512], F32, kind="ExternalInput").ap()
    d_wqt = nc.dram_tensor("wqt_sw", [32, 128, 32 * 128], F32, kind="ExternalInput").ap()
    d_wot = nc.dram_tensor("wot_sw", [32, 128, 32 * 128], F32, kind="ExternalInput").ap()
    d_wft = nc.dram_tensor("wft_sw", [32, 128, 32 * 128], F32, kind="ExternalInput").ap()
    d_wk2 = nc.dram_tensor("wk2", [128, 4, 1024], F32, kind="ExternalInput").ap()
    d_wv2 = nc.dram_tensor("wv2", [128, 4, 1024], F32, kind="ExternalInput").ap()
    d_scl = nc.dram_tensor("scale_pp", [128, 1024], F32, kind="ExternalInput").ap()
    d_msk = nc.dram_tensor("mask01", [16, 128, 128], F32, kind="ExternalInput").ap()

    d_qt = nc.dram_tensor("qt_spill", [32, 128, 1024], BF16, kind="Internal").ap()
    d_ot = nc.dram_tensor("ot_spill", [32, 128, 1024], BF16, kind="Internal").ap()
    d_x1 = nc.dram_tensor("x1_spill", [32, 128, 1024], F32, kind="Internal").ap()
    d_out = nc.dram_tensor("o", [32, 128, 1024], F32, kind="ExternalOutput").ap()

    d_dbg = {}
    if "cfull" in debug_taps:
        d_dbg["cfull"] = nc.dram_tensor("dbg_cfull", [128, 4 * S], BF16, kind="ExternalOutput").ap()
    if "ab" in debug_taps:
        d_dbg["ab"] = nc.dram_tensor("dbg_ab", [128, 2048], BF16, kind="ExternalOutput").ap()
        d_dbg["rows"] = nc.dram_tensor("dbg_rows", [1, 3072], F32, kind="ExternalOutput").ap()
    if "ab2" in debug_taps:
        d_dbg["ab2"] = nc.dram_tensor("dbg_ab2", [128, 2048], BF16, kind="ExternalOutput").ap()
        d_dbg["x1n"] = nc.dram_tensor("dbg_x1n", [128, 32 * 1024], BF16, kind="ExternalOutput").ap()
    if "qt" in debug_taps:
        d_dbg["qt"] = nc.dram_tensor("dbg_qt", [32, 128, 1024], BF16, kind="ExternalOutput").ap()
    if "ot" in debug_taps:
        d_dbg["ot"] = nc.dram_tensor("dbg_ot", [32, 128, 1024], BF16, kind="ExternalOutput").ap()
    if "x1" in debug_taps:
        d_dbg["x1"] = nc.dram_tensor("dbg_x1", [32, 128, 1024], F32, kind="ExternalOutput").ap()

    P = Prog()

    with ExitStack() as ctx:
        E = ctx.enter_context
        _SBUF_BASE = 16512

        def A(name, shape, dtype, offset):
            return nc.alloc_sbuf_tensor_at(name, shape, dtype, offset=offset + _SBUF_BASE)

        sb_pt = A("pt", [128, 4 * 512], BF16, offset=SM_PT)
        sb_otsb = A("otsb", [128, 2 * 1024], BF16, offset=SM_OTSB)
        sb_invl = A("invl", [128, 1024], F32, offset=SM_INVL)
        sb_abc = A("abc", [128, 1024], BF16, offset=SM_ABC)
        sb_bbc = A("bbc", [128, 1024], BF16, offset=SM_BBC)
        sb_tmp = A("tmpb", [128, 1024], BF16, offset=SM_TMP)
        sb_scl = A("sclpp", [128, 1024], F32, offset=SM_SCL)
        sb_msk = A("maskb", [128, 16 * 128], BF16, offset=SM_MSK)
        sb_xsq = A("xsq", [128, 1024], BF16, offset=SM_XSQ)
        sb_ra = A("rowa", [1, 1024], F32, offset=SM_RA)
        sb_rb = A("rowb", [1, 1024], F32, offset=SM_RB)
        sb_rc = A("rowc", [1, 1024], F32, offset=SM_RC)
        sb_rbf = A("rowbf", [1, 2048], BF16, offset=SM_RBF)
        sb_one = A("onec", [128, 8], BF16, offset=SM_ONE)
        sb_onr = A("oner", [1, 128], BF16, offset=SM_ONR)
        sb_dum = A("dum", [128, 16], F32, offset=SM_DUM)

        sb_xt = A("xt", [128, 2 * 32 * 512], BF16, offset=BIG0)
        sb_xo = A("xo", [128, 32 * 1024], BF16, offset=BIG0)
        sb_cf = A("cfull", [128, 4 * S], BF16, offset=BIG1)
        sb_wdt = A("wdtb", [128, 32 * 512], BF16, offset=MID)
        sb_wq = A("wqb", [128, 2 * 32 * 128], BF16, offset=MID)
        sb_qt = A("qtb", [128, 2 * 1024], BF16, offset=MID + 16384)
        sb_kcm = A("kcmp", [128, 16384], BF16, offset=BIG0)
        sb_vcm = A("vcmp", [128, 16384], BF16, offset=BIG0 + 32768)
        sb_kt = A("ktp", [128, 2 * 8192], BF16, offset=BIG1)
        sb_vp = A("vp", [128, 64 * 256], BF16, offset=BIG1 + 32768)
        sb_qg = A("qg", [128, 2 * 4 * 1024], BF16, offset=MID)
        sb_wk2 = A("wk2b", [128, 4 * 1024], BF16, offset=MID + 16384)
        sb_wv2 = A("wv2b", [128, 4 * 1024], BF16, offset=MID + 24576)
        sb_ota = A("otall", [128, 32 * 1024], BF16, offset=BIG0)
        sb_x1b = A("x1b", [128, 32 * 1024], BF16, offset=BIG1)
        sb_wo = A("wob", [128, 2 * 32 * 128], BF16, offset=MID)
        sb_xr = A("xr", [128, 2 * 1024], F32, offset=MID + 16384)
        sb_x1s = A("x1s", [128, 2 * 1024], F32, offset=MID + 24576)
        sb_wf = A("wfb", [128, 2 * 32 * 128], F32, offset=MID)
        sb_wb = A("wbb", [128, 2 * 32 * 128], BF16, offset=BIG0)
        sb_x1r = A("x1r", [128, 2 * 1024], F32, offset=BIG0 + 16384)
        sb_os = A("osb", [128, 2 * 1024], F32, offset=BIG0 + 24576)

        ps = [E(nc.psum_tensor(f"ps{i}", [128, 512], F32)) for i in range(4)]
        pc = E(nc.psum_tensor("pc", [128, 1024], F32))
        pd = E(nc.psum_tensor("pd", [128, 1024], F32))

        sems = {}
        for s_ in ("GQ", "SQ", "TM", "VE", "SE", "PB",
                   "CH0", "CH1", "WQ0", "WQ1", "QG0", "QG1", "WO0", "WO1",
                   "QS0", "QS1", "OS0", "OS1", "OP", "XR0", "XR1",
                   "XS0", "XS1", "WF0", "WF1", "XB0", "XB1", "OU0", "OU1", "GS"):
            sems[s_] = E(nc.semaphore(s_))

        blk = E(nc.Block())

        def barrier():
            P.nbar += 1
            tgt = 5 * P.nbar
            P.bump("PB", 5)
            P.q["tensor"].append(lambda E_, SS: E_.matmul(
                ps[0][0:8, 0:4], sb_onr[0:1, 0:8], sb_onr[0:1, 4:8],
                start=True, stop=True, skip_group_check=True).then_inc(SS["PB"], 1))
            P.q["sync"].append(lambda E_, SS: E_.nop().then_inc(SS["PB"], 1))
            P.q["gpsimd"].append(lambda E_, SS: E_.nop().then_inc(SS["PB"], 1))
            P.q["vector"].append(
                lambda E_, SS: E_.tensor_copy(sb_dum[:, 0:4], sb_dum[:, 4:8]).then_inc(SS["PB"], 1))
            P.q["scalar"].append(
                lambda E_, SS: E_.copy(sb_dum[:, 8:12], sb_dum[:, 12:16]).then_inc(SS["PB"], 1))
            for e in ENGS:
                P.wait(e, "PB", tgt)

        def emit_stats(src_aps, sumA, sumB, n_si):
            tm_sq = {}
            occ = {}
            lastocc = {}
            for i, (_, co) in enumerate(src_aps):
                lastocc[co] = i
            for i, (src, co) in enumerate(src_aps):
                slot = (i % 2) * 512
                if i >= 2:
                    P.wait("scalar", "TM", tm_sq[i - 2])
                P.op("scalar", lambda E_, SS, src=src, slot=slot: E_.activation(
                    sb_xsq[:, slot:slot + 512], src, AF.Square), inc=("SE", 1))
                sev = P.val("SE")
                P.wait("tensor", "SE", sev)
                pa, ca = sumA[co]
                pb, cb = sumB[co]
                stA = co not in occ
                occ[co] = True
                last = lastocc[co] == i
                P.op("tensor", lambda E_, SS, src=src, pa=pa, ca=ca, stA=stA, last=last: E_.matmul(
                    pa[0:1, ca:ca + 512], sb_one[:, 0:1], src, start=stA, stop=last))
                v = P.op("tensor", lambda E_, SS, pb=pb, cb=cb, slot=slot, stA=stA, last=last: E_.matmul(
                    pb[0:1, cb:cb + 512], sb_one[:, 0:1], sb_xsq[:, slot:slot + 512],
                    start=stA, stop=last), inc=("TM", 1))
                tm_sq[i] = v

        def ln_rows(n_si, out_co, rowA_srcs, rowB_srcs):
            P.wait("vector", "TM", P.val("TM"))
            off = 0
            for p_, c_, n_ in rowA_srcs:
                P.op("vector", lambda E_, SS, p_=p_, c_=c_, n_=n_, off=off: E_.tensor_copy(
                    sb_ra[0:1, off:off + n_], p_[0:1, c_:c_ + n_]))
                off += n_
            off = 0
            for p_, c_, n_ in rowB_srcs:
                P.op("vector", lambda E_, SS, p_=p_, c_=c_, n_=n_, off=off: E_.tensor_copy(
                    sb_rb[0:1, off:off + n_], p_[0:1, c_:c_ + n_]))
                off += n_
            n = n_si
            P.op("vector", lambda E_, SS, n=n: E_.tensor_scalar_mul(sb_ra[0:1, 0:n], sb_ra[0:1, 0:n], 1.0 / HID))
            P.op("vector", lambda E_, SS, n=n: E_.tensor_scalar_mul(sb_rb[0:1, 0:n], sb_rb[0:1, 0:n], 1.0 / HID))
            P.op("vector", lambda E_, SS, n=n: E_.tensor_mul(sb_rc[0:1, 0:n], sb_ra[0:1, 0:n], sb_ra[0:1, 0:n]))
            P.op("vector", lambda E_, SS, n=n: E_.tensor_sub(sb_rb[0:1, 0:n], sb_rb[0:1, 0:n], sb_rc[0:1, 0:n]))
            P.op("vector", lambda E_, SS, n=n: E_.tensor_scalar_add(sb_rb[0:1, 0:n], sb_rb[0:1, 0:n], EPS),
                 inc=("VE", 1))
            P.wait("scalar", "VE", P.val("VE"))
            P.op("scalar", lambda E_, SS, n=n: E_.activation(sb_rc[0:1, 0:n], sb_rb[0:1, 0:n], AF.Ln))
            P.op("scalar", lambda E_, SS, n=n: E_.activation(sb_rc[0:1, 0:n], sb_rc[0:1, 0:n], AF.Exp, scale=-0.5),
                 inc=("SE", 1))
            P.wait("vector", "SE", P.val("SE"))
            P.op("vector", lambda E_, SS, n=n: E_.tensor_copy(sb_rbf[0:1, 0:n], sb_rc[0:1, 0:n]))
            P.op("vector", lambda E_, SS, n=n: E_.tensor_mul(sb_ra[0:1, 0:n], sb_ra[0:1, 0:n], sb_rc[0:1, 0:n]))
            P.op("vector", lambda E_, SS, n=n: E_.tensor_scalar_mul(sb_ra[0:1, 0:n], sb_ra[0:1, 0:n], -1.0))
            P.op("vector", lambda E_, SS, n=n: E_.tensor_copy(sb_rbf[0:1, 1024:1024 + n], sb_ra[0:1, 0:n]),
                 inc=("VE", 1))
            P.wait("tensor", "VE", P.val("VE"))
            for h in range(0, n, 512):
                nn = min(512, n - h)
                P.op("tensor", lambda E_, SS, h=h, nn=nn: E_.matmul(
                    pc[:, h:h + nn], sb_onr[0:1, 0:128], sb_rbf[0:1, h:h + nn], start=True, stop=True))
            hs = list(range(0, n, 512))
            for h in hs[:-1]:
                P.op("tensor", lambda E_, SS, h=h: E_.matmul(
                    pd[:, h:h + 512], sb_onr[0:1, 0:128], sb_rbf[0:1, 1024 + h:1024 + h + 512],
                    start=True, stop=True))
            h = hs[-1]
            nn = min(512, n - h)
            tv = P.op("tensor", lambda E_, SS, h=h, nn=nn: E_.matmul(
                pd[:, h:h + nn], sb_onr[0:1, 0:128], sb_rbf[0:1, 1024 + h:1024 + h + nn],
                start=True, stop=True), inc=("TM", 1))
            P.wait("vector", "TM", tv)
            P.op("vector", lambda E_, SS, n=n, co=out_co: E_.tensor_copy(sb_abc[:, co:co + n], pc[:, 0:n]))
            P.op("vector", lambda E_, SS, n=n, co=out_co: E_.tensor_copy(sb_bbc[:, co:co + n], pd[:, 0:n]),
                 inc=("VE", 1))
            return P.val("VE")

        def emit_ln_apply(tiles, bcast_ve):
            P.wait("vector", "VE", bcast_ve)
            for x_ap, co in tiles:
                P.op("vector", lambda E_, SS, x_ap=x_ap, co=co: E_.tensor_mul(
                    sb_tmp[:, 0:512], x_ap, sb_abc[:, co:co + 512]))
                P.op("vector", lambda E_, SS, x_ap=x_ap, co=co: E_.tensor_add(
                    x_ap, sb_tmp[:, 0:512], sb_bbc[:, co:co + 512]))
            return P.op("vector", lambda E_, SS: E_.tensor_copy(sb_dum[:, 0:4], sb_dum[:, 4:8]),
                        inc=("VE", 1))

        try:
            P.op("vector", lambda E_, SS: E_.memset(sb_one[:, :], 1.0))
            P.op("vector", lambda E_, SS: E_.memset(sb_onr[:, :], 1.0))
            P.op("vector", lambda E_, SS: E_.memset(sb_dum[:, :], 0.0), inc=("VE", 1))
            P.op("sync", lambda E_, SS: E_.dma_start(out=sb_scl[:, :], in_=d_scl), inc=("SQ", 16))
            P.op("gpsimd", lambda E_, SS: E_.dma_start(
                out=sb_msk[:, :].rearrange("p (k m) -> p k m", k=16),
                in_=d_msk.rearrange("k p m -> p k m")), inc=("GS", 16))
            gs_msk = P.val("GS")

            P.op("gpsimd", lambda E_, SS: E_.dma_start(out=sb_wdt[:, :], in_=d_wdt), inc=("GS", 16))
            gs_wdt = P.val("GS")
            P.wait("tensor", "GS", gs_wdt)

            NCH = nchunks
            tm_down = [0] * NCH
            gq_ch = [0] * NCH
            ve_bcast_prev = 0

            def chunk_load(j):
                if j >= 2:
                    P.wait("gpsimd", "TM", tm_down[j - 2])
                buf = (j % 2) * 32 * 512
                P.op("gpsimd", lambda E_, SS, j=j, buf=buf: E_.dma_start(
                    out=sb_xt[:, buf:buf + 32 * 512], in_=d_xt[j]), inc=(f"CH{j % 2}", 16))
                gq_ch[j] = (f"CH{j % 2}", P.val(f"CH{j % 2}"))

            chunk_load(0)
            for j in range(NCH):
                if j + 1 < NCH:
                    chunk_load(j + 1)
                buf = (j % 2) * 32 * 512
                co = (j % 2) * 512
                P.wait("scalar", *gq_ch[j])
                P.wait("tensor", *gq_ch[j])
                P.wait("tensor", "VE", ve_bcast_prev)
                if a_level >= 1:
                    srcs = [(sb_xt[:, buf + t * 512:buf + t * 512 + 512], 0) for t in range(32)]
                    emit_stats(srcs, {0: (pc, 0)}, {0: (pd, 0)}, 512)
                if a_level >= 2:
                    ve_bcast_prev = ln_rows(512, co, [(pc, 0, 512)], [(pd, 0, 512)])
                lnv_a = emit_ln_apply([(sb_xt[:, buf + t * 512:buf + t * 512 + 512], co) for t in range(16)],
                                      ve_bcast_prev)
                lnv_b = emit_ln_apply([(sb_xt[:, buf + t * 512:buf + t * 512 + 512], co) for t in range(16, 32)],
                                      lnv_a)
                P.wait("tensor", "VE", lnv_a)
                for ct in range(4):
                    v = None
                    for t in range(32):
                        if ct == 0 and t == 16:
                            P.wait("tensor", "VE", lnv_b)
                        v = P.op("tensor", lambda E_, SS, ct=ct, t=t, buf=buf: E_.matmul(
                            ps[ct][:, 0:512],
                            sb_wdt[:, t * 512 + ct * 128:t * 512 + ct * 128 + 128],
                            sb_xt[:, buf + t * 512:buf + t * 512 + 512],
                            start=(t == 0), stop=(t == 31)),
                            inc=("TM", 1) if t == 31 else None)
                    P.wait("vector", "TM", v)
                    P.op("vector", lambda E_, SS, ct=ct, j=j: E_.tensor_copy(
                        sb_cf[:, ct * S + 512 * j:ct * S + 512 * j + 512], ps[ct][:, 0:512]),
                        inc=("VE", 1))
                tm_down[j] = P.val("TM")
            barrier()

            if "cfull" in d_dbg:
                P.op("sync", lambda E_, SS: E_.dma_start(out=d_dbg["cfull"], in_=sb_cf[:, :]), inc=("SQ", 16))
                P.wait("sync", "SQ", P.val("SQ"))
                barrier()
            if "ab" in d_dbg:
                P.op("sync", lambda E_, SS: E_.dma_start(out=d_dbg["ab"][:, 0:1024], in_=sb_abc[:, :]), inc=("SQ", 16))
                P.op("sync", lambda E_, SS: E_.dma_start(out=d_dbg["ab"][:, 1024:2048], in_=sb_bbc[:, :]), inc=("SQ", 16))
                P.op("sync", lambda E_, SS: E_.dma_start(out=d_dbg["rows"][0:1, 0:1024], in_=sb_ra[0:1, :]), inc=("SQ", 16))
                P.op("sync", lambda E_, SS: E_.dma_start(out=d_dbg["rows"][0:1, 1024:2048], in_=sb_rb[0:1, :]), inc=("SQ", 16))
                P.op("sync", lambda E_, SS: E_.dma_start(out=d_dbg["rows"][0:1, 2048:3072], in_=sb_rc[0:1, :]), inc=("SQ", 16))
                P.wait("sync", "SQ", P.val("SQ"))
                barrier()

            if max_phase < 2:
                _finish(P); raise _Done()
            P.op("gpsimd", lambda E_, SS: E_.dma_start(out=sb_xo[:, :], in_=d_xo2), inc=("GS", 16))
            gq_xo = P.val("GS")
            P.wait("scalar", "GS", gq_xo)
            P.wait("tensor", "GS", gq_xo)
            P.wait("tensor", "VE", ve_bcast_prev)
            srcs = []
            for t in range(32):
                for h in range(2):
                    srcs.append((sb_xo[:, t * 1024 + h * 512:t * 1024 + h * 512 + 512], h * 512))
            emit_stats(srcs, {0: (pc, 0), 512: (pc, 512)}, {0: (pd, 0), 512: (pd, 512)}, 1024)
            bve = ln_rows(1024, 0, [(pc, 0, 1024)], [(pd, 0, 1024)])
            tiles = []
            for t in range(32):
                for h in range(2):
                    tiles.append((sb_xo[:, t * 1024 + h * 512:t * 1024 + h * 512 + 512], h * 512))
            lnv = emit_ln_apply(tiles, bve)

            tm_q = [0] * 32
            sq_qt = [0] * 32
            ve_qt_evac = [0] * 32

            def wq_load(dt):
                if dt >= 2:
                    P.wait("gpsimd", "TM", tm_q[dt - 2])
                buf = (dt % 2) * 32 * 128
                P.op("gpsimd", lambda E_, SS, dt=dt, buf=buf: E_.dma_start(
                    out=sb_wq[:, buf:buf + 32 * 128], in_=d_wqt[dt]), inc=(f"WQ{dt % 2}", 16))
                return (f"WQ{dt % 2}", P.val(f"WQ{dt % 2}"))

            gq_wq = [0] * 32
            gq_wq[0] = wq_load(0)
            P.wait("tensor", "VE", lnv)
            for dt in range(32):
                if dt + 1 < 32:
                    gq_wq[dt + 1] = wq_load(dt + 1)
                buf = (dt % 2) * 32 * 128
                py = pc if dt % 2 == 0 else pd
                P.wait("tensor", *gq_wq[dt])
                if dt >= 2:
                    P.wait("tensor", "VE", ve_qt_evac[dt - 2])
                for t in range(32):
                    for h in range(2):
                        tm_q[dt] = P.op("tensor", lambda E_, SS, t=t, h=h, buf=buf, py=py: E_.matmul(
                            py[:, h * 512:h * 512 + 512],
                            sb_wq[:, buf + t * 128:buf + t * 128 + 128],
                            sb_xo[:, t * 1024 + h * 512:t * 1024 + h * 512 + 512],
                            start=(t == 0), stop=(t == 31)),
                            inc=("TM", 1) if (t == 31 and h == 1) else None)
                P.wait("vector", "TM", tm_q[dt])
                if dt >= 2:
                    P.wait("vector", *sq_qt[dt - 2])
                qb = (dt % 2) * 1024
                P.op("vector", lambda E_, SS, qb=qb, py=py: E_.tensor_scalar_mul(
                    sb_qt[:, qb:qb + 1024], py[:, 0:1024], QSCALE), inc=("VE", 1))
                ve_qt_evac[dt] = P.val("VE")
                P.wait("sync", "VE", ve_qt_evac[dt])
                P.op("sync", lambda E_, SS, dt=dt, qb=qb: E_.dma_start(
                    out=d_qt[dt], in_=sb_qt[:, qb:qb + 1024]), inc=(f"QS{dt % 2}", 16))
                sq_qt[dt] = (f"QS{dt % 2}", P.val(f"QS{dt % 2}"))
            sq_qt_all = (P.val("QS0"), P.val("QS1"))
            barrier()

            if "qt" in d_dbg:
                for dt in range(32):
                    P.op("sync", lambda E_, SS, dt=dt: E_.dma_start(out=d_dbg["qt"][dt], in_=d_qt[dt]),
                         inc=("SQ", 16))
                P.wait("sync", "SQ", P.val("SQ"))
                barrier()

            if max_phase < 3:
                _finish(P); raise _Done()
            P.wait("gpsimd", "QS0", sq_qt_all[0])
            P.wait("gpsimd", "QS1", sq_qt_all[1])
            P.op("gpsimd", lambda E_, SS: E_.dma_start(
                out=sb_wk2[:, :].rearrange("p (r m) -> p r m", r=4), in_=d_wk2), inc=("GS", 16))
            P.op("gpsimd", lambda E_, SS: E_.dma_start(
                out=sb_wv2[:, :].rearrange("p (r m) -> p r m", r=4), in_=d_wv2), inc=("GS", 16))
            gq_wkv = P.val("GS")

            P.wait("vector", "GS", gs_msk)
            for rset in (range(4), range(4, 8)):
                for h in range(8):
                    b = 64 * (h % 2)
                    cbase = (h // 2) * S
                    for r in rset:
                        dst = sb_kcm if r < 4 else sb_vcm
                        ro = (r % 4) * 1024
                        P.op("vector", lambda E_, SS, b=b, cbase=cbase, r=r, dst=dst, ro=ro, h=h: E_.tensor_copy(
                            dst[b:b + 64, (h // 2) * 4096 + ro:(h // 2) * 4096 + ro + 1024],
                            sb_cf[b:b + 64, cbase + r:cbase + r + 8 * 1023 + 1:8]))
                if rset.start == 0:
                    ve_matk = P.op("vector", lambda E_, SS: E_.tensor_copy(sb_dum[:, 0:4], sb_dum[:, 4:8]),
                                   inc=("VE", 1))
            ve_mat = P.op("vector", lambda E_, SS: E_.tensor_copy(sb_dum[:, 0:4], sb_dum[:, 4:8]),
                          inc=("VE", 1))

            g_i = [0]
            se_exp = {}
            tm_l = {}

            def emit_pvl(gi, sk_t, cs, nn, stb, msk, vem, hkL):
                if msk:
                    P.wait("tensor", "VE", vem)
                else:
                    P.wait("tensor", "SE", se_exp[gi])
                P.op("tensor", lambda E_, SS, sk_t=sk_t, cs=cs, nn=nn, stb=stb, hkL=hkL: E_.matmul(
                    pc[:, cs:cs + nn],
                    sb_vp[:, sk_t * 256 + hkL * 128:sk_t * 256 + hkL * 128 + 128],
                    sb_pt[:, stb * 512:stb * 512 + nn],
                    start=(sk_t == 0), stop=(sk_t == 63), skip_group_check=True))
                v = P.op("tensor", lambda E_, SS, cs=cs, nn=nn, stb=stb, sk_t=sk_t: E_.matmul(
                    pd[0:1, cs:cs + nn], sb_one[:, 0:1],
                    sb_pt[:, stb * 512:stb * 512 + nn],
                    start=(sk_t == 0), stop=(sk_t == 63), skip_group_check=True), inc=("TM", 1))
                tm_l[gi] = v
            tm_slast = {}
            ve_ot = 0
            sq_ot = [0] * 32
            last_pc_reader_ve = ve_mat
            last_pd_reader_ve = ve_mat
            ve_kt_evac = ve_mat
            se_last_exp = 0

            P.wait("tensor", "VE", ve_matk)
            P.wait("tensor", "GS", gq_wkv)

            for pair in range(4):
                hk0 = 2 * pair
                for hkL in range(2):
                    hk = hk0 + hkL
                    for h in range(8):
                        pp = pc if (h % 2 == 0) else pd
                        P.wait("tensor", "VE", last_pc_reader_ve if pp is pc else last_pd_reader_ve)
                        b = 64 * (h % 2)
                        v = None
                        for half in range(2):
                            for r in range(4):
                                v = P.op("tensor", lambda E_, SS, pp=pp, b=b, h=h, r=r, half=half, hk=hk: E_.matmul(
                                    pp[:, half * 512:half * 512 + 512],
                                    sb_wk2[b:b + 64, r * 1024 + hk * 128:r * 1024 + hk * 128 + 128],
                                    sb_kcm[b:b + 64, (h // 2) * 4096 + r * 1024 + half * 512:
                                           (h // 2) * 4096 + r * 1024 + half * 512 + 512],
                                    start=(r == 0), stop=(r == 3)),
                                    inc=("TM", 1) if (half == 1 and r == 3) else None)
                        P.wait("vector", "TM", v)
                        P.op("vector", lambda E_, SS, hkL=hkL, h=h, pp=pp: E_.tensor_copy(
                            sb_kt[:, hkL * 8192 + h * 1024:hkL * 8192 + h * 1024 + 1024], pp[:, 0:1024]),
                            inc=("VE", 1))
                        if pp is pc:
                            last_pc_reader_ve = P.val("VE")
                        else:
                            last_pd_reader_ve = P.val("VE")
                ve_v_evac = [0, 0]
                if pair == 0:
                    P.wait("tensor", "VE", ve_mat)
                for svt in range(64):
                    h = svt // 8
                    b = 64 * (h % 2)
                    pp = ps[svt % 2]
                    if pair > 0 and svt < 2:
                        P.wait("tensor", "SE", se_last_exp)
                    if svt >= 2:
                        P.wait("tensor", "VE", ve_v_evac[svt % 2])
                    v = None
                    for r in range(4):
                        v = P.op("tensor", lambda E_, SS, pp=pp, b=b, h=h, r=r, svt=svt, hk0=hk0: E_.matmul(
                            pp[:, 0:256],
                            sb_vcm[b:b + 64, (h // 2) * 4096 + r * 1024 + 128 * (svt % 8):
                                   (h // 2) * 4096 + r * 1024 + 128 * (svt % 8) + 128],
                            sb_wv2[b:b + 64, r * 1024 + hk0 * 128:r * 1024 + hk0 * 128 + 256],
                            start=(r == 0), stop=(r == 3)),
                            inc=("TM", 1) if r == 3 else None)
                    P.wait("vector", "TM", v)
                    P.op("vector", lambda E_, SS, svt=svt, pp=pp: E_.tensor_copy(
                        sb_vp[:, svt * 256:svt * 256 + 256], pp[:, 0:256]), inc=("VE", 1))
                    ve_v_evac[svt % 2] = P.val("VE")
                ve_kv = P.val("VE")
                P.wait("tensor", "VE", ve_kv)

                for hkL in range(2):
                    hk = hk0 + hkL
                    qbuf = (hk % 2) * 4096
                    P.wait("gpsimd", "QS0", sq_qt_all[0])
                    P.wait("gpsimd", "QS1", sq_qt_all[1])
                    if hk >= 2:
                        P.wait("gpsimd", "TM", tm_slast.get(hk - 2, 0))
                    P.op("gpsimd", lambda E_, SS, hk=hk, qbuf=qbuf: E_.dma_start(
                        out=sb_qg[:, qbuf:qbuf + 4096].rearrange("h p m -> p h m" if False else "p (h m) -> p h m", h=4),
                        in_=d_qt[4 * hk:4 * hk + 4].rearrange("h p m -> p h m")), inc=(f"QG{hk % 2}", 16))
                    P.wait("tensor", f"QG{hk % 2}", P.val(f"QG{hk % 2}"))

                    for ql in range(4):
                        qh = 4 * hk + ql
                        units = []
                        for sk_t in range(64):
                            k = sk_t // 8
                            c0 = 128 * k
                            if c0 < 512:
                                units.append((sk_t, c0, 512 - c0, True))
                                units.append((sk_t, 512, 512, False))
                            else:
                                units.append((sk_t, c0, 1024 - c0, True))
                        nu = len(units)
                        pend = []
                        for iu, (sk_t, cs, nn, msk) in enumerate(units):
                            gi = g_i[0] + iu
                            stb = gi % 4
                            if gi >= 4:
                                P.wait("tensor", "SE", se_exp[gi - 4])
                            if iu == 0:
                                P.wait("tensor", "VE", ve_ot if qh > 0 else ve_kv)
                            tms = P.op("tensor", lambda E_, SS, stb=stb, hkL=hkL, sk_t=sk_t, ql=ql,
                                 qbuf=qbuf, cs=cs, nn=nn: E_.matmul(
                                     ps[stb][:, 0:nn],
                                     sb_kt[:, hkL * 8192 + sk_t * 128:hkL * 8192 + sk_t * 128 + 128],
                                     sb_qg[:, qbuf + ql * 1024 + cs:qbuf + ql * 1024 + cs + nn],
                                     start=True, stop=True, skip_group_check=True), inc=("TM", 1))
                            if iu == nu - 1:
                                tm_slast[hk] = tms
                            P.wait("scalar", "TM", tms)
                            if gi >= 4:
                                P.wait("scalar", "TM", tm_l[gi - 4])
                            P.op("scalar", lambda E_, SS, stb=stb, nn=nn: E_.activation(
                                sb_pt[:, stb * 512:stb * 512 + nn], ps[stb][:, 0:nn], AF.Exp),
                                inc=("SE", 1))
                            se_exp[gi] = P.val("SE")
                            se_last_exp = se_exp[gi]
                            vem = None
                            if msk:
                                P.wait("vector", "SE", se_exp[gi])
                                k = sk_t // 8
                                mi = (k % 2) * 8 + (sk_t % 8)
                                P.op("vector", lambda E_, SS, stb=stb, mi=mi: E_.tensor_mul(
                                    sb_pt[:, stb * 512:stb * 512 + 128],
                                    sb_pt[:, stb * 512:stb * 512 + 128],
                                    sb_msk[:, mi * 128:mi * 128 + 128]), inc=("VE", 1))
                                vem = P.val("VE")
                            pend.append((gi, sk_t, cs, nn, stb, msk, vem, hkL))
                            if len(pend) > 3:
                                emit_pvl(*pend.pop(0))
                        for p_ in pend:
                            emit_pvl(*p_)
                        g_i[0] += nu

                        P.wait("vector", "TM", tm_l[g_i[0] - 1])
                        P.op("vector", lambda E_, SS: E_.tensor_copy(sb_ra[0:1, 0:1024], pd[0:1, 0:1024]))
                        P.op("vector", lambda E_, SS: E_.reciprocal(sb_rb[0:1, 0:1024], sb_ra[0:1, 0:1024]))
                        P.op("vector", lambda E_, SS: E_.tensor_copy(sb_rbf[0:1, 0:1024], sb_rb[0:1, 0:1024]),
                             inc=("VE", 1))
                        P.wait("tensor", "VE", P.val("VE"))
                        tv = None
                        for h2 in range(2):
                            tv = P.op("tensor", lambda E_, SS, h2=h2: E_.matmul(
                                pd[:, h2 * 512:h2 * 512 + 512], sb_onr[0:1, 0:128],
                                sb_rbf[0:1, h2 * 512:h2 * 512 + 512], start=True, stop=True),
                                inc=("TM", 1) if h2 == 1 else None)
                        P.wait("vector", "TM", tv)
                        if qh >= 2:
                            P.wait("vector", *sq_ot[qh - 2])
                        ob = (qh % 2) * 1024
                        P.op("vector", lambda E_, SS: E_.tensor_copy(sb_invl[:, :], pd[:, 0:1024]))
                        P.op("vector", lambda E_, SS, ob=ob: E_.tensor_mul(
                            sb_otsb[:, ob:ob + 1024], pc[:, 0:1024], sb_invl[:, :]), inc=("VE", 1))
                        ve_ot = P.val("VE")
                        P.wait("sync", "VE", ve_ot)
                        P.op("sync", lambda E_, SS, qh=qh, ob=ob: E_.dma_start(
                            out=d_ot[qh], in_=sb_otsb[:, ob:ob + 1024]), inc=(f"OS{qh % 2}", 16))
                        sq_ot[qh] = (f"OS{qh % 2}", P.val(f"OS{qh % 2}"))
                last_pc_reader_ve = ve_ot
                last_pd_reader_ve = ve_ot

            tm_c_end = P.val("TM")
            for qh in range(32):
                P.wait("sync", "TM", tm_c_end)
                P.wait("sync", *sq_ot[qh])
                P.op("sync", lambda E_, SS, qh=qh: E_.dma_start(
                    out=sb_ota[:, qh * 1024:qh * 1024 + 1024], in_=d_ot[qh]), inc=("OP", 16))
            sq_ota_pre = P.val("OP")
            barrier()

            if "ot" in d_dbg:
                for qh in range(32):
                    P.op("sync", lambda E_, SS, qh=qh: E_.dma_start(out=d_dbg["ot"][qh], in_=d_ot[qh]),
                         inc=("SQ", 16))
                P.wait("sync", "SQ", P.val("SQ"))
                barrier()

            if max_phase < 4:
                _finish(P); raise _Done()
            P.wait("tensor", "OP", sq_ota_pre)

            tm_wo = [0] * 32
            ve_x1 = [0] * 32
            sq_x1 = [0] * 32
            se_x1b = [0] * 32
            tm_st = [0] * 32
            gq_wo = [0] * 32
            sq_xr = [0] * 32

            def wo_load(ot):
                if ot >= 2:
                    P.wait("gpsimd", "TM", tm_wo[ot - 2])
                buf = (ot % 2) * 4096
                P.op("gpsimd", lambda E_, SS, ot=ot, buf=buf: E_.dma_start(
                    out=sb_wo[:, buf:buf + 4096], in_=d_wot[ot]), inc=(f"WO{ot % 2}", 16))
                gq_wo[ot] = (f"WO{ot % 2}", P.val(f"WO{ot % 2}"))

            def xr_load(ot):
                if ot >= 2:
                    P.wait("sync", "VE", ve_x1[ot - 2])
                buf = (ot % 2) * 1024
                P.op("sync", lambda E_, SS, ot=ot, buf=buf: E_.dma_start(
                    out=sb_xr[:, buf:buf + 1024], in_=d_xo[128 * ot:128 * ot + 128, :]), inc=(f"XR{ot % 2}", 16))
                sq_xr[ot] = (f"XR{ot % 2}", P.val(f"XR{ot % 2}"))

            wo_load(0)
            xr_load(0)
            pend_stats = []

            def emit_stats_d(ot):
                sq_slot = sb_xsq if ot % 2 == 0 else sb_tmp
                P.wait("tensor", "SE", se_x1b[ot])
                for h in range(2):
                    P.op("tensor", lambda E_, SS, ot=ot, h=h: E_.matmul(
                        ps[h][0:1, 0:512], sb_one[:, 0:1],
                        sb_x1b[:, ot * 1024 + h * 512:ot * 1024 + h * 512 + 512],
                        start=(ot == 0), stop=(ot == 31)))
                    tm_st[ot] = P.op("tensor", lambda E_, SS, ot=ot, h=h, sq_slot=sq_slot: E_.matmul(
                        ps[2 + h][0:1, 0:512], sb_one[:, 0:1], sq_slot[:, h * 512:h * 512 + 512],
                        start=(ot == 0), stop=(ot == 31)),
                        inc=("TM", 1) if h == 1 else None)

            for ot in range(32):
                if ot + 1 < 32:
                    wo_load(ot + 1)
                    xr_load(ot + 1)
                buf = (ot % 2) * 4096
                py = pc if ot % 2 == 0 else pd
                P.wait("tensor", *gq_wo[ot])
                if ot >= 2:
                    P.wait("tensor", "VE", ve_x1[ot - 2])
                for t in range(32):
                    for h in range(2):
                        tm_wo[ot] = P.op("tensor", lambda E_, SS, t=t, h=h, buf=buf, py=py: E_.matmul(
                            py[:, h * 512:h * 512 + 512],
                            sb_wo[:, buf + t * 128:buf + t * 128 + 128],
                            sb_ota[:, t * 1024 + h * 512:t * 1024 + h * 512 + 512],
                            start=(t == 0), stop=(t == 31)),
                            inc=("TM", 1) if (t == 31 and h == 1) else None)
                P.wait("vector", "TM", tm_wo[ot])
                P.wait("vector", *sq_xr[ot])
                if ot >= 2:
                    P.wait("vector", *sq_x1[ot - 2])
                xb = (ot % 2) * 1024
                P.op("vector", lambda E_, SS, xb=xb, py=py: E_.tensor_add(
                    sb_x1s[:, xb:xb + 1024], py[:, 0:1024], sb_xr[:, xb:xb + 1024]), inc=("VE", 1))
                ve_x1[ot] = P.val("VE")
                P.wait("scalar", "VE", ve_x1[ot])
                if ot >= 2:
                    P.wait("scalar", "TM", tm_st[ot - 2])
                sq_slot = sb_xsq if ot % 2 == 0 else sb_tmp
                P.op("scalar", lambda E_, SS, ot=ot, xb=xb: E_.copy(
                    sb_x1b[:, ot * 1024:ot * 1024 + 1024], sb_x1s[:, xb:xb + 1024]))
                P.op("scalar", lambda E_, SS, ot=ot, sq_slot=sq_slot: E_.activation(
                    sq_slot[:, 0:1024], sb_x1b[:, ot * 1024:ot * 1024 + 1024], AF.Square),
                    inc=("SE", 1))
                se_x1b[ot] = P.val("SE")
                if ot >= 1:
                    emit_stats_d(ot - 1)
                P.wait("sync", "VE", ve_x1[ot])
                P.op("sync", lambda E_, SS, ot=ot, xb=xb: E_.dma_start(
                    out=d_x1[ot], in_=sb_x1s[:, xb:xb + 1024]), inc=(f"XS{ot % 2}", 16))
                sq_x1[ot] = (f"XS{ot % 2}", P.val(f"XS{ot % 2}"))
            emit_stats_d(31)
            sq_x1_all = (P.val("XS0"), P.val("XS1"))

            bve2 = ln_rows(1024, 0,
                           [(ps[0], 0, 512), (ps[1], 0, 512)],
                           [(ps[2], 0, 512), (ps[3], 0, 512)])
            barrier()

            if "x1" in d_dbg:
                for ot in range(32):
                    P.op("sync", lambda E_, SS, ot=ot: E_.dma_start(out=d_dbg["x1"][ot], in_=d_x1[ot]),
                         inc=("SQ", 16))
                P.wait("sync", "SQ", P.val("SQ"))
                barrier()

            if max_phase < 5:
                _finish(P); raise _Done()
            tiles = []
            for t in range(32):
                for h in range(2):
                    tiles.append((sb_x1b[:, t * 1024 + h * 512:t * 1024 + h * 512 + 512], h * 512))
            lnv2 = emit_ln_apply(tiles, bve2)
            P.wait("tensor", "VE", lnv2)
            if "ab2" in d_dbg:
                P.wait("sync", "VE", lnv2)
                P.op("sync", lambda E_, SS: E_.dma_start(out=d_dbg["ab2"][:, 0:1024], in_=sb_abc[:, :]), inc=("SQ", 16))
                P.op("sync", lambda E_, SS: E_.dma_start(out=d_dbg["ab2"][:, 1024:2048], in_=sb_bbc[:, :]), inc=("SQ", 16))
                P.op("sync", lambda E_, SS: E_.dma_start(out=d_dbg["x1n"], in_=sb_x1b[:, :]), inc=("SQ", 16))
                P.wait("sync", "SQ", P.val("SQ"))
                barrier()

            tm_ffn = [0] * 32
            se_wb = [0] * 32
            ve_os = [0] * 32
            sq_os = [0] * 32
            sq_wf = [0] * 32
            sq_x1r = [0] * 32

            def wf_load(ot):
                if ot < 2:
                    P.wait("sync", "XS0", sq_x1_all[0])
                    P.wait("sync", "XS1", sq_x1_all[1])
                if ot >= 2:
                    P.wait("sync", "SE", se_wb[ot - 2])
                buf = (ot % 2) * 4096
                P.op("sync", lambda E_, SS, ot=ot, buf=buf: E_.dma_start(
                    out=sb_wf[:, buf:buf + 4096], in_=d_wft[ot]), inc=(f"WF{ot % 2}", 16))
                sq_wf[ot] = (f"WF{ot % 2}", P.val(f"WF{ot % 2}"))

            def x1r_load(ot):
                P.wait("sync", "XS0", sq_x1_all[0])
                P.wait("sync", "XS1", sq_x1_all[1])
                if ot >= 2:
                    P.wait("sync", "VE", ve_os[ot - 2])
                buf = (ot % 2) * 1024
                P.op("sync", lambda E_, SS, ot=ot, buf=buf: E_.dma_start(
                    out=sb_x1r[:, buf:buf + 1024], in_=d_x1[ot]), inc=(f"XB{ot % 2}", 16))
                sq_x1r[ot] = (f"XB{ot % 2}", P.val(f"XB{ot % 2}"))

            P.wait("scalar", "SQ", 16)
            wf_load(0)
            x1r_load(0)
            for ot in range(32):
                if ot + 1 < 32:
                    wf_load(ot + 1)
                    x1r_load(ot + 1)
                buf = (ot % 2) * 4096
                py = pc if ot % 2 == 0 else pd
                P.wait("scalar", *sq_wf[ot])
                if ot >= 2:
                    P.wait("scalar", "TM", tm_ffn[ot - 2])
                for it in range(32):
                    P.op("scalar", lambda E_, SS, ot=ot, it=it, buf=buf: E_.mul(
                        sb_wb[:, buf + it * 128:buf + it * 128 + 128],
                        sb_wf[:, buf + it * 128:buf + it * 128 + 128],
                        sb_scl[:, ot * 32 + it:ot * 32 + it + 1]))
                P.op("scalar", lambda E_, SS: E_.copy(sb_dum[:, 8:12], sb_dum[:, 12:16]), inc=("SE", 1))
                se_wb[ot] = P.val("SE")
                P.wait("tensor", "SE", se_wb[ot])
                if ot >= 2:
                    P.wait("tensor", "VE", ve_os[ot - 2])
                for it in range(32):
                    for h in range(2):
                        tm_ffn[ot] = P.op("tensor", lambda E_, SS, it=it, h=h, buf=buf, py=py: E_.matmul(
                            py[:, h * 512:h * 512 + 512],
                            sb_wb[:, buf + it * 128:buf + it * 128 + 128],
                            sb_x1b[:, it * 1024 + h * 512:it * 1024 + h * 512 + 512],
                            start=(it == 0), stop=(it == 31)),
                            inc=("TM", 1) if (it == 31 and h == 1) else None)
                P.wait("vector", "TM", tm_ffn[ot])
                P.wait("vector", *sq_x1r[ot])
                if ot >= 2:
                    P.wait("vector", *sq_os[ot - 2])
                xb = (ot % 2) * 1024
                P.op("vector", lambda E_, SS, xb=xb, py=py: E_.tensor_add(
                    sb_os[:, xb:xb + 1024], py[:, 0:1024], sb_x1r[:, xb:xb + 1024]), inc=("VE", 1))
                ve_os[ot] = P.val("VE")
                P.wait("sync", "VE", ve_os[ot])
                P.op("sync", lambda E_, SS, ot=ot, xb=xb: E_.dma_start(
                    out=d_out[ot], in_=sb_os[:, xb:xb + 1024]), inc=(f"OU{ot % 2}", 16))
                sq_os[ot] = (f"OU{ot % 2}", P.val(f"OU{ot % 2}"))

            P.wait("sync", "OU0", P.val("OU0"))
            P.wait("sync", "OU1", P.val("OU1"))
            P.wait("gpsimd", "OU0", P.val("OU0"))
            P.wait("gpsimd", "OU1", P.val("OU1"))


        except _Done:
            pass
        _dispatch(P, blk, sems)
    return nc


def _finish(P):
    pass


def _dispatch(P, blk, sems):
        @blk.tensor
        def _(e):
            for f in P.q["tensor"]:
                f(e, sems)

        @blk.vector
        def _(e):
            for f in P.q["vector"]:
                f(e, sems)

        @blk.scalar
        def _(e):
            for f in P.q["scalar"]:
                f(e, sems)

        @blk.gpsimd
        def _(e):
            for f in P.q["gpsimd"]:
                f(e, sems)

        @blk.sync
        def _(e):
            for f in P.q["sync"]:
                f(e, sems)


def _stage(inputs):
    x = np.asarray(inputs["L0_attn_hidden_states"], dtype=np.float32)[0]
    Wq = np.asarray(inputs["L0_attn_Wq"], dtype=np.float32)
    Wkd = np.asarray(inputs["L0_attn_Wk_down"], dtype=np.float32)
    Wvd = np.asarray(inputs["L0_attn_Wv_down"], dtype=np.float32)
    Wku = np.asarray(inputs["L0_attn_Wk_up"], dtype=np.float32)
    Wvu = np.asarray(inputs["L0_attn_Wv_up"], dtype=np.float32)
    Wo = np.asarray(inputs["L0_attn_Wo"], dtype=np.float32)
    Wf = np.asarray(inputs["L0_ffn_weight"], dtype=np.float32)
    scl = np.asarray(inputs["L0_ffn_weight_scale"], dtype=np.float32)

    wd = np.concatenate(
        [np.concatenate([Wkd[32 * h:32 * h + 32], Wvd[32 * h:32 * h + 32]], axis=0)
         for h in range(8)], axis=0)
    wdt = np.ascontiguousarray(
        wd.T.reshape(32, 128, 512).transpose(1, 0, 2).reshape(128, 32 * 512))

    xt_sw = np.ascontiguousarray(
        x.T.reshape(32, 128, 16, 512).transpose(2, 1, 0, 3).reshape(16, 128, 32 * 512))

    def tiled_T(w):
        return np.ascontiguousarray(
            w.T.reshape(32, 128, 32, 128).transpose(2, 1, 0, 3).reshape(32, 128, 32 * 128))

    wqt = tiled_T(Wq)
    wot = tiled_T(Wo)
    wft = tiled_T(Wf)

    def dup2(wu):
        wt = wu.T
        out = np.empty((128, 4, 1024), np.float32)
        for r in range(4):
            for b in range(2):
                out[64 * b:64 * b + 64, r] = wt[64 * r:64 * r + 64]
        return np.ascontiguousarray(out)

    wk2 = dup2(Wku)
    wv2 = dup2(Wvu)
    scale_pp = np.ascontiguousarray(
        np.broadcast_to(scl.reshape(1, 1024), (128, 1024)).astype(np.float32))

    shared = dict(xt_sw=xt_sw, wdt_sw=wdt, wqt_sw=wqt, wot_sw=wot, wft_sw=wft,
                  wk2=wk2, wv2=wv2, scale_pp=scale_pp)

    tri = np.tril(np.ones((128, 128), np.float32)).T
    in_maps = []
    for c in range(NCORE):
        blocks = zig_blocks(c)
        rows = np.concatenate([np.arange(b * 128, b * 128 + 128) for b in blocks])
        xt_own = np.ascontiguousarray(x[rows].T)
        mask = np.empty((16, 128, 128), np.float32)
        for kpar in range(2):
            bpos = c if kpar == 0 else 7 - c
            for m in range(8):
                if m < bpos:
                    mask[kpar * 8 + m] = 1.0
                elif m == bpos:
                    mask[kpar * 8 + m] = tri
                else:
                    mask[kpar * 8 + m] = 0.0
        im = dict(shared)
        im["xt_own"] = xt_own
        im["xo_sw"] = np.ascontiguousarray(
            xt_own.reshape(32, 128, 1024).transpose(1, 0, 2).reshape(128, 32 * 1024))
        im["mask01"] = np.ascontiguousarray(mask)
        in_maps.append(im)
    return in_maps


def kernel(**inputs):
    in_maps = _stage(inputs)
    nc = build_program(tuple(DEBUG_TAPS), MAX_PHASE, NCHUNKS, A_LEVEL, LR_LEVEL, NCORES_RUN)
    res = run_bass_kernel_spmd(nc, in_maps[:NCORES_RUN], core_ids=list(range(NCORES_RUN)))
    out = np.zeros((1, S, HID), np.float32)
    for c in range(NCORES_RUN):
        x2t = np.asarray(res.results[c]["o"]).reshape(HID, 1024)
        for j, b in enumerate(zig_blocks(c)):
            out[0, b * 128:(b + 1) * 128, :] = x2t[:, j * 128:(j + 1) * 128].T
    if DEBUG_TAPS:
        kernel._dbg = [{k: np.asarray(v) for k, v in r.items() if k.startswith("dbg_")}
                       for r in res.results]
    return out


# version 17
# speedup vs baseline: 1.2013x; 1.0008x over previous
import math
import numpy as np
from contextlib import ExitStack

import concourse.bass as bass
import concourse.mybir as mybir
from concourse.bass_utils import run_bass_kernel_spmd

F32 = mybir.dt.float32
BF16 = mybir.dt.bfloat16
AF = mybir.ActivationFunctionType

NCORE = 8
S = 8192
HID = 4096
QSCALE = 1.0 / math.sqrt(128.0)
EPS = 1e-5

DEBUG_TAPS = []
MAX_PHASE = 5
NCORES_RUN = 8
NCHUNKS = 16
A_LEVEL = 4
LR_LEVEL = 4


def zig_blocks(c):
    return sorted([c, 15 - c, 16 + c, 31 - c, 32 + c, 47 - c, 48 + c, 63 - c])


BIG0 = 0
BIG1 = 65536
MID = 131072
SM_PT = 163840
SM_OTSB = 167936
SM_INVL = 172032
SM_ABC = 176128
SM_BBC = 178176
SM_TMP = 180224
SM_SCL = 182272
SM_MSK = 186368
SM_XSQ = 190464
SM_RA = 192512
SM_RB = 196608
SM_RC = 200704
SM_RBF = 204800
SM_ONE = 208896
SM_ONR = 208928
SM_DUM = 209216

ENGS = ("tensor", "vector", "scalar", "gpsimd", "sync")


class _Done(Exception):
    pass


class Prog:
    def __init__(self):
        self.q = {e: [] for e in ENGS}
        self.cnt = {}
        self.nbar = 0

    def bump(self, sem, n):
        self.cnt[sem] = self.cnt.get(sem, 0) + n
        return self.cnt[sem]

    def val(self, sem):
        return self.cnt.get(sem, 0)

    def wait(self, eng, sem, v):
        if v <= 0:
            return
        assert v <= self.val(sem), f"{eng} waits {sem}>={v} but only {self.val(sem)} emitted"
        self.q[eng].append(lambda E, SS, sem=sem, v=v: E.wait_ge(SS[sem], v))

    def op(self, eng, builder, inc=None):
        if inc is not None:
            sem, n = inc
            v = self.bump(sem, n)
            self.q[eng].append(
                lambda E, SS, b=builder, sem=sem, n=n: b(E, SS).then_inc(SS[sem], n))
            return v
        self.q[eng].append(lambda E, SS, b=builder: b(E, SS))
        return None


def build_program(debug_taps=(), max_phase=5, nchunks=16, a_level=4, lr_level=4, ncores=8):
    nc = bass.Bass("TRN2", target_bir_lowering=False)

    d_xt = nc.dram_tensor("xt_sw", [16, 128, 32 * 512], F32, kind="ExternalInput").ap()
    d_xo = nc.dram_tensor("xt_own", [HID, 1024], F32, kind="ExternalInput").ap()
    d_xo2 = nc.dram_tensor("xo_sw", [128, 32 * 1024], F32, kind="ExternalInput").ap()
    d_wdt = nc.dram_tensor("wdt_sw", [128, 32 * 512], F32, kind="ExternalInput").ap()
    d_wqt = nc.dram_tensor("wqt_sw", [32, 128, 32 * 128], F32, kind="ExternalInput").ap()
    d_wot = nc.dram_tensor("wot_sw", [32, 128, 32 * 128], F32, kind="ExternalInput").ap()
    d_wft = nc.dram_tensor("wft_sw", [32, 128, 32 * 128], F32, kind="ExternalInput").ap()
    d_wk2 = nc.dram_tensor("wk2", [128, 4, 1024], F32, kind="ExternalInput").ap()
    d_wv2 = nc.dram_tensor("wv2", [128, 4, 1024], F32, kind="ExternalInput").ap()
    d_scl = nc.dram_tensor("scale_pp", [128, 1024], F32, kind="ExternalInput").ap()
    d_msk = nc.dram_tensor("mask01", [16, 128, 128], F32, kind="ExternalInput").ap()

    d_qt = nc.dram_tensor("qt_spill", [32, 128, 1024], BF16, kind="Internal").ap()
    d_ot = nc.dram_tensor("ot_spill", [32, 128, 1024], BF16, kind="Internal").ap()
    d_x1 = nc.dram_tensor("x1_spill", [32, 128, 1024], F32, kind="Internal").ap()
    d_out = nc.dram_tensor("o", [32, 128, 1024], F32, kind="ExternalOutput").ap()

    d_dbg = {}
    if "cfull" in debug_taps:
        d_dbg["cfull"] = nc.dram_tensor("dbg_cfull", [128, 4 * S], BF16, kind="ExternalOutput").ap()
    if "ab" in debug_taps:
        d_dbg["ab"] = nc.dram_tensor("dbg_ab", [128, 2048], BF16, kind="ExternalOutput").ap()
        d_dbg["rows"] = nc.dram_tensor("dbg_rows", [1, 3072], F32, kind="ExternalOutput").ap()
    if "ab2" in debug_taps:
        d_dbg["ab2"] = nc.dram_tensor("dbg_ab2", [128, 2048], BF16, kind="ExternalOutput").ap()
        d_dbg["x1n"] = nc.dram_tensor("dbg_x1n", [128, 32 * 1024], BF16, kind="ExternalOutput").ap()
    if "qt" in debug_taps:
        d_dbg["qt"] = nc.dram_tensor("dbg_qt", [32, 128, 1024], BF16, kind="ExternalOutput").ap()
    if "ot" in debug_taps:
        d_dbg["ot"] = nc.dram_tensor("dbg_ot", [32, 128, 1024], BF16, kind="ExternalOutput").ap()
    if "x1" in debug_taps:
        d_dbg["x1"] = nc.dram_tensor("dbg_x1", [32, 128, 1024], F32, kind="ExternalOutput").ap()

    P = Prog()

    with ExitStack() as ctx:
        E = ctx.enter_context
        _SBUF_BASE = 16512

        def A(name, shape, dtype, offset):
            return nc.alloc_sbuf_tensor_at(name, shape, dtype, offset=offset + _SBUF_BASE)

        sb_pt = A("pt", [128, 4 * 512], BF16, offset=SM_PT)
        sb_otsb = A("otsb", [128, 2 * 1024], BF16, offset=SM_OTSB)
        sb_invl = A("invl", [128, 1024], F32, offset=SM_INVL)
        sb_abc = A("abc", [128, 1024], BF16, offset=SM_ABC)
        sb_bbc = A("bbc", [128, 1024], BF16, offset=SM_BBC)
        sb_tmp = A("tmpb", [128, 1024], BF16, offset=SM_TMP)
        sb_scl = A("sclpp", [128, 1024], F32, offset=SM_SCL)
        sb_msk = A("maskb", [128, 16 * 128], BF16, offset=SM_MSK)
        sb_xsq = A("xsq", [128, 1024], BF16, offset=SM_XSQ)
        sb_ra = A("rowa", [1, 1024], F32, offset=SM_RA)
        sb_rb = A("rowb", [1, 1024], F32, offset=SM_RB)
        sb_rc = A("rowc", [1, 1024], F32, offset=SM_RC)
        sb_rbf = A("rowbf", [1, 2048], BF16, offset=SM_RBF)
        sb_one = A("onec", [128, 8], BF16, offset=SM_ONE)
        sb_onr = A("oner", [1, 128], BF16, offset=SM_ONR)
        sb_dum = A("dum", [128, 16], F32, offset=SM_DUM)

        sb_xt = A("xt", [128, 2 * 32 * 512], BF16, offset=BIG0)
        sb_xo = A("xo", [128, 32 * 1024], BF16, offset=BIG0)
        sb_cf = A("cfull", [128, 4 * S], BF16, offset=BIG1)
        sb_wdt = A("wdtb", [128, 32 * 512], BF16, offset=MID)
        sb_wq = A("wqb", [128, 2 * 32 * 128], BF16, offset=MID)
        sb_qt = A("qtb", [128, 2 * 1024], BF16, offset=MID + 16384)
        sb_kcm = A("kcmp", [128, 16384], BF16, offset=BIG0)
        sb_vcm = A("vcmp", [128, 16384], BF16, offset=BIG0 + 32768)
        sb_kt = A("ktp", [128, 2 * 8192], BF16, offset=BIG1)
        sb_vp = A("vp", [128, 64 * 256], BF16, offset=BIG1 + 32768)
        sb_qg = A("qg", [128, 2 * 4 * 1024], BF16, offset=MID)
        sb_wk2 = A("wk2b", [128, 4 * 1024], BF16, offset=MID + 16384)
        sb_wv2 = A("wv2b", [128, 4 * 1024], BF16, offset=MID + 24576)
        sb_ota = A("otall", [128, 32 * 1024], BF16, offset=BIG0)
        sb_x1b = A("x1b", [128, 32 * 1024], BF16, offset=BIG1)
        sb_wo = A("wob", [128, 2 * 32 * 128], BF16, offset=MID)
        sb_xr = A("xr", [128, 2 * 1024], F32, offset=MID + 16384)
        sb_x1s = A("x1s", [128, 2 * 1024], F32, offset=MID + 24576)
        sb_wf = A("wfb", [128, 2 * 32 * 128], F32, offset=MID)
        sb_wb = A("wbb", [128, 2 * 32 * 128], BF16, offset=BIG0)
        sb_x1r = A("x1r", [128, 2 * 1024], F32, offset=BIG0 + 16384)
        sb_os = A("osb", [128, 2 * 1024], F32, offset=BIG0 + 24576)

        ps = [E(nc.psum_tensor(f"ps{i}", [128, 512], F32)) for i in range(4)]
        pc = E(nc.psum_tensor("pc", [128, 1024], F32))
        pd = E(nc.psum_tensor("pd", [128, 1024], F32))

        sems = {}
        for s_ in ("GQ", "SQ", "TM", "VE", "SE", "PB",
                   "CH0", "CH1", "WQ0", "WQ1", "QG0", "QG1", "WO0", "WO1",
                   "QS0", "QS1", "OS0", "OS1", "OP", "XR0", "XR1",
                   "XS0", "XS1", "WF0", "WF1", "XB0", "XB1", "OU0", "OU1", "GS"):
            sems[s_] = E(nc.semaphore(s_))

        blk = E(nc.Block())

        def barrier():
            P.nbar += 1
            tgt = 5 * P.nbar
            P.bump("PB", 5)
            P.q["tensor"].append(lambda E_, SS: E_.matmul(
                ps[0][0:8, 0:4], sb_onr[0:1, 0:8], sb_onr[0:1, 4:8],
                start=True, stop=True, skip_group_check=True).then_inc(SS["PB"], 1))
            P.q["sync"].append(lambda E_, SS: E_.nop().then_inc(SS["PB"], 1))
            P.q["gpsimd"].append(lambda E_, SS: E_.nop().then_inc(SS["PB"], 1))
            P.q["vector"].append(
                lambda E_, SS: E_.tensor_copy(sb_dum[:, 0:4], sb_dum[:, 4:8]).then_inc(SS["PB"], 1))
            P.q["scalar"].append(
                lambda E_, SS: E_.copy(sb_dum[:, 8:12], sb_dum[:, 12:16]).then_inc(SS["PB"], 1))
            for e in ENGS:
                P.wait(e, "PB", tgt)

        def emit_stats(src_aps, sumA, sumB, n_si):
            tm_sq = {}
            occ = {}
            lastocc = {}
            for i, (_, co) in enumerate(src_aps):
                lastocc[co] = i
            for i, (src, co) in enumerate(src_aps):
                slot = (i % 2) * 512
                if i >= 2:
                    P.wait("scalar", "TM", tm_sq[i - 2])
                P.op("scalar", lambda E_, SS, src=src, slot=slot: E_.activation(
                    sb_xsq[:, slot:slot + 512], src, AF.Square), inc=("SE", 1))
                sev = P.val("SE")
                P.wait("tensor", "SE", sev)
                pa, ca = sumA[co]
                pb, cb = sumB[co]
                stA = co not in occ
                occ[co] = True
                last = lastocc[co] == i
                P.op("tensor", lambda E_, SS, src=src, pa=pa, ca=ca, stA=stA, last=last: E_.matmul(
                    pa[0:1, ca:ca + 512], sb_one[:, 0:1], src, start=stA, stop=last))
                v = P.op("tensor", lambda E_, SS, pb=pb, cb=cb, slot=slot, stA=stA, last=last: E_.matmul(
                    pb[0:1, cb:cb + 512], sb_one[:, 0:1], sb_xsq[:, slot:slot + 512],
                    start=stA, stop=last), inc=("TM", 1))
                tm_sq[i] = v

        def ln_rows(n_si, out_co, rowA_srcs, rowB_srcs):
            P.wait("vector", "TM", P.val("TM"))
            off = 0
            for p_, c_, n_ in rowA_srcs:
                P.op("vector", lambda E_, SS, p_=p_, c_=c_, n_=n_, off=off: E_.tensor_copy(
                    sb_ra[0:1, off:off + n_], p_[0:1, c_:c_ + n_]))
                off += n_
            off = 0
            for p_, c_, n_ in rowB_srcs:
                P.op("vector", lambda E_, SS, p_=p_, c_=c_, n_=n_, off=off: E_.tensor_copy(
                    sb_rb[0:1, off:off + n_], p_[0:1, c_:c_ + n_]))
                off += n_
            n = n_si
            P.op("vector", lambda E_, SS, n=n: E_.tensor_scalar_mul(sb_ra[0:1, 0:n], sb_ra[0:1, 0:n], 1.0 / HID))
            P.op("vector", lambda E_, SS, n=n: E_.tensor_scalar_mul(sb_rb[0:1, 0:n], sb_rb[0:1, 0:n], 1.0 / HID))
            P.op("vector", lambda E_, SS, n=n: E_.tensor_mul(sb_rc[0:1, 0:n], sb_ra[0:1, 0:n], sb_ra[0:1, 0:n]))
            P.op("vector", lambda E_, SS, n=n: E_.tensor_sub(sb_rb[0:1, 0:n], sb_rb[0:1, 0:n], sb_rc[0:1, 0:n]))
            P.op("vector", lambda E_, SS, n=n: E_.tensor_scalar_add(sb_rb[0:1, 0:n], sb_rb[0:1, 0:n], EPS),
                 inc=("VE", 1))
            P.wait("scalar", "VE", P.val("VE"))
            P.op("scalar", lambda E_, SS, n=n: E_.activation(sb_rc[0:1, 0:n], sb_rb[0:1, 0:n], AF.Ln))
            P.op("scalar", lambda E_, SS, n=n: E_.activation(sb_rc[0:1, 0:n], sb_rc[0:1, 0:n], AF.Exp, scale=-0.5),
                 inc=("SE", 1))
            P.wait("vector", "SE", P.val("SE"))
            P.op("vector", lambda E_, SS, n=n: E_.tensor_copy(sb_rbf[0:1, 0:n], sb_rc[0:1, 0:n]))
            P.op("vector", lambda E_, SS, n=n: E_.tensor_mul(sb_ra[0:1, 0:n], sb_ra[0:1, 0:n], sb_rc[0:1, 0:n]))
            P.op("vector", lambda E_, SS, n=n: E_.tensor_scalar_mul(sb_ra[0:1, 0:n], sb_ra[0:1, 0:n], -1.0))
            P.op("vector", lambda E_, SS, n=n: E_.tensor_copy(sb_rbf[0:1, 1024:1024 + n], sb_ra[0:1, 0:n]),
                 inc=("VE", 1))
            P.wait("tensor", "VE", P.val("VE"))
            for h in range(0, n, 512):
                nn = min(512, n - h)
                P.op("tensor", lambda E_, SS, h=h, nn=nn: E_.matmul(
                    pc[:, h:h + nn], sb_onr[0:1, 0:128], sb_rbf[0:1, h:h + nn], start=True, stop=True))
            hs = list(range(0, n, 512))
            for h in hs[:-1]:
                P.op("tensor", lambda E_, SS, h=h: E_.matmul(
                    pd[:, h:h + 512], sb_onr[0:1, 0:128], sb_rbf[0:1, 1024 + h:1024 + h + 512],
                    start=True, stop=True))
            h = hs[-1]
            nn = min(512, n - h)
            tv = P.op("tensor", lambda E_, SS, h=h, nn=nn: E_.matmul(
                pd[:, h:h + nn], sb_onr[0:1, 0:128], sb_rbf[0:1, 1024 + h:1024 + h + nn],
                start=True, stop=True), inc=("TM", 1))
            P.wait("vector", "TM", tv)
            P.op("vector", lambda E_, SS, n=n, co=out_co: E_.tensor_copy(sb_abc[:, co:co + n], pc[:, 0:n]))
            P.op("vector", lambda E_, SS, n=n, co=out_co: E_.tensor_copy(sb_bbc[:, co:co + n], pd[:, 0:n]),
                 inc=("VE", 1))
            return P.val("VE")

        def emit_ln_apply(tiles, bcast_ve):
            P.wait("vector", "VE", bcast_ve)
            for x_ap, co in tiles:
                P.op("vector", lambda E_, SS, x_ap=x_ap, co=co: E_.tensor_mul(
                    sb_tmp[:, 0:512], x_ap, sb_abc[:, co:co + 512]))
                P.op("vector", lambda E_, SS, x_ap=x_ap, co=co: E_.tensor_add(
                    x_ap, sb_tmp[:, 0:512], sb_bbc[:, co:co + 512]))
            return P.op("vector", lambda E_, SS: E_.tensor_copy(sb_dum[:, 0:4], sb_dum[:, 4:8]),
                        inc=("VE", 1))

        try:
            P.op("vector", lambda E_, SS: E_.memset(sb_one[:, :], 1.0))
            P.op("vector", lambda E_, SS: E_.memset(sb_onr[:, :], 1.0))
            P.op("vector", lambda E_, SS: E_.memset(sb_dum[:, :], 0.0), inc=("VE", 1))
            P.op("sync", lambda E_, SS: E_.dma_start(out=sb_scl[:, :], in_=d_scl), inc=("SQ", 16))
            P.op("gpsimd", lambda E_, SS: E_.dma_start(
                out=sb_msk[:, :].rearrange("p (k m) -> p k m", k=16),
                in_=d_msk.rearrange("k p m -> p k m")), inc=("GS", 16))
            gs_msk = P.val("GS")

            P.op("gpsimd", lambda E_, SS: E_.dma_start(out=sb_wdt[:, :], in_=d_wdt), inc=("GS", 16))
            gs_wdt = P.val("GS")
            P.wait("tensor", "GS", gs_wdt)

            NCH = nchunks
            tm_down = [0] * NCH
            gq_ch = [0] * NCH
            ve_bcast_prev = 0

            def chunk_load(j):
                if j >= 2:
                    P.wait("gpsimd", "TM", tm_down[j - 2])
                buf = (j % 2) * 32 * 512
                P.op("gpsimd", lambda E_, SS, j=j, buf=buf: E_.dma_start(
                    out=sb_xt[:, buf:buf + 32 * 512], in_=d_xt[j]), inc=(f"CH{j % 2}", 16))
                gq_ch[j] = (f"CH{j % 2}", P.val(f"CH{j % 2}"))

            chunk_load(0)
            for j in range(NCH):
                if j + 1 < NCH:
                    chunk_load(j + 1)
                buf = (j % 2) * 32 * 512
                co = (j % 2) * 512
                P.wait("scalar", *gq_ch[j])
                P.wait("tensor", *gq_ch[j])
                P.wait("tensor", "VE", ve_bcast_prev)
                if a_level >= 1:
                    srcs = [(sb_xt[:, buf + t * 512:buf + t * 512 + 512], 0) for t in range(32)]
                    emit_stats(srcs, {0: (pc, 0)}, {0: (pd, 0)}, 512)
                if a_level >= 2:
                    ve_bcast_prev = ln_rows(512, co, [(pc, 0, 512)], [(pd, 0, 512)])
                lnq = []
                prev = ve_bcast_prev
                for q in range(4):
                    prev = emit_ln_apply(
                        [(sb_xt[:, buf + t * 512:buf + t * 512 + 512], co) for t in range(8 * q, 8 * q + 8)],
                        prev)
                    lnq.append(prev)
                P.wait("tensor", "VE", lnq[0])
                for ct in range(4):
                    v = None
                    for t in range(32):
                        if ct == 0 and t in (8, 16, 24):
                            P.wait("tensor", "VE", lnq[t // 8])
                        v = P.op("tensor", lambda E_, SS, ct=ct, t=t, buf=buf: E_.matmul(
                            ps[ct][:, 0:512],
                            sb_wdt[:, t * 512 + ct * 128:t * 512 + ct * 128 + 128],
                            sb_xt[:, buf + t * 512:buf + t * 512 + 512],
                            start=(t == 0), stop=(t == 31)),
                            inc=("TM", 1) if t == 31 else None)
                    P.wait("vector", "TM", v)
                    P.op("vector", lambda E_, SS, ct=ct, j=j: E_.tensor_copy(
                        sb_cf[:, ct * S + 512 * j:ct * S + 512 * j + 512], ps[ct][:, 0:512]),
                        inc=("VE", 1))
                tm_down[j] = P.val("TM")
            barrier()

            if "cfull" in d_dbg:
                P.op("sync", lambda E_, SS: E_.dma_start(out=d_dbg["cfull"], in_=sb_cf[:, :]), inc=("SQ", 16))
                P.wait("sync", "SQ", P.val("SQ"))
                barrier()
            if "ab" in d_dbg:
                P.op("sync", lambda E_, SS: E_.dma_start(out=d_dbg["ab"][:, 0:1024], in_=sb_abc[:, :]), inc=("SQ", 16))
                P.op("sync", lambda E_, SS: E_.dma_start(out=d_dbg["ab"][:, 1024:2048], in_=sb_bbc[:, :]), inc=("SQ", 16))
                P.op("sync", lambda E_, SS: E_.dma_start(out=d_dbg["rows"][0:1, 0:1024], in_=sb_ra[0:1, :]), inc=("SQ", 16))
                P.op("sync", lambda E_, SS: E_.dma_start(out=d_dbg["rows"][0:1, 1024:2048], in_=sb_rb[0:1, :]), inc=("SQ", 16))
                P.op("sync", lambda E_, SS: E_.dma_start(out=d_dbg["rows"][0:1, 2048:3072], in_=sb_rc[0:1, :]), inc=("SQ", 16))
                P.wait("sync", "SQ", P.val("SQ"))
                barrier()

            if max_phase < 2:
                _finish(P); raise _Done()
            P.op("gpsimd", lambda E_, SS: E_.dma_start(out=sb_xo[:, :], in_=d_xo2), inc=("GS", 16))
            gq_xo = P.val("GS")
            P.wait("scalar", "GS", gq_xo)
            P.wait("tensor", "GS", gq_xo)
            P.wait("tensor", "VE", ve_bcast_prev)
            srcs = []
            for t in range(32):
                for h in range(2):
                    srcs.append((sb_xo[:, t * 1024 + h * 512:t * 1024 + h * 512 + 512], h * 512))
            emit_stats(srcs, {0: (pc, 0), 512: (pc, 512)}, {0: (pd, 0), 512: (pd, 512)}, 1024)
            bve = ln_rows(1024, 0, [(pc, 0, 1024)], [(pd, 0, 1024)])
            tiles = []
            for t in range(32):
                for h in range(2):
                    tiles.append((sb_xo[:, t * 1024 + h * 512:t * 1024 + h * 512 + 512], h * 512))
            lnv = emit_ln_apply(tiles, bve)

            tm_q = [0] * 32
            sq_qt = [0] * 32
            ve_qt_evac = [0] * 32

            def wq_load(dt):
                if dt >= 2:
                    P.wait("gpsimd", "TM", tm_q[dt - 2])
                buf = (dt % 2) * 32 * 128
                P.op("gpsimd", lambda E_, SS, dt=dt, buf=buf: E_.dma_start(
                    out=sb_wq[:, buf:buf + 32 * 128], in_=d_wqt[dt]), inc=(f"WQ{dt % 2}", 16))
                return (f"WQ{dt % 2}", P.val(f"WQ{dt % 2}"))

            gq_wq = [0] * 32
            gq_wq[0] = wq_load(0)
            P.wait("tensor", "VE", lnv)
            for dt in range(32):
                if dt + 1 < 32:
                    gq_wq[dt + 1] = wq_load(dt + 1)
                buf = (dt % 2) * 32 * 128
                py = pc if dt % 2 == 0 else pd
                P.wait("tensor", *gq_wq[dt])
                if dt >= 2:
                    P.wait("tensor", "VE", ve_qt_evac[dt - 2])
                for t in range(32):
                    for h in range(2):
                        tm_q[dt] = P.op("tensor", lambda E_, SS, t=t, h=h, buf=buf, py=py: E_.matmul(
                            py[:, h * 512:h * 512 + 512],
                            sb_wq[:, buf + t * 128:buf + t * 128 + 128],
                            sb_xo[:, t * 1024 + h * 512:t * 1024 + h * 512 + 512],
                            start=(t == 0), stop=(t == 31)),
                            inc=("TM", 1) if (t == 31 and h == 1) else None)
                P.wait("vector", "TM", tm_q[dt])
                if dt >= 2:
                    P.wait("vector", *sq_qt[dt - 2])
                qb = (dt % 2) * 1024
                P.op("vector", lambda E_, SS, qb=qb, py=py: E_.tensor_scalar_mul(
                    sb_qt[:, qb:qb + 1024], py[:, 0:1024], QSCALE), inc=("VE", 1))
                ve_qt_evac[dt] = P.val("VE")
                P.wait("sync", "VE", ve_qt_evac[dt])
                P.op("sync", lambda E_, SS, dt=dt, qb=qb: E_.dma_start(
                    out=d_qt[dt], in_=sb_qt[:, qb:qb + 1024]), inc=(f"QS{dt % 2}", 16))
                sq_qt[dt] = (f"QS{dt % 2}", P.val(f"QS{dt % 2}"))
            sq_qt_all = (P.val("QS0"), P.val("QS1"))
            barrier()

            if "qt" in d_dbg:
                for dt in range(32):
                    P.op("sync", lambda E_, SS, dt=dt: E_.dma_start(out=d_dbg["qt"][dt], in_=d_qt[dt]),
                         inc=("SQ", 16))
                P.wait("sync", "SQ", P.val("SQ"))
                barrier()

            if max_phase < 3:
                _finish(P); raise _Done()
            P.wait("gpsimd", "QS0", sq_qt_all[0])
            P.wait("gpsimd", "QS1", sq_qt_all[1])
            P.op("gpsimd", lambda E_, SS: E_.dma_start(
                out=sb_wk2[:, :].rearrange("p (r m) -> p r m", r=4), in_=d_wk2), inc=("GS", 16))
            P.op("gpsimd", lambda E_, SS: E_.dma_start(
                out=sb_wv2[:, :].rearrange("p (r m) -> p r m", r=4), in_=d_wv2), inc=("GS", 16))
            gq_wkv = P.val("GS")

            P.wait("vector", "GS", gs_msk)
            for rset in (range(4), range(4, 8)):
                for h in range(8):
                    b = 64 * (h % 2)
                    cbase = (h // 2) * S
                    for r in rset:
                        dst = sb_kcm if r < 4 else sb_vcm
                        ro = (r % 4) * 1024
                        P.op("vector", lambda E_, SS, b=b, cbase=cbase, r=r, dst=dst, ro=ro, h=h: E_.tensor_copy(
                            dst[b:b + 64, (h // 2) * 4096 + ro:(h // 2) * 4096 + ro + 1024],
                            sb_cf[b:b + 64, cbase + r:cbase + r + 8 * 1023 + 1:8]))
                if rset.start == 0:
                    ve_matk = P.op("vector", lambda E_, SS: E_.tensor_copy(sb_dum[:, 0:4], sb_dum[:, 4:8]),
                                   inc=("VE", 1))
            ve_mat = P.op("vector", lambda E_, SS: E_.tensor_copy(sb_dum[:, 0:4], sb_dum[:, 4:8]),
                          inc=("VE", 1))

            g_i = [0]
            se_exp = {}
            tm_l = {}

            def emit_pvl(gi, sk_t, cs, nn, stb, msk, vem, hkL):
                if msk:
                    P.wait("tensor", "VE", vem)
                else:
                    P.wait("tensor", "SE", se_exp[gi])
                P.op("tensor", lambda E_, SS, sk_t=sk_t, cs=cs, nn=nn, stb=stb, hkL=hkL: E_.matmul(
                    pc[:, cs:cs + nn],
                    sb_vp[:, sk_t * 256 + hkL * 128:sk_t * 256 + hkL * 128 + 128],
                    sb_pt[:, stb * 512:stb * 512 + nn],
                    start=(sk_t == 0), stop=(sk_t == 63), skip_group_check=True))
                v = P.op("tensor", lambda E_, SS, cs=cs, nn=nn, stb=stb, sk_t=sk_t: E_.matmul(
                    pd[0:1, cs:cs + nn], sb_one[:, 0:1],
                    sb_pt[:, stb * 512:stb * 512 + nn],
                    start=(sk_t == 0), stop=(sk_t == 63), skip_group_check=True), inc=("TM", 1))
                tm_l[gi] = v
            tm_slast = {}
            ve_ot = 0
            sq_ot = [0] * 32
            last_pc_reader_ve = ve_mat
            last_pd_reader_ve = ve_mat
            ve_kt_evac = ve_mat
            se_last_exp = 0

            P.wait("tensor", "VE", ve_matk)
            P.wait("tensor", "GS", gq_wkv)

            for pair in range(4):
                hk0 = 2 * pair
                for hkL in range(2):
                    hk = hk0 + hkL
                    for h in range(8):
                        pp = pc if (h % 2 == 0) else pd
                        P.wait("tensor", "VE", last_pc_reader_ve if pp is pc else last_pd_reader_ve)
                        b = 64 * (h % 2)
                        v = None
                        for half in range(2):
                            for r in range(4):
                                v = P.op("tensor", lambda E_, SS, pp=pp, b=b, h=h, r=r, half=half, hk=hk: E_.matmul(
                                    pp[:, half * 512:half * 512 + 512],
                                    sb_wk2[b:b + 64, r * 1024 + hk * 128:r * 1024 + hk * 128 + 128],
                                    sb_kcm[b:b + 64, (h // 2) * 4096 + r * 1024 + half * 512:
                                           (h // 2) * 4096 + r * 1024 + half * 512 + 512],
                                    start=(r == 0), stop=(r == 3)),
                                    inc=("TM", 1) if (half == 1 and r == 3) else None)
                        P.wait("vector", "TM", v)
                        P.op("vector", lambda E_, SS, hkL=hkL, h=h, pp=pp: E_.tensor_copy(
                            sb_kt[:, hkL * 8192 + h * 1024:hkL * 8192 + h * 1024 + 1024], pp[:, 0:1024]),
                            inc=("VE", 1))
                        if pp is pc:
                            last_pc_reader_ve = P.val("VE")
                        else:
                            last_pd_reader_ve = P.val("VE")
                ve_v_evac = [0, 0]
                if pair == 0:
                    P.wait("tensor", "VE", ve_mat)
                for svt in range(64):
                    h = svt // 8
                    b = 64 * (h % 2)
                    pp = ps[svt % 2]
                    if pair > 0 and svt < 2:
                        P.wait("tensor", "SE", se_last_exp)
                    if svt >= 2:
                        P.wait("tensor", "VE", ve_v_evac[svt % 2])
                    v = None
                    for r in range(4):
                        v = P.op("tensor", lambda E_, SS, pp=pp, b=b, h=h, r=r, svt=svt, hk0=hk0: E_.matmul(
                            pp[:, 0:256],
                            sb_vcm[b:b + 64, (h // 2) * 4096 + r * 1024 + 128 * (svt % 8):
                                   (h // 2) * 4096 + r * 1024 + 128 * (svt % 8) + 128],
                            sb_wv2[b:b + 64, r * 1024 + hk0 * 128:r * 1024 + hk0 * 128 + 256],
                            start=(r == 0), stop=(r == 3)),
                            inc=("TM", 1) if r == 3 else None)
                    P.wait("vector", "TM", v)
                    P.op("vector", lambda E_, SS, svt=svt, pp=pp: E_.tensor_copy(
                        sb_vp[:, svt * 256:svt * 256 + 256], pp[:, 0:256]), inc=("VE", 1))
                    ve_v_evac[svt % 2] = P.val("VE")
                ve_kv = P.val("VE")
                P.wait("tensor", "VE", ve_kv)

                for hkL in range(2):
                    hk = hk0 + hkL
                    qbuf = (hk % 2) * 4096
                    P.wait("gpsimd", "QS0", sq_qt_all[0])
                    P.wait("gpsimd", "QS1", sq_qt_all[1])
                    if hk >= 2:
                        P.wait("gpsimd", "TM", tm_slast.get(hk - 2, 0))
                    P.op("gpsimd", lambda E_, SS, hk=hk, qbuf=qbuf: E_.dma_start(
                        out=sb_qg[:, qbuf:qbuf + 4096].rearrange("h p m -> p h m" if False else "p (h m) -> p h m", h=4),
                        in_=d_qt[4 * hk:4 * hk + 4].rearrange("h p m -> p h m")), inc=(f"QG{hk % 2}", 16))
                    P.wait("tensor", f"QG{hk % 2}", P.val(f"QG{hk % 2}"))

                    for ql in range(4):
                        qh = 4 * hk + ql
                        units = []
                        for sk_t in range(64):
                            k = sk_t // 8
                            c0 = 128 * k
                            if c0 < 512:
                                units.append((sk_t, c0, 512 - c0, True))
                                units.append((sk_t, 512, 512, False))
                            else:
                                units.append((sk_t, c0, 1024 - c0, True))
                        nu = len(units)
                        pend = []
                        for iu, (sk_t, cs, nn, msk) in enumerate(units):
                            gi = g_i[0] + iu
                            stb = gi % 4
                            if gi >= 4:
                                P.wait("tensor", "SE", se_exp[gi - 4])
                            if iu == 0:
                                P.wait("tensor", "VE", ve_ot if qh > 0 else ve_kv)
                            tms = P.op("tensor", lambda E_, SS, stb=stb, hkL=hkL, sk_t=sk_t, ql=ql,
                                 qbuf=qbuf, cs=cs, nn=nn: E_.matmul(
                                     ps[stb][:, 0:nn],
                                     sb_kt[:, hkL * 8192 + sk_t * 128:hkL * 8192 + sk_t * 128 + 128],
                                     sb_qg[:, qbuf + ql * 1024 + cs:qbuf + ql * 1024 + cs + nn],
                                     start=True, stop=True, skip_group_check=True), inc=("TM", 1))
                            if iu == nu - 1:
                                tm_slast[hk] = tms
                            P.wait("scalar", "TM", tms)
                            if gi >= 4:
                                P.wait("scalar", "TM", tm_l[gi - 4])
                            P.op("scalar", lambda E_, SS, stb=stb, nn=nn: E_.activation(
                                sb_pt[:, stb * 512:stb * 512 + nn], ps[stb][:, 0:nn], AF.Exp),
                                inc=("SE", 1))
                            se_exp[gi] = P.val("SE")
                            se_last_exp = se_exp[gi]
                            vem = None
                            if msk:
                                P.wait("vector", "SE", se_exp[gi])
                                k = sk_t // 8
                                mi = (k % 2) * 8 + (sk_t % 8)
                                P.op("vector", lambda E_, SS, stb=stb, mi=mi: E_.tensor_mul(
                                    sb_pt[:, stb * 512:stb * 512 + 128],
                                    sb_pt[:, stb * 512:stb * 512 + 128],
                                    sb_msk[:, mi * 128:mi * 128 + 128]), inc=("VE", 1))
                                vem = P.val("VE")
                            pend.append((gi, sk_t, cs, nn, stb, msk, vem, hkL))
                            if len(pend) > 3:
                                emit_pvl(*pend.pop(0))
                        for p_ in pend:
                            emit_pvl(*p_)
                        g_i[0] += nu

                        P.wait("vector", "TM", tm_l[g_i[0] - 1])
                        P.op("vector", lambda E_, SS: E_.tensor_copy(sb_ra[0:1, 0:1024], pd[0:1, 0:1024]))
                        P.op("vector", lambda E_, SS: E_.reciprocal(sb_rb[0:1, 0:1024], sb_ra[0:1, 0:1024]))
                        P.op("vector", lambda E_, SS: E_.tensor_copy(sb_rbf[0:1, 0:1024], sb_rb[0:1, 0:1024]),
                             inc=("VE", 1))
                        P.wait("tensor", "VE", P.val("VE"))
                        tv = None
                        for h2 in range(2):
                            tv = P.op("tensor", lambda E_, SS, h2=h2: E_.matmul(
                                pd[:, h2 * 512:h2 * 512 + 512], sb_onr[0:1, 0:128],
                                sb_rbf[0:1, h2 * 512:h2 * 512 + 512], start=True, stop=True),
                                inc=("TM", 1) if h2 == 1 else None)
                        P.wait("vector", "TM", tv)
                        if qh >= 2:
                            P.wait("vector", *sq_ot[qh - 2])
                        ob = (qh % 2) * 1024
                        P.op("vector", lambda E_, SS: E_.tensor_copy(sb_invl[:, :], pd[:, 0:1024]))
                        P.op("vector", lambda E_, SS, ob=ob: E_.tensor_mul(
                            sb_otsb[:, ob:ob + 1024], pc[:, 0:1024], sb_invl[:, :]), inc=("VE", 1))
                        ve_ot = P.val("VE")
                        P.wait("sync", "VE", ve_ot)
                        P.op("sync", lambda E_, SS, qh=qh, ob=ob: E_.dma_start(
                            out=d_ot[qh], in_=sb_otsb[:, ob:ob + 1024]), inc=(f"OS{qh % 2}", 16))
                        sq_ot[qh] = (f"OS{qh % 2}", P.val(f"OS{qh % 2}"))
                last_pc_reader_ve = ve_ot
                last_pd_reader_ve = ve_ot

            tm_c_end = P.val("TM")
            for qh in range(32):
                P.wait("sync", "TM", tm_c_end)
                P.wait("sync", *sq_ot[qh])
                P.op("sync", lambda E_, SS, qh=qh: E_.dma_start(
                    out=sb_ota[:, qh * 1024:qh * 1024 + 1024], in_=d_ot[qh]), inc=("OP", 16))
            sq_ota_pre = P.val("OP")
            barrier()

            if "ot" in d_dbg:
                for qh in range(32):
                    P.op("sync", lambda E_, SS, qh=qh: E_.dma_start(out=d_dbg["ot"][qh], in_=d_ot[qh]),
                         inc=("SQ", 16))
                P.wait("sync", "SQ", P.val("SQ"))
                barrier()

            if max_phase < 4:
                _finish(P); raise _Done()
            P.wait("tensor", "OP", sq_ota_pre)

            tm_wo = [0] * 32
            ve_x1 = [0] * 32
            sq_x1 = [0] * 32
            se_x1b = [0] * 32
            tm_st = [0] * 32
            gq_wo = [0] * 32
            sq_xr = [0] * 32

            def wo_load(ot):
                if ot >= 2:
                    P.wait("gpsimd", "TM", tm_wo[ot - 2])
                buf = (ot % 2) * 4096
                P.op("gpsimd", lambda E_, SS, ot=ot, buf=buf: E_.dma_start(
                    out=sb_wo[:, buf:buf + 4096], in_=d_wot[ot]), inc=(f"WO{ot % 2}", 16))
                gq_wo[ot] = (f"WO{ot % 2}", P.val(f"WO{ot % 2}"))

            def xr_load(ot):
                if ot >= 2:
                    P.wait("sync", "VE", ve_x1[ot - 2])
                buf = (ot % 2) * 1024
                P.op("sync", lambda E_, SS, ot=ot, buf=buf: E_.dma_start(
                    out=sb_xr[:, buf:buf + 1024], in_=d_xo[128 * ot:128 * ot + 128, :]), inc=(f"XR{ot % 2}", 16))
                sq_xr[ot] = (f"XR{ot % 2}", P.val(f"XR{ot % 2}"))

            wo_load(0)
            xr_load(0)
            pend_stats = []

            def emit_stats_d(ot):
                sq_slot = sb_xsq if ot % 2 == 0 else sb_tmp
                P.wait("tensor", "SE", se_x1b[ot])
                for h in range(2):
                    P.op("tensor", lambda E_, SS, ot=ot, h=h: E_.matmul(
                        ps[h][0:1, 0:512], sb_one[:, 0:1],
                        sb_x1b[:, ot * 1024 + h * 512:ot * 1024 + h * 512 + 512],
                        start=(ot == 0), stop=(ot == 31)))
                    tm_st[ot] = P.op("tensor", lambda E_, SS, ot=ot, h=h, sq_slot=sq_slot: E_.matmul(
                        ps[2 + h][0:1, 0:512], sb_one[:, 0:1], sq_slot[:, h * 512:h * 512 + 512],
                        start=(ot == 0), stop=(ot == 31)),
                        inc=("TM", 1) if h == 1 else None)

            for ot in range(32):
                if ot + 1 < 32:
                    wo_load(ot + 1)
                    xr_load(ot + 1)
                buf = (ot % 2) * 4096
                py = pc if ot % 2 == 0 else pd
                P.wait("tensor", *gq_wo[ot])
                if ot >= 2:
                    P.wait("tensor", "VE", ve_x1[ot - 2])
                for t in range(32):
                    for h in range(2):
                        tm_wo[ot] = P.op("tensor", lambda E_, SS, t=t, h=h, buf=buf, py=py: E_.matmul(
                            py[:, h * 512:h * 512 + 512],
                            sb_wo[:, buf + t * 128:buf + t * 128 + 128],
                            sb_ota[:, t * 1024 + h * 512:t * 1024 + h * 512 + 512],
                            start=(t == 0), stop=(t == 31)),
                            inc=("TM", 1) if (t == 31 and h == 1) else None)
                P.wait("vector", "TM", tm_wo[ot])
                P.wait("vector", *sq_xr[ot])
                if ot >= 2:
                    P.wait("vector", *sq_x1[ot - 2])
                xb = (ot % 2) * 1024
                P.op("vector", lambda E_, SS, xb=xb, py=py: E_.tensor_add(
                    sb_x1s[:, xb:xb + 1024], py[:, 0:1024], sb_xr[:, xb:xb + 1024]), inc=("VE", 1))
                ve_x1[ot] = P.val("VE")
                P.wait("scalar", "VE", ve_x1[ot])
                if ot >= 2:
                    P.wait("scalar", "TM", tm_st[ot - 2])
                sq_slot = sb_xsq if ot % 2 == 0 else sb_tmp
                P.op("scalar", lambda E_, SS, ot=ot, xb=xb: E_.copy(
                    sb_x1b[:, ot * 1024:ot * 1024 + 1024], sb_x1s[:, xb:xb + 1024]))
                P.op("scalar", lambda E_, SS, ot=ot, sq_slot=sq_slot: E_.activation(
                    sq_slot[:, 0:1024], sb_x1b[:, ot * 1024:ot * 1024 + 1024], AF.Square),
                    inc=("SE", 1))
                se_x1b[ot] = P.val("SE")
                if ot >= 1:
                    emit_stats_d(ot - 1)
                P.wait("sync", "VE", ve_x1[ot])
                P.op("sync", lambda E_, SS, ot=ot, xb=xb: E_.dma_start(
                    out=d_x1[ot], in_=sb_x1s[:, xb:xb + 1024]), inc=(f"XS{ot % 2}", 16))
                sq_x1[ot] = (f"XS{ot % 2}", P.val(f"XS{ot % 2}"))
            emit_stats_d(31)
            sq_x1_all = (P.val("XS0"), P.val("XS1"))

            bve2 = ln_rows(1024, 0,
                           [(ps[0], 0, 512), (ps[1], 0, 512)],
                           [(ps[2], 0, 512), (ps[3], 0, 512)])
            barrier()

            if "x1" in d_dbg:
                for ot in range(32):
                    P.op("sync", lambda E_, SS, ot=ot: E_.dma_start(out=d_dbg["x1"][ot], in_=d_x1[ot]),
                         inc=("SQ", 16))
                P.wait("sync", "SQ", P.val("SQ"))
                barrier()

            if max_phase < 5:
                _finish(P); raise _Done()
            tiles = []
            for t in range(32):
                for h in range(2):
                    tiles.append((sb_x1b[:, t * 1024 + h * 512:t * 1024 + h * 512 + 512], h * 512))
            lnv2 = emit_ln_apply(tiles, bve2)
            P.wait("tensor", "VE", lnv2)
            if "ab2" in d_dbg:
                P.wait("sync", "VE", lnv2)
                P.op("sync", lambda E_, SS: E_.dma_start(out=d_dbg["ab2"][:, 0:1024], in_=sb_abc[:, :]), inc=("SQ", 16))
                P.op("sync", lambda E_, SS: E_.dma_start(out=d_dbg["ab2"][:, 1024:2048], in_=sb_bbc[:, :]), inc=("SQ", 16))
                P.op("sync", lambda E_, SS: E_.dma_start(out=d_dbg["x1n"], in_=sb_x1b[:, :]), inc=("SQ", 16))
                P.wait("sync", "SQ", P.val("SQ"))
                barrier()

            tm_ffn = [0] * 32
            se_wb = [0] * 32
            ve_os = [0] * 32
            sq_os = [0] * 32
            sq_wf = [0] * 32
            sq_x1r = [0] * 32

            def wf_load(ot):
                if ot < 2:
                    P.wait("sync", "XS0", sq_x1_all[0])
                    P.wait("sync", "XS1", sq_x1_all[1])
                if ot >= 2:
                    P.wait("sync", "SE", se_wb[ot - 2])
                buf = (ot % 2) * 4096
                P.op("sync", lambda E_, SS, ot=ot, buf=buf: E_.dma_start(
                    out=sb_wf[:, buf:buf + 4096], in_=d_wft[ot]), inc=(f"WF{ot % 2}", 16))
                sq_wf[ot] = (f"WF{ot % 2}", P.val(f"WF{ot % 2}"))

            def x1r_load(ot):
                P.wait("sync", "XS0", sq_x1_all[0])
                P.wait("sync", "XS1", sq_x1_all[1])
                if ot >= 2:
                    P.wait("sync", "VE", ve_os[ot - 2])
                buf = (ot % 2) * 1024
                P.op("sync", lambda E_, SS, ot=ot, buf=buf: E_.dma_start(
                    out=sb_x1r[:, buf:buf + 1024], in_=d_x1[ot]), inc=(f"XB{ot % 2}", 16))
                sq_x1r[ot] = (f"XB{ot % 2}", P.val(f"XB{ot % 2}"))

            P.wait("scalar", "SQ", 16)
            wf_load(0)
            x1r_load(0)
            for ot in range(32):
                if ot + 1 < 32:
                    wf_load(ot + 1)
                    x1r_load(ot + 1)
                buf = (ot % 2) * 4096
                py = pc if ot % 2 == 0 else pd
                P.wait("scalar", *sq_wf[ot])
                if ot >= 2:
                    P.wait("scalar", "TM", tm_ffn[ot - 2])
                for it in range(32):
                    P.op("scalar", lambda E_, SS, ot=ot, it=it, buf=buf: E_.mul(
                        sb_wb[:, buf + it * 128:buf + it * 128 + 128],
                        sb_wf[:, buf + it * 128:buf + it * 128 + 128],
                        sb_scl[:, ot * 32 + it:ot * 32 + it + 1]))
                P.op("scalar", lambda E_, SS: E_.copy(sb_dum[:, 8:12], sb_dum[:, 12:16]), inc=("SE", 1))
                se_wb[ot] = P.val("SE")
                P.wait("tensor", "SE", se_wb[ot])
                if ot >= 2:
                    P.wait("tensor", "VE", ve_os[ot - 2])
                for it in range(32):
                    for h in range(2):
                        tm_ffn[ot] = P.op("tensor", lambda E_, SS, it=it, h=h, buf=buf, py=py: E_.matmul(
                            py[:, h * 512:h * 512 + 512],
                            sb_wb[:, buf + it * 128:buf + it * 128 + 128],
                            sb_x1b[:, it * 1024 + h * 512:it * 1024 + h * 512 + 512],
                            start=(it == 0), stop=(it == 31)),
                            inc=("TM", 1) if (it == 31 and h == 1) else None)
                P.wait("vector", "TM", tm_ffn[ot])
                P.wait("vector", *sq_x1r[ot])
                if ot >= 2:
                    P.wait("vector", *sq_os[ot - 2])
                xb = (ot % 2) * 1024
                P.op("vector", lambda E_, SS, xb=xb, py=py: E_.tensor_add(
                    sb_os[:, xb:xb + 1024], py[:, 0:1024], sb_x1r[:, xb:xb + 1024]), inc=("VE", 1))
                ve_os[ot] = P.val("VE")
                P.wait("sync", "VE", ve_os[ot])
                P.op("sync", lambda E_, SS, ot=ot, xb=xb: E_.dma_start(
                    out=d_out[ot], in_=sb_os[:, xb:xb + 1024]), inc=(f"OU{ot % 2}", 16))
                sq_os[ot] = (f"OU{ot % 2}", P.val(f"OU{ot % 2}"))

            P.wait("sync", "OU0", P.val("OU0"))
            P.wait("sync", "OU1", P.val("OU1"))
            P.wait("gpsimd", "OU0", P.val("OU0"))
            P.wait("gpsimd", "OU1", P.val("OU1"))


        except _Done:
            pass
        _dispatch(P, blk, sems)
    return nc


def _finish(P):
    pass


def _dispatch(P, blk, sems):
        @blk.tensor
        def _(e):
            for f in P.q["tensor"]:
                f(e, sems)

        @blk.vector
        def _(e):
            for f in P.q["vector"]:
                f(e, sems)

        @blk.scalar
        def _(e):
            for f in P.q["scalar"]:
                f(e, sems)

        @blk.gpsimd
        def _(e):
            for f in P.q["gpsimd"]:
                f(e, sems)

        @blk.sync
        def _(e):
            for f in P.q["sync"]:
                f(e, sems)


def _stage(inputs):
    x = np.asarray(inputs["L0_attn_hidden_states"], dtype=np.float32)[0]
    Wq = np.asarray(inputs["L0_attn_Wq"], dtype=np.float32)
    Wkd = np.asarray(inputs["L0_attn_Wk_down"], dtype=np.float32)
    Wvd = np.asarray(inputs["L0_attn_Wv_down"], dtype=np.float32)
    Wku = np.asarray(inputs["L0_attn_Wk_up"], dtype=np.float32)
    Wvu = np.asarray(inputs["L0_attn_Wv_up"], dtype=np.float32)
    Wo = np.asarray(inputs["L0_attn_Wo"], dtype=np.float32)
    Wf = np.asarray(inputs["L0_ffn_weight"], dtype=np.float32)
    scl = np.asarray(inputs["L0_ffn_weight_scale"], dtype=np.float32)

    wd = np.concatenate(
        [np.concatenate([Wkd[32 * h:32 * h + 32], Wvd[32 * h:32 * h + 32]], axis=0)
         for h in range(8)], axis=0)
    wdt = np.ascontiguousarray(
        wd.T.reshape(32, 128, 512).transpose(1, 0, 2).reshape(128, 32 * 512))

    xt_sw = np.ascontiguousarray(
        x.T.reshape(32, 128, 16, 512).transpose(2, 1, 0, 3).reshape(16, 128, 32 * 512))

    def tiled_T(w):
        return np.ascontiguousarray(
            w.T.reshape(32, 128, 32, 128).transpose(2, 1, 0, 3).reshape(32, 128, 32 * 128))

    wqt = tiled_T(Wq)
    wot = tiled_T(Wo)
    wft = tiled_T(Wf)

    def dup2(wu):
        wt = wu.T
        out = np.empty((128, 4, 1024), np.float32)
        for r in range(4):
            for b in range(2):
                out[64 * b:64 * b + 64, r] = wt[64 * r:64 * r + 64]
        return np.ascontiguousarray(out)

    wk2 = dup2(Wku)
    wv2 = dup2(Wvu)
    scale_pp = np.ascontiguousarray(
        np.broadcast_to(scl.reshape(1, 1024), (128, 1024)).astype(np.float32))

    shared = dict(xt_sw=xt_sw, wdt_sw=wdt, wqt_sw=wqt, wot_sw=wot, wft_sw=wft,
                  wk2=wk2, wv2=wv2, scale_pp=scale_pp)

    tri = np.tril(np.ones((128, 128), np.float32)).T
    in_maps = []
    for c in range(NCORE):
        blocks = zig_blocks(c)
        rows = np.concatenate([np.arange(b * 128, b * 128 + 128) for b in blocks])
        xt_own = np.ascontiguousarray(x[rows].T)
        mask = np.empty((16, 128, 128), np.float32)
        for kpar in range(2):
            bpos = c if kpar == 0 else 7 - c
            for m in range(8):
                if m < bpos:
                    mask[kpar * 8 + m] = 1.0
                elif m == bpos:
                    mask[kpar * 8 + m] = tri
                else:
                    mask[kpar * 8 + m] = 0.0
        im = dict(shared)
        im["xt_own"] = xt_own
        im["xo_sw"] = np.ascontiguousarray(
            xt_own.reshape(32, 128, 1024).transpose(1, 0, 2).reshape(128, 32 * 1024))
        im["mask01"] = np.ascontiguousarray(mask)
        in_maps.append(im)
    return in_maps


def kernel(**inputs):
    in_maps = _stage(inputs)
    nc = build_program(tuple(DEBUG_TAPS), MAX_PHASE, NCHUNKS, A_LEVEL, LR_LEVEL, NCORES_RUN)
    res = run_bass_kernel_spmd(nc, in_maps[:NCORES_RUN], core_ids=list(range(NCORES_RUN)))
    out = np.zeros((1, S, HID), np.float32)
    for c in range(NCORES_RUN):
        x2t = np.asarray(res.results[c]["o"]).reshape(HID, 1024)
        for j, b in enumerate(zig_blocks(c)):
            out[0, b * 128:(b + 1) * 128, :] = x2t[:, j * 128:(j + 1) * 128].T
    if DEBUG_TAPS:
        kernel._dbg = [{k: np.asarray(v) for k, v in r.items() if k.startswith("dbg_")}
                       for r in res.results]
    return out
